# version 40
# baseline (speedup 1.0000x reference)
import numpy as np
import ml_dtypes
from contextlib import ExitStack

import concourse.bass as bass
import concourse.mybir as mybir

F32 = mybir.dt.float32
BF16 = mybir.dt.bfloat16
AF = mybir.ActivationFunctionType
ALU = mybir.AluOpType
bfdt = ml_dtypes.bfloat16

D = 1024
KC = 8
H = 16
NEG = -1.0e30
DMA16 = 16


class Plan:
    ENGS = ("sync", "pool", "pe", "dve", "act")

    def __init__(self):
        self.ops = {e: [] for e in self.ENGS}
        self.cnt = {}

    def op(self, eng, fn, waits=(), sig=None, inc=1):
        ws = [w for w in waits if w is not None]
        self.ops[eng].append([fn, ws, sig, inc])
        if sig is not None:
            self.cnt[sig] = self.cnt.get(sig, 0) + inc
            return (sig, self.cnt[sig])
        return None

    def emit(self, block, sems):
        def make(ops):
            def run(e):
                seen = {}
                for fn, waits, sig, inc in ops:
                    for s, v in waits:
                        if seen.get(s, 0) < v:
                            e.wait_ge(sems[s], v)
                            seen[s] = v
                    instr = fn(e)
                    if sig is not None:
                        instr.then_inc(sems[sig], inc)
            return run
        block.sync(make(self.ops["sync"]))
        block.gpsimd(make(self.ops["pool"]))
        block.tensor(make(self.ops["pe"]))
        block.vector(make(self.ops["dve"]))
        block.scalar(make(self.ops["act"]))


def build_nc(L):
    T = L // 128
    G = L // 512
    assert L % 512 == 0

    nc = bass.Bass("TRN2", target_bir_lowering=False)

    x_d = nc.dram_tensor("x", [L, D], F32, kind="ExternalInput")
    wqkv_d = nc.dram_tensor("wqkv", [D, 3 * D], F32, kind="ExternalInput")
    bqkv_d = nc.dram_tensor("bqkv", [3 * D], F32, kind="ExternalInput")
    wproj_d = nc.dram_tensor("wproj", [D, D], F32, kind="ExternalInput")
    bproj_d = nc.dram_tensor("bproj", [D], F32, kind="ExternalInput")
    wffn_d = nc.dram_tensor("wffn", [D, D], F32, kind="ExternalInput")
    bffn_d = nc.dram_tensor("bffn", [D], F32, kind="ExternalInput")
    vr_d = nc.dram_tensor("vr", [D, 32], F32, kind="ExternalInput")
    cmat_d = nc.dram_tensor("cmat", [D, 32], F32, kind="ExternalInput")
    identb_d = nc.dram_tensor("identb", [128, 128], BF16, kind="ExternalInput")
    identf_d = nc.dram_tensor("identf", [66, 66], F32, kind="ExternalInput")
    tri01_d = nc.dram_tensor("tri01", [128, 128], BF16, kind="ExternalInput")
    y_d = nc.dram_tensor("y", [L, D], F32, kind="ExternalOutput")

    x_t = x_d.rearrange("(a p) n -> a p n", p=128)
    y_t = y_d.rearrange("(a p) n -> a p n", p=128)

    with ExitStack() as ctx:
        E = ctx.enter_context
        q_fm = E(nc.sbuf_tensor("q_fm", [128, KC, L], BF16))
        k_fm = E(nc.sbuf_tensor("k_fm", [128, KC, L], BF16))
        v_on = E(nc.sbuf_tensor("v_on", [128, T, 1118], BF16))
        wq_a = E(nc.sbuf_tensor("wq_a", [128, KC, 3 * D], BF16))
        xnt_a = E(nc.sbuf_tensor("xnt_a", [128, 8192], BF16))
        xf_a = E(nc.sbuf_tensor("xf_a", [128, 6656], F32))
        xn_b = E(nc.sbuf_tensor("xn_b", [128, 2, D], BF16))
        bias_a = E(nc.sbuf_tensor("bias_a", [128, 2 * D], F32))
        bv_bc = bias_a[:, 0:D]
        bpj_bc = bias_a[:, 0:D]
        bfn_bc = bias_a[:, D:2 * D]
        tri01 = E(nc.sbuf_tensor("tri01_sb", [128, 128], BF16))
        id_b = E(nc.sbuf_tensor("id_b", [128, 128], BF16))
        id_f = E(nc.sbuf_tensor("id_f", [66, 66], F32))
        bqkvT = E(nc.sbuf_tensor("bqkvT", [128, 24], F32))
        bqkv_sb = E(nc.sbuf_tensor("bqkv_sb", [24, 128], F32))
        vr_bf = E(nc.sbuf_tensor("vr_bf", [128, KC, 32], BF16))
        c_st = E(nc.sbuf_tensor("c_st", [128, KC, 32], BF16))
        ct_bf = E(nc.sbuf_tensor("ct_bf", [32, D], BF16))
        stats = E(nc.sbuf_tensor("stats", [128, 2, 12], F32))
        mv = E(nc.sbuf_tensor("mv", [128, 2, 2], F32))
        ln_t = E(nc.sbuf_tensor("ln_t", [128, 2], F32))
        rstd = E(nc.sbuf_tensor("rstd", [128, 2], F32))
        negmr = E(nc.sbuf_tensor("negmr", [128, 2], F32))
        recips = E(nc.sbuf_tensor("recips", [128, 4], F32))
        eps_t = E(nc.sbuf_tensor("eps_t", [128, 1], F32))
        ps_sc = E(nc.psum_tensor("ps_sc", [128, 2048], F32))
        ps_w = E(nc.psum_tensor("ps_w", [128, 512], F32))
        ps_bf = E(nc.psum_tensor("ps_bf", [128, 1024], BF16))
        ps_pv = [E(nc.psum_tensor(f"ps_pv{i}", [128, 512], F32)) for i in range(2)]
        semnames = ["dI", "dX0", "dX1", "dX2", "dY0", "dY1", "dWq", "dWk", "dWv", "dWs", "dWp", "dWf0", "dWf1", "dWf2", "dWf3", "dBp", "pe", "dv", "ac", "pl"]
        sems = {n: E(nc.semaphore(n)) for n in semnames}
        block = E(nc.Block())

        P = Plan()

        def xnT(buf):
            return xnt_a[:, buf * 4096:(buf + 1) * 4096].rearrange("p (c n) -> p c n", c=KC)
        def aoT(buf):
            return xnt_a[:, buf * 1024:(buf + 1) * 1024].rearrange("p (c n) -> p c n", c=KC)
        def hT(buf):
            return xnt_a[:, 2048 + buf * 1024:2048 + (buf + 1) * 1024].rearrange("p (c n) -> p c n", c=KC)
        def chunkpair(cp):
            if cp >= 2:
                o = 5120 + (cp - 2) * 512
                return xf_a[:, o:o + 512].bitcast(BF16)
            return xnt_a[:, 4096 + cp * 1024:4096 + (cp + 1) * 1024]
        u_bf = xnt_a[0:32, 7168:7296]
        def ao_g0(qt):
            return xnt_a[:, qt * 1024:(qt + 1) * 1024]
        pv_sbB = xnt_a[:, 6144:7168]
        def x_in(i):
            return xf_a[:, i * D:(i + 1) * D]
        x_rr = x_in
        def x2b(i):
            return xf_a[:, 3072 + i * D:3072 + (i + 1) * D]

        wstage = xf_a[:, 0:4096].bitcast(BF16).rearrange("p (c n) -> p c n", c=KC)
        wproj_bf = wq_a[:, :, 0:D]
        wt_bf = wq_a[:, :, D:2 * D]
        def ao(par, qt):
            return wq_a[:, 4 * par + qt, 2 * D:3 * D]
        psbf3 = ps_bf.rearrange("p (c n) -> p c n", c=KC)

        def sdma(dst, src, waits=(), sig="dI"):
            return P.op("sync", lambda e, d=dst, s=src: e.dma_start(out=d, in_=s),
                        waits=waits, sig=sig, inc=DMA16)
        m_idb = sdma(id_b[:], identb_d.ap())
        m_idf = sdma(id_f[:], identf_d.ap())
        m_cm = sdma(tri01[:], tri01_d.ap())
        m_bq = sdma(bqkv_sb[:], bqkv_d.rearrange("(a b) -> a b", b=128))
        m_init = ("dI", P.cnt["dI"])
        m_idb = m_idf = m_cm = m_bq = m_init

        def gdma(dst, src, waits=(), sig="dWs"):
            return P.op("pool", lambda e, d=dst, s=src: e.dma_start(out=d, in_=s),
                        waits=waits, sig=sig, inc=DMA16)
        wqkv_r = wqkv_d.rearrange("(c p) n -> p c n", p=128)
        m_wq = gdma(wq_a[:, :, 0:D], wqkv_r[:, :, 0:D], sig="dWq")
        m_wk = gdma(wq_a[:, :, D:2 * D], wqkv_r[:, :, D:2 * D], sig="dWk")
        m_wv = gdma(wq_a[:, :, 2 * D:3 * D], wqkv_r[:, :, 2 * D:3 * D], sig="dWv")
        gdma(vr_bf[:, :, :], vr_d.rearrange("(c p) n -> p c n", p=128))
        gdma(c_st[:, :, :], cmat_d.rearrange("(c p) n -> p c n", p=128))
        m_vr = m_cst = ("dWs", 64)
        gdma(bv_bc, bqkv_d.rearrange("(a b) -> a b", a=3)[2:3, :].broadcast_to((128, D)),
             sig="dWs")
        gdma(bfn_bc, bffn_d.rearrange("(a b) -> a b", a=1).broadcast_to((128, D)),
             sig="dWs")
        m_bv_bc = m_bfn_bc = ("dWs", 64)

        m_eps = P.op("dve", lambda e: e.memset(eps_t[:], 1e-5), sig="dv")
        m_vones = P.op("dve", lambda e: e.memset(
                           v_on[:, :, 0:1056].rearrange("p t (h n) -> p t h n", h=H)[:, :, :, 64:66],
                           1.0),
                       waits=[m_eps], sig="dv")
        m_vones = P.op("dve", lambda e: e.memset(v_on[:, :, 1056:1118], 0.0),
                       waits=[m_vones], sig="dv")
        m_tbq = P.op("pe", lambda e: e.transpose(ps_w[0:128, 0:24], bqkv_sb[0:24, 0:128],
                                                 id_f[0:24, 0:24]),
                     waits=[m_bq, m_idf], sig="pe")
        m_bqkvT = P.op("dve", lambda e: e.tensor_copy(bqkvT[:], ps_w[0:128, 0:24]),
                       waits=[m_tbq, m_vones], sig="dv")
        m_bias = {"v": m_bv_bc, "f": m_bfn_bc}
        m_psw_free = None

        r_xin = [None, None, None]
        w_xn = [None, None]
        r_xn = [None, None]
        w_xnt = [None, None]
        r_xnt = [None, None]
        r_psbf = None
        m_ln_last = [None, None]
        r_rstd = [None, None]
        bank_reader = [None, None, None, None]
        qkv_bank = 0
        m_v_evac = [None, None]
        last_A_pe = None

        def ln_stats(src, extra_wait, lp):
            m1 = P.op("dve", lambda e, lp=lp: e.bn_stats(stats[:, lp, 0:6], src[:, 0:512]),
                      waits=[extra_wait, m_ln_last[lp]], sig="dv")
            m2 = P.op("dve", lambda e, lp=lp: e.bn_stats(stats[:, lp, 6:12], src[:, 512:1024]),
                      waits=[m1], sig="dv")
            m3 = P.op("dve", lambda e, lp=lp: e.bn_aggr(mv[:, lp, :], stats[:, lp, :]),
                      waits=[m2], sig="dv")
            return m3

        def ln_fin(src, i2, m3, lp):
            m4 = P.op("act", lambda e, lp=lp: e.activation(ln_t[:, lp:lp + 1], mv[:, lp, 1:2],
                                                           AF.Ln, bias=eps_t[:], scale=1.0),
                      waits=[m3, r_rstd[lp], m_eps], sig="ac")
            m5 = P.op("act", lambda e, lp=lp: e.activation(rstd[:, lp:lp + 1], ln_t[:, lp:lp + 1],
                                                           AF.Exp, bias=0.0, scale=-0.5),
                      waits=[m4], sig="ac")
            m6 = P.op("dve", lambda e, lp=lp: e.tensor_scalar(negmr[:, lp:lp + 1], mv[:, lp, 0:1],
                                                              rstd[:, lp:lp + 1], -1.0,
                                                              ALU.mult, ALU.mult),
                      waits=[m5, m3], sig="dv")
            m7 = P.op("dve", lambda e, i2=i2, lp=lp: e.tensor_scalar(
                          xn_b[:, i2, :], src, rstd[:, lp:lp + 1], negmr[:, lp:lp + 1],
                          ALU.mult, ALU.add),
                      waits=[m6, r_xn[i2]], sig="dv")
            w_xn[i2] = m7
            m_ln_last[lp] = m7
            r_rstd[lp] = m7
            return m7

        def ln_chain(src, i2, extra_wait, lp):
            return ln_fin(src, i2, ln_stats(src, extra_wait, lp), lp)

        def transpose8(src1024_bf, waits_first):
            m_tr = None
            for c in range(KC):
                m_tr = P.op("pe", lambda e, c=c: e.transpose(
                                ps_bf[:, c * 128:(c + 1) * 128],
                                src1024_bf[:, c * 128:(c + 1) * 128], id_b[:]),
                            waits=(waits_first + [m_idb]) if c == 0 else [],
                            sig="pe" if c == KC - 1 else None)
            return m_tr

        qkv_banks = [ps_sc[:, i * 512:(i + 1) * 512] for i in range(4)]

        def mats_gen(tb):
            nonlocal qkv_bank, m_v_evac, last_A_pe
            tbuf = tb % 2
            for oc in range(16):
                mm = None
                for kc in range(KC):
                    mm = P.op("pe", lambda e, oc=oc, kc=kc, b=qkv_bank, tbuf=tbuf:
                              e.matmul(qkv_banks[b][:, :],
                                       wq_a[:, kc, oc * 128:(oc + 1) * 128],
                                       xnT(tbuf)[:, kc, :],
                                       start=(kc == 0), stop=(kc == KC - 1)),
                              waits=[(m_wq if oc < 8 else m_wk), w_xnt[tbuf],
                                     bank_reader[qkv_bank]] if kc == 0 else [],
                              sig="pe" if kc == KC - 1 else None)
                dst = q_fm if oc < 8 else k_fm
                cc = oc % 8
                m_qk = P.op("act", lambda e, dst=dst, cc=cc, oc=oc, b=qkv_bank, tb=tb:
                            e.activation(dst[:, cc, tb * 512:(tb + 1) * 512],
                                         qkv_banks[b][:, :], AF.Identity,
                                         bias=bqkvT[:, oc:oc + 1], scale=1.0),
                            waits=[mm, m_bqkvT], sig="ac")
                bank_reader[qkv_bank] = m_qk
                r_xnt[tbuf] = mm
                qkv_bank = (qkv_bank + 1) % 4
                if oc % 2 == 1:
                    yield
            for tt in range(4):
                kt = tb * 4 + tt
                mmv = [None, None]
                for half in range(2):
                    for kc in range(KC):
                        mmv[half] = P.op("pe", lambda e, half=half, kc=kc, tbuf=tbuf, tt=tt:
                                   e.matmul(ps_pv[half][:, 0:512],
                                            xnT(tbuf)[:, kc, tt * 128:(tt + 1) * 128],
                                            wq_a[:, kc, 2 * D + half * 512:2 * D + (half + 1) * 512],
                                            start=(kc == 0), stop=(kc == KC - 1)),
                                   waits=[m_v_evac[half], w_xnt[tbuf], m_wv] if kc == 0 else [],
                                   sig="pe" if kc == KC - 1 else None)
                r_xnt[tbuf] = mmv[1]
                for half in range(2):
                    m_v_evac[half] = P.op("dve", lambda e, kt=kt, half=half: e.tensor_tensor(
                                        v_on[:, kt, half * 528:(half + 1) * 528]
                                        .rearrange("p (h n) -> p h n", h=H // 2)[:, :, 0:64],
                                        ps_pv[half].rearrange("p (h n) -> p h n", h=H // 2),
                                        bv_bc[:, half * 512:(half + 1) * 512]
                                        .rearrange("p (h n) -> p h n", h=H // 2), ALU.add),
                                    waits=[mmv[half], m_bias["v"], m_vones], sig="dv")
                last_A_pe = mmv[1]
                yield

        a_gen = [None]

        def pump_a():
            if a_gen[0] is not None:
                try:
                    next(a_gen[0])
                except StopIteration:
                    a_gen[0] = None

        def drain_a():
            while a_gen[0] is not None:
                pump_a()

        for t in range(T):
            tb, ti = t // 4, t % 4
            i3, i2 = t % 3, t % 2
            tbuf = tb % 2
            m_x = sdma(x_in(i3), x_t[t], waits=[r_xin[i3]], sig=f"dX{i3}")
            m_xn = ln_chain(x_in(i3), i2, m_x, t % 2)
            r_xin[i3] = m_xn
            pump_a()
            m_tr = transpose8(xn_b[:, i2, :], [m_xn, r_psbf])
            r_xn[i2] = m_tr
            m_ev = P.op("dve", lambda e, tbuf=tbuf, ti=ti: e.tensor_copy(
                            xnT(tbuf)[:, :, ti * 128:(ti + 1) * 128], psbf3),
                        waits=[m_tr, r_xnt[tbuf]], sig="dv")
            r_psbf = m_ev
            pump_a()
            if ti == 3:
                w_xnt[tbuf] = m_ev
                drain_a()
                a_gen[0] = mats_gen(tb)
        drain_a()
        m_qkv_done = bank_reader[(qkv_bank - 1) % 4]
        m_v_done = m_v_evac[1]

        m_wproj = gdma(wproj_bf, wproj_d.rearrange("(c p) n -> p c n", p=128),
                       waits=[last_A_pe], sig="dWp")
        m_bias["p"] = sdma(bpj_bc,
                           bproj_d.rearrange("(a b) -> a b", a=1).broadcast_to((128, D)),
                           waits=[m_v_done], sig="dBp")
        wffn_r = wffn_d.rearrange("(r p) n -> p r n", p=128)
        arena_guard = [r_xin[0], r_xin[1], r_xin[2]]
        m_wfc = []
        for c4 in range(4):
            g4 = arena_guard[c4 * 2:c4 * 2 + 2]
            m_wfc.append(gdma(wstage[:, c4 * 2:c4 * 2 + 2, :],
                              wffn_r[:, c4 * 2:c4 * 2 + 2, :],
                              waits=g4, sig=f"dWf{c4}"))
        m_wt_done = None
        for r in range(KC):
            m_tr = None
            for dc in range(KC):
                m_tr = P.op("pe", lambda e, r=r, dc=dc: e.transpose(
                                ps_bf[:, dc * 128:(dc + 1) * 128],
                                wstage[:, r, dc * 128:(dc + 1) * 128], id_b[:]),
                            waits=[m_wfc[r // 2], r_psbf] if dc == 0 else [],
                            sig="pe" if dc == KC - 1 else None)
            r_psbf = P.op("dve", lambda e, r=r: e.tensor_copy(
                              wt_bf[:, :, r * 128:(r + 1) * 128],
                              ps_bf[:, :].rearrange("p (c n) -> p c n", c=KC)),
                          waits=[m_tr, m_wproj if r == 0 else None], sig="dv")
        m_wt_done = m_tr
        m_wt_evac = r_psbf
        m_trc = None
        for c in range(KC):
            m_trc = P.op("pe", lambda e, c=c: e.transpose(
                             ps_bf[0:32, c * 128:(c + 1) * 128], c_st[:, c, :], id_b[:]),
                         waits=[m_cst, r_psbf] if c == 0 else [],
                         sig="pe" if c == KC - 1 else None)
        m_ct = P.op("dve", lambda e: e.tensor_copy(
                        ct_bf[0:32, :], ps_bf[0:32, :].rearrange("p (c n) -> p c n", c=KC)),
                    waits=[m_trc], sig="dv")
        r_psbf = m_ct

        pair_reader = [m_qkv_done, m_qkv_done]
        ch_reader = [last_A_pe] * 5
        gki = 0
        m_pv_read = [m_v_evac[0], m_v_evac[1]]
        w_ao = [None, None]
        r_ao = [None, None]
        r_psw = m_bqkvT
        r_aoT = [last_A_pe, last_A_pe]
        r_hT = [last_A_pe, last_A_pe]
        w_x2 = [None, None]
        r_x2 = [m_wt_done, m_wt_done]
        r_xrr = [m_wt_done, m_wt_done, m_wt_done]
        m_xr_all = [None] * T
        next_pref = 0
        r_pvsb = last_A_pe
        r_recips = None
        w_ao_g0 = [None]

        def emit_B_pair(g, hc, defer=None):
            nonlocal gki, r_pvsb, r_recips
            par = g % 2
            he, ho = 2 * hc, 2 * hc + 1
            K = 4 * g + 4
            pend = []
            m_pv_last = [None, None]
            if defer is not None:
                defer[0]()
            def pop_pv(drain=False):
                nonlocal m_pv_last
                while len(pend) > 6 or (drain and pend):
                    cp2, off2, ncol2, gate2, ki2, hd2 = pend.pop(0)
                    m_pv_last[hd2] = P.op("pe", lambda e, cp=cp2, off=off2, ncol=ncol2,
                                          hd=hd2, hh=2 * hc + hd2, ki2=ki2,
                                          st=(ki2 == 0), sp=(ki2 == K - 1):
                                          e.matmul(ps_pv[hd][0:66, off * 128:512],
                                                   v_on[:, ki2, 66 * hh:66 * hh + 66],
                                                   chunkpair(cp)[:, hd * 512:hd * 512 + ncol],
                                                   start=st, stop=sp),
                                          waits=[gate2, (m_pv_read[hd2] if ki2 == 0 else None),
                                                 (m_v_done if ki2 == 0 else None)],
                                          sig="pe")
                    ch_reader[cp2] = m_pv_last[hd2]

            for ki in range(K):
                off = max(0, ki - 4 * g)
                ncol = 512 - off * 128
                pp = gki % 2
                cp = gki % 5
                m_sc = None
                for hd, hp in ((0, 0), (1, 64)):
                    m_sc = P.op("pe", lambda e, pp=pp, hd=hd, hp=hp, off=off, ki=ki, hc=hc, g=g:
                                e.matmul(ps_sc[:, pp * 1024 + hd * 512 + off * 128:
                                              pp * 1024 + hd * 512 + 512],
                                         k_fm[hp:hp + 64, hc, ki * 128:(ki + 1) * 128],
                                         q_fm[hp:hp + 64, hc, g * 512 + off * 128:(g + 1) * 512],
                                         start=True, stop=True,
                                         tile_position=(hp, 0), skip_group_check=True),
                                waits=[pair_reader[pp], m_qkv_done] if hd == 0 else [],
                                sig="pe" if hd == 1 else None)
                m_ex = P.op("act", lambda e, pp=pp, cp=cp, off=off, ncol=ncol:
                            e.activation(chunkpair(cp).rearrange("p (a n) -> p a n", a=2)
                                         [:, :, 0:ncol],
                                         ps_sc[:, pp * 1024:(pp + 1) * 1024]
                                         .rearrange("p (a n) -> p a n", a=2)
                                         [:, :, off * 128:512], AF.Exp,
                                         bias=0.0, scale=0.125),
                            waits=[m_sc, ch_reader[cp]], sig="ac")
                pair_reader[pp] = m_ex
                for hd in range(2):
                    gate = m_ex
                    if ki >= 4 * g:
                        gate = P.op("pool", lambda e, cp=cp, hd=hd: e.tensor_tensor(
                                        chunkpair(cp)[:, hd * 512:hd * 512 + 128],
                                        chunkpair(cp)[:, hd * 512:hd * 512 + 128],
                                        tri01[:], ALU.mult),
                                    waits=[m_ex, m_cm], sig="pl")
                    pend.append((cp, off, ncol, gate, ki, hd))
                gki += 1
                if ki % 2 == 0:
                    pop_pv()
                pump()
            pop_pv(drain=True)
            if defer is not None:
                defer[1]()
            cell = []

            def norm_a(m_pv_last=tuple(m_pv_last)):
                nonlocal r_pvsb
                for hd in range(2):
                    m_evac = P.op("dve", lambda e, hd=hd: e.tensor_copy(
                                      pv_sbB[0:66, hd * 512:(hd + 1) * 512],
                                      ps_pv[hd][0:66, 0:512]),
                                  waits=[m_pv_last[hd], r_pvsb], sig="dv")
                    m_pv_read[hd] = m_evac
                    cell.append(m_evac)

            def normalize(par=par, hc=hc, g=g):
                nonlocal r_pvsb, r_recips, r_psbf
                m_pvT = None
                for hd in range(2):
                    for qt in range(4):
                        m_pvT = P.op("pe", lambda e, qt=qt, hd=hd: e.transpose(
                                         ps_bf[:, hd * 288 + qt * 72:hd * 288 + qt * 72 + 66],
                                         pv_sbB[0:66, hd * 512 + qt * 128:hd * 512 + (qt + 1) * 128],
                                         id_b[0:66, 0:66]),
                                     waits=[cell[hd], r_psbf] if qt == 0 else [],
                                     sig="pe" if (hd == 1 and qt == 3) else None)
                r_pvsb = m_pvT
                m_last = None
                for hd in range(2):
                    hh = 2 * hc + hd
                    m_rc = P.op("dve", lambda e, hd=hd: e.reciprocal(
                                    recips.rearrange("p (a b) -> p a b", b=1),
                                    ps_bf[:, hd * 288:hd * 288 + 288]
                                    .rearrange("p (a b) -> p a b", b=72)[:, :, 64:65]),
                                waits=[m_pvT, r_recips], sig="dv")
                    for qt in range(4):
                        dst = (ao(par, qt) if g > 0 else ao_g0(qt))
                        m_last = P.op("dve", lambda e, qt=qt, hh=hh, hd=hd, dst=dst:
                                      e.tensor_scalar(
                                          dst[:, hh * 64:(hh + 1) * 64],
                                          ps_bf[:, hd * 288 + qt * 72:hd * 288 + qt * 72 + 64],
                                          recips[:, qt:qt + 1], None, ALU.mult),
                                      waits=[m_rc, ((r_ao[par] if g > 0 else r_aoT[qt % 2])
                                                    if hc == 0 and hd == 0 and qt == 0 else None),
                                             (last_A_pe if g <= 2 and hc == 0 and hd == 0 and qt == 0 else None)],
                                      sig="dv")
                    r_recips = m_last
                r_psbf = m_last
                if hc == H // 2 - 1:
                    if g > 0:
                        w_ao[par] = m_last
                    else:
                        w_ao_g0[0] = m_last
            return (norm_a, normalize)

        cstate = {}

        def gen_C1a1(cg, qt):
            nonlocal next_pref, r_psbf
            par = cg % 2
            qi = 4 * cg + qt
            i3, i2 = qi % 3, qi % 2
            while next_pref <= min(qi + 2, T - 1):
                m_xr_all[next_pref] = sdma(x_rr(next_pref % 3), x_t[next_pref],
                                           waits=[r_xrr[next_pref % 3]], sig=f"dX{next_pref % 3}")
                next_pref += 1
            m_xpb = P.op("pool", lambda e, i3=i3: e.tensor_tensor(
                             x_rr(i3), x_rr(i3), bpj_bc[:], ALU.add),
                         waits=[m_xr_all[qi], m_bias["p"]], sig="pl")
            yield
            yield
            yield
            src_ao = ao(par, qt) if cg > 0 else ao_g0(qt)
            m_tr = transpose8(src_ao, [w_ao[par] if cg > 0 else w_ao_g0[0], r_psbf])
            yield
            if cg > 0:
                r_ao[par] = m_tr
            m_aoT = P.op("dve", lambda e, src_ao=src_ao: e.tensor_copy(
                             src_ao.rearrange("p (c n) -> p c n", c=KC), psbf3),
                         waits=[m_tr, r_aoT[i2]], sig="dv")
            r_psbf = m_aoT
            cstate[(cg, qt)] = [i2, qi, i3, src_ao, m_xpb, m_aoT, None, None]

        def gen_C1a2(cg, qt, use_sc=False):
            nonlocal next_pref, r_psw
            st = cstate[(cg, qt)]
            i2, qi, i3, src_ao, m_xpb, m_aoT = st[0], st[1], st[2], st[3], st[4], st[5]
            aoTv = src_ao.rearrange("p (c n) -> p c n", c=KC)
            mmp = None
            m_x2 = None
            for half in range(2):
                pdst = ps_sc[:, half * 512:(half + 1) * 512] if use_sc else ps_w[:, 0:512]
                if use_sc:
                    w0 = [m_aoT, m_wproj,
                          r_scp[half] if r_scp[half] is not None else pair_reader[0]]
                else:
                    w0 = [m_aoT, m_wproj, r_psw] if half == 0 else [m_x2]
                for kc in range(KC):
                    mmp = P.op("pe", lambda e, kc=kc, aoTv=aoTv, pdst=pdst, half=half:
                               e.matmul(pdst,
                                        aoTv[:, kc, :],
                                        wproj_bf[:, kc, half * 512:(half + 1) * 512],
                                        start=(kc == 0), stop=(kc == KC - 1)),
                               waits=w0 if kc == 0 else [],
                               sig="pe")
                    if kc == 3:
                        yield
                m_x2 = P.op("dve", lambda e, i2=i2, i3=i3, half=half, pdst=pdst, mmp=mmp:
                            e.scalar_tensor_tensor(
                                x2b(i2)[:, half * 512:(half + 1) * 512], pdst, 1.0,
                                x_rr(i3)[:, half * 512:(half + 1) * 512], ALU.mult, ALU.add),
                            waits=[mmp, m_xpb, r_x2[i2] if (half == 0 or use_sc) else None],
                            sig="dv")
                if use_sc:
                    r_scp[half] = m_x2
                yield
            r_aoT[i2] = mmp
            if not use_sc:
                r_psw = m_x2
            r_xrr[i3] = m_x2
            if next_pref <= min(qi + 3, T - 1):
                m_xr_all[next_pref] = sdma(x_rr(next_pref % 3), x_t[next_pref],
                                           waits=[r_xrr[next_pref % 3]], sig=f"dX{next_pref % 3}")
                next_pref += 1
            w_x2[i2] = m_x2
            yield
            st[6] = ln_stats(x2b(i2), m_x2, qi % 2)

        def gen_C1b(cg, qt):
            st = cstate[(cg, qt)]
            st[7] = ln_fin(x2b(st[0]), st[0], st[6], st[1] % 2)
            yield

        def gen_C2(cg, qt, use_pv=False):
            nonlocal r_psw, r_psbf
            st = cstate.pop((cg, qt))
            i2, qi, m_h = st[0], st[1], st[7]
            m_x2f = P.op("pool", lambda e, i2=i2: e.tensor_tensor(
                             x2b(i2), x2b(i2), bfn_bc[:], ALU.add),
                         waits=[m_h, m_bias["f"], r_x2[i2]], sig="pl")
            yield
            yield
            m_tr = transpose8(xn_b[:, i2, :], [m_h, r_psbf])
            yield
            m_hT = P.op("dve", lambda e, i2=i2: e.tensor_copy(
                            xn_b.rearrange("p a (c n) -> p a c n", c=KC)[:, i2],
                            psbf3),
                        waits=[m_tr, r_hT[i2]], sig="dv")
            r_xn[i2] = m_hT
            r_psbf = m_hT
            yield
            hTv = xn_b.rearrange("p a (c n) -> p a c n", c=KC)
            mmu = None
            for kc in range(KC):
                mmu = P.op("pe", lambda e, kc=kc, i2=i2:
                           e.matmul(ps_w[0:32, 0:128], vr_bf[:, kc, :], hTv[:, i2, kc, :],
                                    start=(kc == 0), stop=(kc == KC - 1)),
                           waits=[m_hT, m_vr, r_psw] if kc == 0 else [],
                           sig="pe" if kc == KC - 1 else None)
            m_u = P.op("dve", lambda e: e.tensor_copy(u_bf, ps_w[0:32, 0:128]),
                       waits=[mmu], sig="dv")
            r_psw = m_u
            yield
            def fdst(half):
                return ps_pv[half][:, 0:512] if use_pv else ps_w[:, 0:512]
            mmf = None
            m_y = None
            for half in range(2):
                w0 = [m_hT, m_wt_evac]
                if not use_pv:
                    w0.append(r_psw if half == 0 else m_y)
                if use_pv:
                    w0.append(r_pvffn[half] if r_pvffn[half] is not None
                              else m_pv_read[half])
                for kc in range(KC):
                    P.op("pe", lambda e, half=half, kc=kc, i2=i2:
                         e.matmul(fdst(half),
                                  hTv[:, i2, kc, :],
                                  wt_bf[:, kc, half * 512:(half + 1) * 512],
                                  start=(kc == 0), stop=False),
                         waits=w0 if kc == 0 else [],
                         sig=None)
                    if kc == 3:
                        yield
                mmf = P.op("pe", lambda e, half=half:
                           e.matmul(fdst(half), u_bf,
                                    ct_bf[0:32, half * 512:(half + 1) * 512],
                                    start=False, stop=True),
                           waits=[m_u, m_ct] if half == 0 else [],
                           sig="pe")
                m_y = P.op("dve", lambda e, i2=i2, half=half, mmf=mmf: e.scalar_tensor_tensor(
                               x2b(i2)[:, half * 512:(half + 1) * 512],
                               fdst(half), 1.0,
                               x2b(i2)[:, half * 512:(half + 1) * 512],
                               ALU.mult, ALU.add),
                           waits=[mmf, m_x2f] if half == 0 else [mmf], sig="dv")
                if use_pv:
                    r_pvffn[half] = m_y
                yield
            r_hT[i2] = mmf
            if not use_pv:
                r_psw = m_y
            m_dy = P.op("sync", lambda e, i2=i2, qi=qi: e.dma_start(
                            out=y_t[qi], in_=x2b(i2)),
                        waits=[m_y], sig=f"dY{i2}", inc=DMA16)
            r_x2[i2] = m_dy

        from collections import deque
        fillers = deque()

        def pump():
            while fillers:
                try:
                    next(fillers[0])
                    return
                except StopIteration:
                    fillers.popleft()

        def drain_fillers():
            while fillers:
                try:
                    next(fillers[0])
                except StopIteration:
                    fillers.popleft()

        slotmap = {
            0: [("c", 2, 2), ("a2", 3, 2)],
            1: [("b", 3, 2), ("a1", 0, 1)],
            2: [("c", 3, 2), ("a2", 0, 1)],
            3: [("b", 0, 1), ("a1", 1, 1)],
            4: [("c", 0, 1), ("a2", 1, 1)],
            5: [("b", 1, 1), ("a1", 2, 1)],
            6: [("c", 1, 1), ("a2", 2, 1)],
            7: [("b", 2, 1), ("a1", 3, 1)],
        }
        gens = {"a1": gen_C1a1, "a2": gen_C1a2, "b": gen_C1b, "c": gen_C2}
        r_pvffn = [None, None]
        r_scp = [None, None]
        norm = None
        for g in range(G):
            for p in range(H // 2):
                norm = emit_B_pair(g, p, defer=norm)
                for kind, qt, goff in slotmap.get(p, []):
                    if g - goff >= 0:
                        fillers.append(gens[kind](g - goff, qt))
        drain_fillers()
        if norm is not None:
            norm[0]()
            norm[1]()
        if G >= 2:
            for job in (gens["c"](G - 2, 2, use_pv=True), gens["a2"](G - 2, 3, use_sc=True),
                        gens["b"](G - 2, 3), gens["c"](G - 2, 3, use_pv=True)):
                fillers.append(job)
                drain_fillers()
        tail = [("a1", 0), ("a2", 0), ("a1", 1), ("a2", 1),
                ("b", 0), ("c", 0), ("b", 1), ("c", 1),
                ("a1", 2), ("a2", 2), ("a1", 3), ("a2", 3),
                ("b", 2), ("c", 2), ("b", 3), ("c", 3)]
        for kind, qt in tail:
            if kind == "c":
                job = gens["c"](G - 1, qt, use_pv=True)
            elif kind == "a2":
                job = gens["a2"](G - 1, qt, use_sc=True)
            else:
                job = gens[kind](G - 1, qt)
            fillers.append(job)
            drain_fillers()
        P.op("sync", lambda e: e.nop(),
             waits=[("dY0", P.cnt.get("dY0", 0)), ("dY1", P.cnt.get("dY1", 0))])

        P.emit(block, sems)
    return nc


_NC_CACHE = {}


def _get_nc(L):
    if L not in _NC_CACHE:
        _NC_CACHE[L] = build_nc(L)
    return _NC_CACHE[L]


def _consts():
    i, j = np.meshgrid(np.arange(128), np.arange(128), indexing="ij")
    tri01 = np.where(i <= j, np.float32(1.0), np.float32(0.0)).astype(bfdt)
    return {
        "identb": np.eye(128, dtype=np.float32).astype(bfdt),
        "identf": np.eye(66, dtype=np.float32),
        "tri01": tri01,
    }


def make_in_maps(x, Wqkv, bqkv, Wproj, bproj, W, b, V_r, C):
    shared = {
        "wqkv": np.ascontiguousarray(np.asarray(Wqkv, dtype=np.float32)),
        "bqkv": np.ascontiguousarray(np.asarray(bqkv, dtype=np.float32)),
        "wproj": np.ascontiguousarray(np.asarray(Wproj, dtype=np.float32)),
        "bproj": np.ascontiguousarray(np.asarray(bproj, dtype=np.float32)),
        "wffn": np.ascontiguousarray(np.asarray(W, dtype=np.float32)),
        "bffn": np.ascontiguousarray(np.asarray(b, dtype=np.float32)),
        "vr": np.ascontiguousarray(np.asarray(V_r, dtype=np.float32)),
        "cmat": np.ascontiguousarray(np.asarray(C, dtype=np.float32)),
        **_consts(),
    }
    return [dict(shared, x=np.ascontiguousarray(x[bi])) for bi in range(x.shape[0])]


def kernel(L0_attn_x, L0_attn_Wqkv, L0_attn_bqkv, L0_attn_Wproj, L0_attn_bproj,
           L0_ffn_W, L0_ffn_b, L0_ffn_V_r, L0_ffn_C):
    from concourse.bass_utils import run_bass_kernel_spmd
    x = np.ascontiguousarray(np.asarray(L0_attn_x, dtype=np.float32))
    B, L, _ = x.shape
    nc = _get_nc(L)
    in_maps = make_in_maps(x, L0_attn_Wqkv, L0_attn_bqkv, L0_attn_Wproj, L0_attn_bproj,
                           L0_ffn_W, L0_ffn_b, L0_ffn_V_r, L0_ffn_C)
    res = run_bass_kernel_spmd(nc, in_maps, core_ids=list(range(B)))
    return np.stack([np.asarray(res.results[bi]["y"]) for bi in range(B)], axis=0)


# version 41
# speedup vs baseline: 1.0004x; 1.0004x over previous
import numpy as np
import ml_dtypes
from contextlib import ExitStack

import concourse.bass as bass
import concourse.mybir as mybir

F32 = mybir.dt.float32
BF16 = mybir.dt.bfloat16
AF = mybir.ActivationFunctionType
ALU = mybir.AluOpType
bfdt = ml_dtypes.bfloat16

D = 1024
KC = 8
H = 16
NEG = -1.0e30
DMA16 = 16


class Plan:
    ENGS = ("sync", "pool", "pe", "dve", "act")

    def __init__(self):
        self.ops = {e: [] for e in self.ENGS}
        self.cnt = {}

    def op(self, eng, fn, waits=(), sig=None, inc=1):
        ws = [w for w in waits if w is not None]
        self.ops[eng].append([fn, ws, sig, inc])
        if sig is not None:
            self.cnt[sig] = self.cnt.get(sig, 0) + inc
            return (sig, self.cnt[sig])
        return None

    def emit(self, block, sems):
        def make(ops):
            def run(e):
                seen = {}
                for fn, waits, sig, inc in ops:
                    for s, v in waits:
                        if seen.get(s, 0) < v:
                            e.wait_ge(sems[s], v)
                            seen[s] = v
                    instr = fn(e)
                    if sig is not None:
                        instr.then_inc(sems[sig], inc)
            return run
        block.sync(make(self.ops["sync"]))
        block.gpsimd(make(self.ops["pool"]))
        block.tensor(make(self.ops["pe"]))
        block.vector(make(self.ops["dve"]))
        block.scalar(make(self.ops["act"]))


def build_nc(L):
    T = L // 128
    G = L // 512
    assert L % 512 == 0

    nc = bass.Bass("TRN2", target_bir_lowering=False)

    x_d = nc.dram_tensor("x", [L, D], F32, kind="ExternalInput")
    wqkv_d = nc.dram_tensor("wqkv", [D, 3 * D], F32, kind="ExternalInput")
    bqkv_d = nc.dram_tensor("bqkv", [3 * D], F32, kind="ExternalInput")
    wproj_d = nc.dram_tensor("wproj", [D, D], F32, kind="ExternalInput")
    bproj_d = nc.dram_tensor("bproj", [D], F32, kind="ExternalInput")
    wffn_d = nc.dram_tensor("wffn", [D, D], F32, kind="ExternalInput")
    bffn_d = nc.dram_tensor("bffn", [D], F32, kind="ExternalInput")
    vr_d = nc.dram_tensor("vr", [D, 32], F32, kind="ExternalInput")
    cmat_d = nc.dram_tensor("cmat", [D, 32], F32, kind="ExternalInput")
    identb_d = nc.dram_tensor("identb", [128, 128], BF16, kind="ExternalInput")
    identf_d = nc.dram_tensor("identf", [66, 66], F32, kind="ExternalInput")
    tri01_d = nc.dram_tensor("tri01", [128, 128], BF16, kind="ExternalInput")
    y_d = nc.dram_tensor("y", [L, D], F32, kind="ExternalOutput")

    x_t = x_d.rearrange("(a p) n -> a p n", p=128)
    y_t = y_d.rearrange("(a p) n -> a p n", p=128)

    with ExitStack() as ctx:
        E = ctx.enter_context
        q_fm = E(nc.sbuf_tensor("q_fm", [128, KC, L], BF16))
        k_fm = E(nc.sbuf_tensor("k_fm", [128, KC, L], BF16))
        v_on = E(nc.sbuf_tensor("v_on", [128, T, 1118], BF16))
        wq_a = E(nc.sbuf_tensor("wq_a", [128, KC, 3 * D], BF16))
        xnt_a = E(nc.sbuf_tensor("xnt_a", [128, 8192], BF16))
        xf_a = E(nc.sbuf_tensor("xf_a", [128, 6656], F32))
        xn_b = E(nc.sbuf_tensor("xn_b", [128, 2, D], BF16))
        bias_a = E(nc.sbuf_tensor("bias_a", [128, 2 * D], F32))
        bv_bc = bias_a[:, 0:D]
        bpj_bc = bias_a[:, 0:D]
        bfn_bc = bias_a[:, D:2 * D]
        tri01 = E(nc.sbuf_tensor("tri01_sb", [128, 128], BF16))
        id_b = E(nc.sbuf_tensor("id_b", [128, 128], BF16))
        id_f = E(nc.sbuf_tensor("id_f", [66, 66], F32))
        bqkvT = E(nc.sbuf_tensor("bqkvT", [128, 24], F32))
        bqkv_sb = E(nc.sbuf_tensor("bqkv_sb", [24, 128], F32))
        vr_bf = E(nc.sbuf_tensor("vr_bf", [128, KC, 32], BF16))
        c_st = E(nc.sbuf_tensor("c_st", [128, KC, 32], BF16))
        ct_bf = E(nc.sbuf_tensor("ct_bf", [32, D], BF16))
        stats = E(nc.sbuf_tensor("stats", [128, 2, 12], F32))
        mv = E(nc.sbuf_tensor("mv", [128, 2, 2], F32))
        ln_t = E(nc.sbuf_tensor("ln_t", [128, 2], F32))
        rstd = E(nc.sbuf_tensor("rstd", [128, 2], F32))
        negmr = E(nc.sbuf_tensor("negmr", [128, 2], F32))
        recips = E(nc.sbuf_tensor("recips", [128, 4], F32))
        eps_t = E(nc.sbuf_tensor("eps_t", [128, 1], F32))
        ps_sc = E(nc.psum_tensor("ps_sc", [128, 2048], F32))
        ps_w = E(nc.psum_tensor("ps_w", [128, 512], F32))
        ps_bf = E(nc.psum_tensor("ps_bf", [128, 1024], BF16))
        ps_pv = [E(nc.psum_tensor(f"ps_pv{i}", [128, 512], F32)) for i in range(2)]
        semnames = ["dI", "dX0", "dX1", "dX2", "dY0", "dY1", "dWq", "dWk", "dWv", "dWs", "dWp", "dWf0", "dWf1", "dWf2", "dWf3", "dBp", "pe", "dv", "ac", "pl"]
        sems = {n: E(nc.semaphore(n)) for n in semnames}
        block = E(nc.Block())

        P = Plan()

        def xnT(buf):
            return xnt_a[:, buf * 4096:(buf + 1) * 4096].rearrange("p (c n) -> p c n", c=KC)
        def aoT(buf):
            return xnt_a[:, buf * 1024:(buf + 1) * 1024].rearrange("p (c n) -> p c n", c=KC)
        def hT(buf):
            return xnt_a[:, 2048 + buf * 1024:2048 + (buf + 1) * 1024].rearrange("p (c n) -> p c n", c=KC)
        def chunkpair(cp):
            if cp >= 2:
                o = 5120 + (cp - 2) * 512
                return xf_a[:, o:o + 512].bitcast(BF16)
            return xnt_a[:, 4096 + cp * 1024:4096 + (cp + 1) * 1024]
        u_bf = xnt_a[0:32, 7168:7296]
        def ao_g0(qt):
            return xnt_a[:, qt * 1024:(qt + 1) * 1024]
        pv_sbB = xnt_a[:, 6144:7168]
        def x_in(i):
            return xf_a[:, i * D:(i + 1) * D]
        x_rr = x_in
        def x2b(i):
            return xf_a[:, 3072 + i * D:3072 + (i + 1) * D]

        wstage = xf_a[:, 0:4096].bitcast(BF16).rearrange("p (c n) -> p c n", c=KC)
        wproj_bf = wq_a[:, :, 0:D]
        wt_bf = wq_a[:, :, D:2 * D]
        def ao(par, qt):
            return wq_a[:, 4 * par + qt, 2 * D:3 * D]
        psbf3 = ps_bf.rearrange("p (c n) -> p c n", c=KC)

        def sdma(dst, src, waits=(), sig="dI"):
            return P.op("sync", lambda e, d=dst, s=src: e.dma_start(out=d, in_=s),
                        waits=waits, sig=sig, inc=DMA16)
        m_idb = sdma(id_b[:], identb_d.ap())
        m_idf = sdma(id_f[:], identf_d.ap())
        m_cm = sdma(tri01[:], tri01_d.ap())
        m_bq = sdma(bqkv_sb[:], bqkv_d.rearrange("(a b) -> a b", b=128))
        m_init = ("dI", P.cnt["dI"])
        m_idb = m_idf = m_cm = m_bq = m_init

        def gdma(dst, src, waits=(), sig="dWs"):
            return P.op("pool", lambda e, d=dst, s=src: e.dma_start(out=d, in_=s),
                        waits=waits, sig=sig, inc=DMA16)
        wqkv_r = wqkv_d.rearrange("(c p) n -> p c n", p=128)
        m_wq = gdma(wq_a[:, :, 0:D], wqkv_r[:, :, 0:D], sig="dWq")
        m_wk = gdma(wq_a[:, :, D:2 * D], wqkv_r[:, :, D:2 * D], sig="dWk")
        m_wv = gdma(wq_a[:, :, 2 * D:3 * D], wqkv_r[:, :, 2 * D:3 * D], sig="dWv")
        gdma(vr_bf[:, :, :], vr_d.rearrange("(c p) n -> p c n", p=128))
        gdma(c_st[:, :, :], cmat_d.rearrange("(c p) n -> p c n", p=128))
        m_vr = m_cst = ("dWs", 64)
        gdma(bv_bc, bqkv_d.rearrange("(a b) -> a b", a=3)[2:3, :].broadcast_to((128, D)),
             sig="dWs")
        gdma(bfn_bc, bffn_d.rearrange("(a b) -> a b", a=1).broadcast_to((128, D)),
             sig="dWs")
        m_bv_bc = m_bfn_bc = ("dWs", 64)

        m_eps = P.op("dve", lambda e: e.memset(eps_t[:], 1e-5), sig="dv")
        m_vones = P.op("dve", lambda e: e.memset(
                           v_on[:, :, 0:1056].rearrange("p t (h n) -> p t h n", h=H)[:, :, :, 64:66],
                           1.0),
                       waits=[m_eps], sig="dv")
        m_vones = P.op("dve", lambda e: e.memset(v_on[:, :, 1056:1118], 0.0),
                       waits=[m_vones], sig="dv")
        m_tbq = P.op("pe", lambda e: e.transpose(ps_w[0:128, 0:24], bqkv_sb[0:24, 0:128],
                                                 id_f[0:24, 0:24]),
                     waits=[m_bq, m_idf], sig="pe")
        m_bqkvT = P.op("dve", lambda e: e.tensor_copy(bqkvT[:], ps_w[0:128, 0:24]),
                       waits=[m_tbq, m_vones], sig="dv")
        m_bias = {"v": m_bv_bc, "f": m_bfn_bc}
        m_psw_free = None

        r_xin = [None, None, None]
        w_xn = [None, None]
        r_xn = [None, None]
        w_xnt = [None, None]
        r_xnt = [None, None]
        r_psbf = None
        m_ln_last = [None, None]
        r_rstd = [None, None]
        bank_reader = [None, None, None, None]
        qkv_bank = 0
        m_v_evac = [None, None]
        last_A_pe = None

        def ln_stats(src, extra_wait, lp):
            m1 = P.op("dve", lambda e, lp=lp: e.bn_stats(stats[:, lp, 0:6], src[:, 0:512]),
                      waits=[extra_wait, m_ln_last[lp]], sig="dv")
            m2 = P.op("dve", lambda e, lp=lp: e.bn_stats(stats[:, lp, 6:12], src[:, 512:1024]),
                      waits=[m1], sig="dv")
            m3 = P.op("dve", lambda e, lp=lp: e.bn_aggr(mv[:, lp, :], stats[:, lp, :]),
                      waits=[m2], sig="dv")
            return m3

        def ln_fin(src, i2, m3, lp):
            m4 = P.op("act", lambda e, lp=lp: e.activation(ln_t[:, lp:lp + 1], mv[:, lp, 1:2],
                                                           AF.Ln, bias=eps_t[:], scale=1.0),
                      waits=[m3, r_rstd[lp], m_eps], sig="ac")
            m5 = P.op("act", lambda e, lp=lp: e.activation(rstd[:, lp:lp + 1], ln_t[:, lp:lp + 1],
                                                           AF.Exp, bias=0.0, scale=-0.5),
                      waits=[m4], sig="ac")
            m6 = P.op("dve", lambda e, lp=lp: e.tensor_scalar(negmr[:, lp:lp + 1], mv[:, lp, 0:1],
                                                              rstd[:, lp:lp + 1], -1.0,
                                                              ALU.mult, ALU.mult),
                      waits=[m5, m3], sig="dv")
            m7 = P.op("dve", lambda e, i2=i2, lp=lp: e.tensor_scalar(
                          xn_b[:, i2, :], src, rstd[:, lp:lp + 1], negmr[:, lp:lp + 1],
                          ALU.mult, ALU.add),
                      waits=[m6, r_xn[i2]], sig="dv")
            w_xn[i2] = m7
            m_ln_last[lp] = m7
            r_rstd[lp] = m7
            return m7

        def ln_chain(src, i2, extra_wait, lp):
            return ln_fin(src, i2, ln_stats(src, extra_wait, lp), lp)

        def transpose8(src1024_bf, waits_first):
            m_tr = None
            for c in range(KC):
                m_tr = P.op("pe", lambda e, c=c: e.transpose(
                                ps_bf[:, c * 128:(c + 1) * 128],
                                src1024_bf[:, c * 128:(c + 1) * 128], id_b[:]),
                            waits=(waits_first + [m_idb]) if c == 0 else [],
                            sig="pe" if c == KC - 1 else None)
            return m_tr

        qkv_banks = [ps_sc[:, i * 512:(i + 1) * 512] for i in range(4)]

        def mats_gen(tb):
            nonlocal qkv_bank, m_v_evac, last_A_pe
            tbuf = tb % 2
            for oc in range(16):
                mm = None
                for kc in range(KC):
                    mm = P.op("pe", lambda e, oc=oc, kc=kc, b=qkv_bank, tbuf=tbuf:
                              e.matmul(qkv_banks[b][:, :],
                                       wq_a[:, kc, oc * 128:(oc + 1) * 128],
                                       xnT(tbuf)[:, kc, :],
                                       start=(kc == 0), stop=(kc == KC - 1)),
                              waits=[(m_wq if oc < 8 else m_wk), w_xnt[tbuf],
                                     bank_reader[qkv_bank]] if kc == 0 else [],
                              sig="pe" if kc == KC - 1 else None)
                dst = q_fm if oc < 8 else k_fm
                cc = oc % 8
                m_qk = P.op("act", lambda e, dst=dst, cc=cc, oc=oc, b=qkv_bank, tb=tb:
                            e.activation(dst[:, cc, tb * 512:(tb + 1) * 512],
                                         qkv_banks[b][:, :], AF.Identity,
                                         bias=bqkvT[:, oc:oc + 1], scale=1.0),
                            waits=[mm, m_bqkvT], sig="ac")
                bank_reader[qkv_bank] = m_qk
                r_xnt[tbuf] = mm
                qkv_bank = (qkv_bank + 1) % 4
                if oc % 2 == 1:
                    yield
            for tt in range(4):
                kt = tb * 4 + tt
                mmv = [None, None]
                for half in range(2):
                    for kc in range(KC):
                        mmv[half] = P.op("pe", lambda e, half=half, kc=kc, tbuf=tbuf, tt=tt:
                                   e.matmul(ps_pv[half][:, 0:512],
                                            xnT(tbuf)[:, kc, tt * 128:(tt + 1) * 128],
                                            wq_a[:, kc, 2 * D + half * 512:2 * D + (half + 1) * 512],
                                            start=(kc == 0), stop=(kc == KC - 1)),
                                   waits=[m_v_evac[half], w_xnt[tbuf], m_wv] if kc == 0 else [],
                                   sig="pe" if kc == KC - 1 else None)
                r_xnt[tbuf] = mmv[1]
                for half in range(2):
                    m_v_evac[half] = P.op("dve", lambda e, kt=kt, half=half: e.tensor_tensor(
                                        v_on[:, kt, half * 528:(half + 1) * 528]
                                        .rearrange("p (h n) -> p h n", h=H // 2)[:, :, 0:64],
                                        ps_pv[half].rearrange("p (h n) -> p h n", h=H // 2),
                                        bv_bc[:, half * 512:(half + 1) * 512]
                                        .rearrange("p (h n) -> p h n", h=H // 2), ALU.add),
                                    waits=[mmv[half], m_bias["v"], m_vones], sig="dv")
                last_A_pe = mmv[1]
                yield

        a_gen = [None]

        def pump_a():
            if a_gen[0] is not None:
                try:
                    next(a_gen[0])
                except StopIteration:
                    a_gen[0] = None

        def drain_a():
            while a_gen[0] is not None:
                pump_a()

        for t in range(T):
            tb, ti = t // 4, t % 4
            i3, i2 = t % 3, t % 2
            tbuf = tb % 2
            m_x = sdma(x_in(i3), x_t[t], waits=[r_xin[i3]], sig=f"dX{i3}")
            m_xn = ln_chain(x_in(i3), i2, m_x, t % 2)
            r_xin[i3] = m_xn
            pump_a()
            m_tr = transpose8(xn_b[:, i2, :], [m_xn, r_psbf])
            r_xn[i2] = m_tr
            m_ev = P.op("dve", lambda e, tbuf=tbuf, ti=ti: e.tensor_copy(
                            xnT(tbuf)[:, :, ti * 128:(ti + 1) * 128], psbf3),
                        waits=[m_tr, r_xnt[tbuf]], sig="dv")
            r_psbf = m_ev
            pump_a()
            if ti == 3:
                w_xnt[tbuf] = m_ev
                drain_a()
                a_gen[0] = mats_gen(tb)
        drain_a()
        m_qkv_done = bank_reader[(qkv_bank - 1) % 4]
        m_v_done = m_v_evac[1]

        m_wproj = gdma(wproj_bf, wproj_d.rearrange("(c p) n -> p c n", p=128),
                       waits=[last_A_pe], sig="dWp")
        m_bias["p"] = sdma(bpj_bc,
                           bproj_d.rearrange("(a b) -> a b", a=1).broadcast_to((128, D)),
                           waits=[m_v_done], sig="dBp")
        wffn_r = wffn_d.rearrange("(r p) n -> p r n", p=128)
        arena_guard = [r_xin[0], r_xin[1], r_xin[2]]
        m_wfc = []
        for c4 in range(4):
            g4 = arena_guard[c4 * 2:c4 * 2 + 2]
            m_wfc.append(gdma(wstage[:, c4 * 2:c4 * 2 + 2, :],
                              wffn_r[:, c4 * 2:c4 * 2 + 2, :],
                              waits=g4, sig=f"dWf{c4}"))
        m_wt_done = None
        for r in range(KC):
            m_tr = None
            for dc in range(KC):
                m_tr = P.op("pe", lambda e, r=r, dc=dc: e.transpose(
                                ps_bf[:, dc * 128:(dc + 1) * 128],
                                wstage[:, r, dc * 128:(dc + 1) * 128], id_b[:]),
                            waits=[m_wfc[r // 2], r_psbf] if dc == 0 else [],
                            sig="pe" if dc == KC - 1 else None)
            r_psbf = P.op("dve", lambda e, r=r: e.tensor_copy(
                              wt_bf[:, :, r * 128:(r + 1) * 128],
                              ps_bf[:, :].rearrange("p (c n) -> p c n", c=KC)),
                          waits=[m_tr, m_wproj if r == 0 else None], sig="dv")
        m_wt_done = m_tr
        m_wt_evac = r_psbf
        m_trc = None
        for c in range(KC):
            m_trc = P.op("pe", lambda e, c=c: e.transpose(
                             ps_bf[0:32, c * 128:(c + 1) * 128], c_st[:, c, :], id_b[:]),
                         waits=[m_cst, r_psbf] if c == 0 else [],
                         sig="pe" if c == KC - 1 else None)
        m_ct = P.op("dve", lambda e: e.tensor_copy(
                        ct_bf[0:32, :], ps_bf[0:32, :].rearrange("p (c n) -> p c n", c=KC)),
                    waits=[m_trc], sig="dv")
        r_psbf = m_ct

        pair_reader = [m_qkv_done, m_qkv_done]
        ch_reader = [last_A_pe] * 5
        gki = 0
        m_pv_read = [m_v_evac[0], m_v_evac[1]]
        w_ao = [None, None]
        r_ao = [None, None]
        r_psw = m_bqkvT
        r_aoT = [last_A_pe, last_A_pe]
        r_hT = [last_A_pe, last_A_pe]
        w_x2 = [None, None]
        r_x2 = [m_wt_done, m_wt_done]
        r_xrr = [m_wt_done, m_wt_done, m_wt_done]
        m_xr_all = [None] * T
        next_pref = 0
        r_pvsb = last_A_pe
        r_recips = None
        w_ao_g0 = [None]

        def emit_B_pair(g, hc, defer=None):
            nonlocal gki, r_pvsb, r_recips
            par = g % 2
            he, ho = 2 * hc, 2 * hc + 1
            K = 4 * g + 4
            pend = []
            m_pv_last = [None, None]
            if defer is not None:
                defer[0]()
            def pop_pv(drain=False, cap=6):
                nonlocal m_pv_last
                while len(pend) > cap or (drain and pend):
                    cp2, off2, ncol2, gate2, ki2, hd2 = pend.pop(0)
                    m_pv_last[hd2] = P.op("pe", lambda e, cp=cp2, off=off2, ncol=ncol2,
                                          hd=hd2, hh=2 * hc + hd2, ki2=ki2,
                                          st=(ki2 == 0), sp=(ki2 == K - 1):
                                          e.matmul(ps_pv[hd][0:66, off * 128:512],
                                                   v_on[:, ki2, 66 * hh:66 * hh + 66],
                                                   chunkpair(cp)[:, hd * 512:hd * 512 + ncol],
                                                   start=st, stop=sp),
                                          waits=[gate2, (m_pv_read[hd2] if ki2 == 0 else None),
                                                 (m_v_done if ki2 == 0 else None)],
                                          sig="pe")
                    ch_reader[cp2] = m_pv_last[hd2]

            for ki in range(K):
                off = max(0, ki - 4 * g)
                ncol = 512 - off * 128
                pp = gki % 2
                cp = gki % 5
                m_sc = None
                for hd, hp in ((0, 0), (1, 64)):
                    m_sc = P.op("pe", lambda e, pp=pp, hd=hd, hp=hp, off=off, ki=ki, hc=hc, g=g:
                                e.matmul(ps_sc[:, pp * 1024 + hd * 512 + off * 128:
                                              pp * 1024 + hd * 512 + 512],
                                         k_fm[hp:hp + 64, hc, ki * 128:(ki + 1) * 128],
                                         q_fm[hp:hp + 64, hc, g * 512 + off * 128:(g + 1) * 512],
                                         start=True, stop=True,
                                         tile_position=(hp, 0), skip_group_check=True),
                                waits=[pair_reader[pp], m_qkv_done] if hd == 0 else [],
                                sig="pe" if hd == 1 else None)
                m_ex = P.op("act", lambda e, pp=pp, cp=cp, off=off, ncol=ncol:
                            e.activation(chunkpair(cp).rearrange("p (a n) -> p a n", a=2)
                                         [:, :, 0:ncol],
                                         ps_sc[:, pp * 1024:(pp + 1) * 1024]
                                         .rearrange("p (a n) -> p a n", a=2)
                                         [:, :, off * 128:512], AF.Exp,
                                         bias=0.0, scale=0.125),
                            waits=[m_sc, ch_reader[cp]], sig="ac")
                pair_reader[pp] = m_ex
                for hd in range(2):
                    gate = m_ex
                    if ki >= 4 * g:
                        gate = P.op("pool", lambda e, cp=cp, hd=hd: e.tensor_tensor(
                                        chunkpair(cp)[:, hd * 512:hd * 512 + 128],
                                        chunkpair(cp)[:, hd * 512:hd * 512 + 128],
                                        tri01[:], ALU.mult),
                                    waits=[m_ex, m_cm], sig="pl")
                    pend.append((cp, off, ncol, gate, ki, hd))
                gki += 1
                if ki % 2 == 1:
                    pop_pv(cap=2 if ki == K - 1 else 6)
                elif ki == K - 2:
                    pop_pv(cap=4)
                pump()
            pop_pv(drain=True)
            if defer is not None:
                defer[1]()
            cell = []

            def norm_a(m_pv_last=tuple(m_pv_last)):
                nonlocal r_pvsb
                for hd in range(2):
                    m_evac = P.op("dve", lambda e, hd=hd: e.tensor_copy(
                                      pv_sbB[0:66, hd * 512:(hd + 1) * 512],
                                      ps_pv[hd][0:66, 0:512]),
                                  waits=[m_pv_last[hd], r_pvsb], sig="dv")
                    m_pv_read[hd] = m_evac
                    cell.append(m_evac)

            def normalize(par=par, hc=hc, g=g):
                nonlocal r_pvsb, r_recips, r_psbf
                m_pvT = None
                for hd in range(2):
                    for qt in range(4):
                        m_pvT = P.op("pe", lambda e, qt=qt, hd=hd: e.transpose(
                                         ps_bf[:, hd * 288 + qt * 72:hd * 288 + qt * 72 + 66],
                                         pv_sbB[0:66, hd * 512 + qt * 128:hd * 512 + (qt + 1) * 128],
                                         id_b[0:66, 0:66]),
                                     waits=[cell[hd], r_psbf] if qt == 0 else [],
                                     sig="pe" if (hd == 1 and qt == 3) else None)
                r_pvsb = m_pvT
                m_last = None
                for hd in range(2):
                    hh = 2 * hc + hd
                    m_rc = P.op("dve", lambda e, hd=hd: e.reciprocal(
                                    recips.rearrange("p (a b) -> p a b", b=1),
                                    ps_bf[:, hd * 288:hd * 288 + 288]
                                    .rearrange("p (a b) -> p a b", b=72)[:, :, 64:65]),
                                waits=[m_pvT, r_recips], sig="dv")
                    for qt in range(4):
                        dst = (ao(par, qt) if g > 0 else ao_g0(qt))
                        m_last = P.op("dve", lambda e, qt=qt, hh=hh, hd=hd, dst=dst:
                                      e.tensor_scalar(
                                          dst[:, hh * 64:(hh + 1) * 64],
                                          ps_bf[:, hd * 288 + qt * 72:hd * 288 + qt * 72 + 64],
                                          recips[:, qt:qt + 1], None, ALU.mult),
                                      waits=[m_rc, ((r_ao[par] if g > 0 else r_aoT[qt % 2])
                                                    if hc == 0 and hd == 0 and qt == 0 else None),
                                             (last_A_pe if g <= 2 and hc == 0 and hd == 0 and qt == 0 else None)],
                                      sig="dv")
                    r_recips = m_last
                r_psbf = m_last
                if hc == H // 2 - 1:
                    if g > 0:
                        w_ao[par] = m_last
                    else:
                        w_ao_g0[0] = m_last
            return (norm_a, normalize)

        cstate = {}

        def gen_C1a1(cg, qt):
            nonlocal next_pref, r_psbf
            par = cg % 2
            qi = 4 * cg + qt
            i3, i2 = qi % 3, qi % 2
            while next_pref <= min(qi + 2, T - 1):
                m_xr_all[next_pref] = sdma(x_rr(next_pref % 3), x_t[next_pref],
                                           waits=[r_xrr[next_pref % 3]], sig=f"dX{next_pref % 3}")
                next_pref += 1
            m_xpb = P.op("pool", lambda e, i3=i3: e.tensor_tensor(
                             x_rr(i3), x_rr(i3), bpj_bc[:], ALU.add),
                         waits=[m_xr_all[qi], m_bias["p"]], sig="pl")
            yield
            yield
            yield
            src_ao = ao(par, qt) if cg > 0 else ao_g0(qt)
            m_tr = transpose8(src_ao, [w_ao[par] if cg > 0 else w_ao_g0[0], r_psbf])
            yield
            if cg > 0:
                r_ao[par] = m_tr
            m_aoT = P.op("dve", lambda e, src_ao=src_ao: e.tensor_copy(
                             src_ao.rearrange("p (c n) -> p c n", c=KC), psbf3),
                         waits=[m_tr, r_aoT[i2]], sig="dv")
            r_psbf = m_aoT
            cstate[(cg, qt)] = [i2, qi, i3, src_ao, m_xpb, m_aoT, None, None]

        def gen_C1a2(cg, qt, use_sc=False):
            nonlocal next_pref, r_psw
            st = cstate[(cg, qt)]
            i2, qi, i3, src_ao, m_xpb, m_aoT = st[0], st[1], st[2], st[3], st[4], st[5]
            aoTv = src_ao.rearrange("p (c n) -> p c n", c=KC)
            mmp = None
            m_x2 = None
            for half in range(2):
                pdst = ps_sc[:, half * 512:(half + 1) * 512] if use_sc else ps_w[:, 0:512]
                if use_sc:
                    w0 = [m_aoT, m_wproj,
                          r_scp[half] if r_scp[half] is not None else pair_reader[0]]
                else:
                    w0 = [m_aoT, m_wproj, r_psw] if half == 0 else [m_x2]
                for kc in range(KC):
                    mmp = P.op("pe", lambda e, kc=kc, aoTv=aoTv, pdst=pdst, half=half:
                               e.matmul(pdst,
                                        aoTv[:, kc, :],
                                        wproj_bf[:, kc, half * 512:(half + 1) * 512],
                                        start=(kc == 0), stop=(kc == KC - 1)),
                               waits=w0 if kc == 0 else [],
                               sig="pe")
                    if kc == 3:
                        yield
                m_x2 = P.op("dve", lambda e, i2=i2, i3=i3, half=half, pdst=pdst, mmp=mmp:
                            e.scalar_tensor_tensor(
                                x2b(i2)[:, half * 512:(half + 1) * 512], pdst, 1.0,
                                x_rr(i3)[:, half * 512:(half + 1) * 512], ALU.mult, ALU.add),
                            waits=[mmp, m_xpb, r_x2[i2] if (half == 0 or use_sc) else None],
                            sig="dv")
                if use_sc:
                    r_scp[half] = m_x2
                yield
            r_aoT[i2] = mmp
            if not use_sc:
                r_psw = m_x2
            r_xrr[i3] = m_x2
            if next_pref <= min(qi + 3, T - 1):
                m_xr_all[next_pref] = sdma(x_rr(next_pref % 3), x_t[next_pref],
                                           waits=[r_xrr[next_pref % 3]], sig=f"dX{next_pref % 3}")
                next_pref += 1
            w_x2[i2] = m_x2
            yield
            st[6] = ln_stats(x2b(i2), m_x2, qi % 2)

        def gen_C1b(cg, qt):
            st = cstate[(cg, qt)]
            st[7] = ln_fin(x2b(st[0]), st[0], st[6], st[1] % 2)
            yield

        def gen_C2(cg, qt, use_pv=False):
            nonlocal r_psw, r_psbf
            st = cstate.pop((cg, qt))
            i2, qi, m_h = st[0], st[1], st[7]
            m_x2f = P.op("pool", lambda e, i2=i2: e.tensor_tensor(
                             x2b(i2), x2b(i2), bfn_bc[:], ALU.add),
                         waits=[m_h, m_bias["f"], r_x2[i2]], sig="pl")
            yield
            yield
            m_tr = transpose8(xn_b[:, i2, :], [m_h, r_psbf])
            yield
            m_hT = P.op("dve", lambda e, i2=i2: e.tensor_copy(
                            xn_b.rearrange("p a (c n) -> p a c n", c=KC)[:, i2],
                            psbf3),
                        waits=[m_tr, r_hT[i2]], sig="dv")
            r_xn[i2] = m_hT
            r_psbf = m_hT
            yield
            hTv = xn_b.rearrange("p a (c n) -> p a c n", c=KC)
            mmu = None
            for kc in range(KC):
                mmu = P.op("pe", lambda e, kc=kc, i2=i2:
                           e.matmul(ps_w[0:32, 0:128], vr_bf[:, kc, :], hTv[:, i2, kc, :],
                                    start=(kc == 0), stop=(kc == KC - 1)),
                           waits=[m_hT, m_vr, r_psw] if kc == 0 else [],
                           sig="pe" if kc == KC - 1 else None)
            m_u = P.op("dve", lambda e: e.tensor_copy(u_bf, ps_w[0:32, 0:128]),
                       waits=[mmu], sig="dv")
            r_psw = m_u
            yield
            def fdst(half):
                return ps_pv[half][:, 0:512] if use_pv else ps_w[:, 0:512]
            mmf = None
            m_y = None
            for half in range(2):
                w0 = [m_hT, m_wt_evac]
                if not use_pv:
                    w0.append(r_psw if half == 0 else m_y)
                if use_pv:
                    w0.append(r_pvffn[half] if r_pvffn[half] is not None
                              else m_pv_read[half])
                for kc in range(KC):
                    P.op("pe", lambda e, half=half, kc=kc, i2=i2:
                         e.matmul(fdst(half),
                                  hTv[:, i2, kc, :],
                                  wt_bf[:, kc, half * 512:(half + 1) * 512],
                                  start=(kc == 0), stop=False),
                         waits=w0 if kc == 0 else [],
                         sig=None)
                    if kc == 3:
                        yield
                mmf = P.op("pe", lambda e, half=half:
                           e.matmul(fdst(half), u_bf,
                                    ct_bf[0:32, half * 512:(half + 1) * 512],
                                    start=False, stop=True),
                           waits=[m_u, m_ct] if half == 0 else [],
                           sig="pe")
                m_y = P.op("dve", lambda e, i2=i2, half=half, mmf=mmf: e.scalar_tensor_tensor(
                               x2b(i2)[:, half * 512:(half + 1) * 512],
                               fdst(half), 1.0,
                               x2b(i2)[:, half * 512:(half + 1) * 512],
                               ALU.mult, ALU.add),
                           waits=[mmf, m_x2f] if half == 0 else [mmf], sig="dv")
                if use_pv:
                    r_pvffn[half] = m_y
                yield
            r_hT[i2] = mmf
            if not use_pv:
                r_psw = m_y
            m_dy = P.op("sync", lambda e, i2=i2, qi=qi: e.dma_start(
                            out=y_t[qi], in_=x2b(i2)),
                        waits=[m_y], sig=f"dY{i2}", inc=DMA16)
            r_x2[i2] = m_dy

        from collections import deque
        fillers = deque()

        def pump():
            while fillers:
                try:
                    next(fillers[0])
                    return
                except StopIteration:
                    fillers.popleft()

        def drain_fillers():
            while fillers:
                try:
                    next(fillers[0])
                except StopIteration:
                    fillers.popleft()

        slotmap = {
            0: [("c", 2, 2), ("a2", 3, 2)],
            1: [("b", 3, 2), ("a1", 0, 1)],
            2: [("c", 3, 2), ("a2", 0, 1)],
            3: [("b", 0, 1), ("a1", 1, 1)],
            4: [("c", 0, 1), ("a2", 1, 1)],
            5: [("b", 1, 1), ("a1", 2, 1)],
            6: [("c", 1, 1), ("a2", 2, 1)],
            7: [("b", 2, 1), ("a1", 3, 1)],
        }
        gens = {"a1": gen_C1a1, "a2": gen_C1a2, "b": gen_C1b, "c": gen_C2}
        r_pvffn = [None, None]
        r_scp = [None, None]
        norm = None
        for g in range(G):
            for p in range(H // 2):
                norm = emit_B_pair(g, p, defer=norm)
                for kind, qt, goff in slotmap.get(p, []):
                    if g - goff >= 0:
                        fillers.append(gens[kind](g - goff, qt))
        drain_fillers()
        if norm is not None:
            norm[0]()
            norm[1]()
        if G >= 2:
            for job in (gens["c"](G - 2, 2, use_pv=True), gens["a2"](G - 2, 3, use_sc=True),
                        gens["b"](G - 2, 3), gens["c"](G - 2, 3, use_pv=True)):
                fillers.append(job)
                drain_fillers()
        tail = [("a1", 0), ("a2", 0), ("a1", 1), ("a2", 1),
                ("b", 0), ("c", 0), ("b", 1), ("c", 1),
                ("a1", 2), ("a2", 2), ("a1", 3), ("a2", 3),
                ("b", 2), ("c", 2), ("b", 3), ("c", 3)]
        for kind, qt in tail:
            if kind == "c":
                job = gens["c"](G - 1, qt, use_pv=True)
            elif kind == "a2":
                job = gens["a2"](G - 1, qt, use_sc=True)
            else:
                job = gens[kind](G - 1, qt)
            fillers.append(job)
            drain_fillers()
        P.op("sync", lambda e: e.nop(),
             waits=[("dY0", P.cnt.get("dY0", 0)), ("dY1", P.cnt.get("dY1", 0))])

        P.emit(block, sems)
    return nc


_NC_CACHE = {}


def _get_nc(L):
    if L not in _NC_CACHE:
        _NC_CACHE[L] = build_nc(L)
    return _NC_CACHE[L]


def _consts():
    i, j = np.meshgrid(np.arange(128), np.arange(128), indexing="ij")
    tri01 = np.where(i <= j, np.float32(1.0), np.float32(0.0)).astype(bfdt)
    return {
        "identb": np.eye(128, dtype=np.float32).astype(bfdt),
        "identf": np.eye(66, dtype=np.float32),
        "tri01": tri01,
    }


def make_in_maps(x, Wqkv, bqkv, Wproj, bproj, W, b, V_r, C):
    shared = {
        "wqkv": np.ascontiguousarray(np.asarray(Wqkv, dtype=np.float32)),
        "bqkv": np.ascontiguousarray(np.asarray(bqkv, dtype=np.float32)),
        "wproj": np.ascontiguousarray(np.asarray(Wproj, dtype=np.float32)),
        "bproj": np.ascontiguousarray(np.asarray(bproj, dtype=np.float32)),
        "wffn": np.ascontiguousarray(np.asarray(W, dtype=np.float32)),
        "bffn": np.ascontiguousarray(np.asarray(b, dtype=np.float32)),
        "vr": np.ascontiguousarray(np.asarray(V_r, dtype=np.float32)),
        "cmat": np.ascontiguousarray(np.asarray(C, dtype=np.float32)),
        **_consts(),
    }
    return [dict(shared, x=np.ascontiguousarray(x[bi])) for bi in range(x.shape[0])]


def kernel(L0_attn_x, L0_attn_Wqkv, L0_attn_bqkv, L0_attn_Wproj, L0_attn_bproj,
           L0_ffn_W, L0_ffn_b, L0_ffn_V_r, L0_ffn_C):
    from concourse.bass_utils import run_bass_kernel_spmd
    x = np.ascontiguousarray(np.asarray(L0_attn_x, dtype=np.float32))
    B, L, _ = x.shape
    nc = _get_nc(L)
    in_maps = make_in_maps(x, L0_attn_Wqkv, L0_attn_bqkv, L0_attn_Wproj, L0_attn_bproj,
                           L0_ffn_W, L0_ffn_b, L0_ffn_V_r, L0_ffn_C)
    res = run_bass_kernel_spmd(nc, in_maps, core_ids=list(range(B)))
    return np.stack([np.asarray(res.results[bi]["y"]) for bi in range(B)], axis=0)


# version 42
# speedup vs baseline: 1.0135x; 1.0131x over previous
import numpy as np
import ml_dtypes
from contextlib import ExitStack

import concourse.bass as bass
import concourse.mybir as mybir

F32 = mybir.dt.float32
BF16 = mybir.dt.bfloat16
AF = mybir.ActivationFunctionType
ALU = mybir.AluOpType
bfdt = ml_dtypes.bfloat16

D = 1024
KC = 8
H = 16
NEG = -1.0e30
DMA16 = 16


class Plan:
    ENGS = ("sync", "pool", "pe", "dve", "act")

    def __init__(self):
        self.ops = {e: [] for e in self.ENGS}
        self.cnt = {}

    def op(self, eng, fn, waits=(), sig=None, inc=1):
        ws = [w for w in waits if w is not None]
        self.ops[eng].append([fn, ws, sig, inc])
        if sig is not None:
            self.cnt[sig] = self.cnt.get(sig, 0) + inc
            return (sig, self.cnt[sig])
        return None

    def emit(self, block, sems):
        def make(ops):
            def run(e):
                seen = {}
                for fn, waits, sig, inc in ops:
                    for s, v in waits:
                        if seen.get(s, 0) < v:
                            e.wait_ge(sems[s], v)
                            seen[s] = v
                    instr = fn(e)
                    if sig is not None:
                        instr.then_inc(sems[sig], inc)
            return run
        block.sync(make(self.ops["sync"]))
        block.gpsimd(make(self.ops["pool"]))
        block.tensor(make(self.ops["pe"]))
        block.vector(make(self.ops["dve"]))
        block.scalar(make(self.ops["act"]))


def build_nc(L):
    T = L // 128
    G = L // 512
    assert L % 512 == 0

    nc = bass.Bass("TRN2", target_bir_lowering=False)

    x_d = nc.dram_tensor("x", [L, D], F32, kind="ExternalInput")
    wqkv_d = nc.dram_tensor("wqkv", [D, 3 * D], F32, kind="ExternalInput")
    bqkv_d = nc.dram_tensor("bqkv", [3 * D], F32, kind="ExternalInput")
    wproj_d = nc.dram_tensor("wproj", [D, D], F32, kind="ExternalInput")
    bproj_d = nc.dram_tensor("bproj", [D], F32, kind="ExternalInput")
    wffn_d = nc.dram_tensor("wffn", [D, D], F32, kind="ExternalInput")
    bffn_d = nc.dram_tensor("bffn", [D], F32, kind="ExternalInput")
    vr_d = nc.dram_tensor("vr", [D, 32], F32, kind="ExternalInput")
    cmat_d = nc.dram_tensor("cmat", [D, 32], F32, kind="ExternalInput")
    identb_d = nc.dram_tensor("identb", [128, 128], BF16, kind="ExternalInput")
    identf_d = nc.dram_tensor("identf", [66, 66], F32, kind="ExternalInput")
    tri01_d = nc.dram_tensor("tri01", [128, 128], BF16, kind="ExternalInput")
    y_d = nc.dram_tensor("y", [L, D], F32, kind="ExternalOutput")

    x_t = x_d.rearrange("(a p) n -> a p n", p=128)
    y_t = y_d.rearrange("(a p) n -> a p n", p=128)

    with ExitStack() as ctx:
        E = ctx.enter_context
        q_fm = E(nc.sbuf_tensor("q_fm", [128, KC, L], BF16))
        k_fm = E(nc.sbuf_tensor("k_fm", [128, KC, L], BF16))
        v_on = E(nc.sbuf_tensor("v_on", [128, T, 1118], BF16))
        wq_a = E(nc.sbuf_tensor("wq_a", [128, KC, 3 * D], BF16))
        xnt_a = E(nc.sbuf_tensor("xnt_a", [128, 8192], BF16))
        xf_a = E(nc.sbuf_tensor("xf_a", [128, 6656], F32))
        xn_b = E(nc.sbuf_tensor("xn_b", [128, 2, D], BF16))
        bias_a = E(nc.sbuf_tensor("bias_a", [128, 2 * D], F32))
        bv_bc = bias_a[:, 0:D]
        bpj_bc = bias_a[:, 0:D]
        bfn_bc = bias_a[:, D:2 * D]
        tri01 = E(nc.sbuf_tensor("tri01_sb", [128, 128], BF16))
        id_b = E(nc.sbuf_tensor("id_b", [128, 128], BF16))
        id_f = E(nc.sbuf_tensor("id_f", [66, 66], F32))
        bqkvT = E(nc.sbuf_tensor("bqkvT", [128, 24], F32))
        bqkv_sb = E(nc.sbuf_tensor("bqkv_sb", [24, 128], F32))
        vr_bf = E(nc.sbuf_tensor("vr_bf", [128, KC, 32], BF16))
        c_st = E(nc.sbuf_tensor("c_st", [128, KC, 32], BF16))
        ct_bf = E(nc.sbuf_tensor("ct_bf", [32, D], BF16))
        stats = E(nc.sbuf_tensor("stats", [128, 2, 12], F32))
        mv = E(nc.sbuf_tensor("mv", [128, 2, 2], F32))
        ln_t = E(nc.sbuf_tensor("ln_t", [128, 2], F32))
        rstd = E(nc.sbuf_tensor("rstd", [128, 2], F32))
        negmr = E(nc.sbuf_tensor("negmr", [128, 2], F32))
        recips = E(nc.sbuf_tensor("recips", [128, 4], F32))
        eps_t = E(nc.sbuf_tensor("eps_t", [128, 1], F32))
        ps_sc = E(nc.psum_tensor("ps_sc", [128, 2048], F32))
        ps_w = E(nc.psum_tensor("ps_w", [128, 512], F32))
        ps_bf = E(nc.psum_tensor("ps_bf", [128, 1024], BF16))
        ps_pv = [E(nc.psum_tensor(f"ps_pv{i}", [128, 512], F32)) for i in range(2)]
        semnames = ["dI", "dX0", "dX1", "dX2", "dY0", "dY1", "dWq", "dWk", "dWv", "dWs", "dWp", "dWf0", "dWf1", "dWf2", "dWf3", "dBp", "pe", "dv", "ac", "pl"]
        sems = {n: E(nc.semaphore(n)) for n in semnames}
        block = E(nc.Block())

        P = Plan()

        def xnT(buf):
            return xnt_a[:, buf * 4096:(buf + 1) * 4096].rearrange("p (c n) -> p c n", c=KC)
        def aoT(buf):
            return xnt_a[:, buf * 1024:(buf + 1) * 1024].rearrange("p (c n) -> p c n", c=KC)
        def hT(buf):
            return xnt_a[:, 2048 + buf * 1024:2048 + (buf + 1) * 1024].rearrange("p (c n) -> p c n", c=KC)
        def chunkpair(cp):
            if cp >= 2:
                o = 5120 + (cp - 2) * 512
                return xf_a[:, o:o + 512].bitcast(BF16)
            return xnt_a[:, 4096 + cp * 1024:4096 + (cp + 1) * 1024]
        u_bf = xnt_a[0:32, 7168:7296]
        def ao_g0(qt):
            return xnt_a[:, qt * 1024:(qt + 1) * 1024]
        pv_sbB = xnt_a[:, 6144:7168]
        def x_in(i):
            return xf_a[:, i * D:(i + 1) * D]
        x_rr = x_in
        def x2b(i):
            return xf_a[:, 3072 + i * D:3072 + (i + 1) * D]

        wstage = xf_a[:, 0:4096].bitcast(BF16).rearrange("p (c n) -> p c n", c=KC)
        wproj_bf = wq_a[:, :, 0:D]
        wt_bf = wq_a[:, :, D:2 * D]
        def ao(par, qt):
            return wq_a[:, 4 * par + qt, 2 * D:3 * D]
        psbf3 = ps_bf.rearrange("p (c n) -> p c n", c=KC)

        def sdma(dst, src, waits=(), sig="dI"):
            return P.op("sync", lambda e, d=dst, s=src: e.dma_start(out=d, in_=s),
                        waits=waits, sig=sig, inc=DMA16)
        m_idb = sdma(id_b[:], identb_d.ap())
        m_idf = sdma(id_f[:], identf_d.ap())
        m_cm = sdma(tri01[:], tri01_d.ap())
        m_bq = sdma(bqkv_sb[:], bqkv_d.rearrange("(a b) -> a b", b=128))
        m_init = ("dI", P.cnt["dI"])
        m_idb = m_idf = m_cm = m_bq = m_init

        def gdma(dst, src, waits=(), sig="dWs"):
            return P.op("pool", lambda e, d=dst, s=src: e.dma_start(out=d, in_=s),
                        waits=waits, sig=sig, inc=DMA16)
        wqkv_r = wqkv_d.rearrange("(c p) n -> p c n", p=128)
        m_wq = gdma(wq_a[:, :, 0:D], wqkv_r[:, :, 0:D], sig="dWq")
        m_wk = gdma(wq_a[:, :, D:2 * D], wqkv_r[:, :, D:2 * D], sig="dWk")
        m_wv = gdma(wq_a[:, :, 2 * D:3 * D], wqkv_r[:, :, 2 * D:3 * D], sig="dWv")
        gdma(vr_bf[:, :, :], vr_d.rearrange("(c p) n -> p c n", p=128))
        gdma(c_st[:, :, :], cmat_d.rearrange("(c p) n -> p c n", p=128))
        m_vr = m_cst = ("dWs", 64)
        gdma(bv_bc, bqkv_d.rearrange("(a b) -> a b", a=3)[2:3, :].broadcast_to((128, D)),
             sig="dWs")
        gdma(bfn_bc, bffn_d.rearrange("(a b) -> a b", a=1).broadcast_to((128, D)),
             sig="dWs")
        m_bv_bc = m_bfn_bc = ("dWs", 64)

        m_eps = P.op("dve", lambda e: e.memset(eps_t[:], 1e-5), sig="dv")
        m_vones = P.op("dve", lambda e: e.memset(
                           v_on[:, :, 0:1056].rearrange("p t (h n) -> p t h n", h=H)[:, :, :, 64:66],
                           1.0),
                       waits=[m_eps], sig="dv")
        m_vones = P.op("dve", lambda e: e.memset(v_on[:, :, 1056:1118], 0.0),
                       waits=[m_vones], sig="dv")
        m_tbq = P.op("pe", lambda e: e.transpose(ps_w[0:128, 0:24], bqkv_sb[0:24, 0:128],
                                                 id_f[0:24, 0:24]),
                     waits=[m_bq, m_idf], sig="pe")
        m_bqkvT = P.op("dve", lambda e: e.tensor_copy(bqkvT[:], ps_w[0:128, 0:24]),
                       waits=[m_tbq, m_vones], sig="dv")
        m_bias = {"v": m_bv_bc, "f": m_bfn_bc}
        m_psw_free = None

        r_xin = [None, None, None]
        w_xn = [None, None]
        r_xn = [None, None]
        w_xnt = [None, None]
        r_xnt = [None, None]
        r_psbf = None
        m_ln_last = [None, None]
        r_rstd = [None, None]
        bank_reader = [None, None, None, None]
        qkv_bank = 0
        m_v_evac = [None, None]
        last_A_pe = None

        def ln_stats(src, extra_wait, lp):
            m1 = P.op("dve", lambda e, lp=lp: e.bn_stats(stats[:, lp, 0:6], src[:, 0:512]),
                      waits=[extra_wait, m_ln_last[lp]], sig="dv")
            m2 = P.op("dve", lambda e, lp=lp: e.bn_stats(stats[:, lp, 6:12], src[:, 512:1024]),
                      waits=[m1], sig="dv")
            m3 = P.op("dve", lambda e, lp=lp: e.bn_aggr(mv[:, lp, :], stats[:, lp, :]),
                      waits=[m2], sig="dv")
            return m3

        def ln_fin(src, i2, m3, lp):
            m4 = P.op("act", lambda e, lp=lp: e.activation(ln_t[:, lp:lp + 1], mv[:, lp, 1:2],
                                                           AF.Ln, bias=eps_t[:], scale=1.0),
                      waits=[m3, r_rstd[lp], m_eps], sig="ac")
            m5 = P.op("act", lambda e, lp=lp: e.activation(rstd[:, lp:lp + 1], ln_t[:, lp:lp + 1],
                                                           AF.Exp, bias=0.0, scale=-0.5),
                      waits=[m4], sig="ac")
            m6 = P.op("dve", lambda e, lp=lp: e.tensor_scalar(negmr[:, lp:lp + 1], mv[:, lp, 0:1],
                                                              rstd[:, lp:lp + 1], -1.0,
                                                              ALU.mult, ALU.mult),
                      waits=[m5, m3], sig="dv")
            m7 = P.op("dve", lambda e, i2=i2, lp=lp: e.tensor_scalar(
                          xn_b[:, i2, :], src, rstd[:, lp:lp + 1], negmr[:, lp:lp + 1],
                          ALU.mult, ALU.add),
                      waits=[m6, r_xn[i2]], sig="dv")
            w_xn[i2] = m7
            m_ln_last[lp] = m7
            r_rstd[lp] = m7
            return m7

        def ln_chain(src, i2, extra_wait, lp):
            return ln_fin(src, i2, ln_stats(src, extra_wait, lp), lp)

        def transpose8(src1024_bf, waits_first):
            m_tr = None
            for c in range(KC):
                m_tr = P.op("pe", lambda e, c=c: e.transpose(
                                ps_bf[:, c * 128:(c + 1) * 128],
                                src1024_bf[:, c * 128:(c + 1) * 128], id_b[:]),
                            waits=(waits_first + [m_idb]) if c == 0 else [],
                            sig="pe" if c == KC - 1 else None)
            return m_tr

        qkv_banks = [ps_sc[:, i * 512:(i + 1) * 512] for i in range(4)]

        def mats_gen(tb):
            nonlocal qkv_bank, m_v_evac, last_A_pe
            tbuf = tb % 2
            for oc in range(16):
                mm = None
                for kc in range(KC):
                    mm = P.op("pe", lambda e, oc=oc, kc=kc, b=qkv_bank, tbuf=tbuf:
                              e.matmul(qkv_banks[b][:, :],
                                       wq_a[:, kc, oc * 128:(oc + 1) * 128],
                                       xnT(tbuf)[:, kc, :],
                                       start=(kc == 0), stop=(kc == KC - 1)),
                              waits=[(m_wq if oc < 8 else m_wk), w_xnt[tbuf],
                                     bank_reader[qkv_bank]] if kc == 0 else [],
                              sig="pe" if kc == KC - 1 else None)
                dst = q_fm if oc < 8 else k_fm
                cc = oc % 8
                m_qk = P.op("act", lambda e, dst=dst, cc=cc, oc=oc, b=qkv_bank, tb=tb:
                            e.activation(dst[:, cc, tb * 512:(tb + 1) * 512],
                                         qkv_banks[b][:, :], AF.Identity,
                                         bias=bqkvT[:, oc:oc + 1], scale=1.0),
                            waits=[mm, m_bqkvT], sig="ac")
                bank_reader[qkv_bank] = m_qk
                r_xnt[tbuf] = mm
                qkv_bank = (qkv_bank + 1) % 4
                if oc % 2 == 1:
                    yield
            for tt in range(4):
                kt = tb * 4 + tt
                mmv = [None, None]
                for half in range(2):
                    for kc in range(KC):
                        mmv[half] = P.op("pe", lambda e, half=half, kc=kc, tbuf=tbuf, tt=tt:
                                   e.matmul(ps_pv[half][:, 0:512],
                                            xnT(tbuf)[:, kc, tt * 128:(tt + 1) * 128],
                                            wq_a[:, kc, 2 * D + half * 512:2 * D + (half + 1) * 512],
                                            start=(kc == 0), stop=(kc == KC - 1)),
                                   waits=[m_v_evac[half], w_xnt[tbuf], m_wv] if kc == 0 else [],
                                   sig="pe" if kc == KC - 1 else None)
                r_xnt[tbuf] = mmv[1]
                for half in range(2):
                    m_v_evac[half] = P.op("dve", lambda e, kt=kt, half=half: e.tensor_tensor(
                                        v_on[:, kt, half * 528:(half + 1) * 528]
                                        .rearrange("p (h n) -> p h n", h=H // 2)[:, :, 0:64],
                                        ps_pv[half].rearrange("p (h n) -> p h n", h=H // 2),
                                        bv_bc[:, half * 512:(half + 1) * 512]
                                        .rearrange("p (h n) -> p h n", h=H // 2), ALU.add),
                                    waits=[mmv[half], m_bias["v"], m_vones], sig="dv")
                last_A_pe = mmv[1]
                yield

        a_gen = [None]

        def pump_a():
            if a_gen[0] is not None:
                try:
                    next(a_gen[0])
                except StopIteration:
                    a_gen[0] = None

        def drain_a():
            while a_gen[0] is not None:
                pump_a()

        for t in range(T):
            tb, ti = t // 4, t % 4
            i3, i2 = t % 3, t % 2
            tbuf = tb % 2
            m_x = sdma(x_in(i3), x_t[t], waits=[r_xin[i3]], sig=f"dX{i3}")
            m_xn = ln_chain(x_in(i3), i2, m_x, t % 2)
            r_xin[i3] = m_xn
            pump_a()
            m_tr = transpose8(xn_b[:, i2, :], [m_xn, r_psbf])
            r_xn[i2] = m_tr
            m_ev = P.op("dve", lambda e, tbuf=tbuf, ti=ti: e.tensor_copy(
                            xnT(tbuf)[:, :, ti * 128:(ti + 1) * 128], psbf3),
                        waits=[m_tr, r_xnt[tbuf]], sig="dv")
            r_psbf = m_ev
            pump_a()
            if ti == 3:
                w_xnt[tbuf] = m_ev
                drain_a()
                a_gen[0] = mats_gen(tb)
        drain_a()
        m_qkv_done = bank_reader[(qkv_bank - 1) % 4]
        m_v_done = m_v_evac[1]

        m_wproj = gdma(wproj_bf, wproj_d.rearrange("(c p) n -> p c n", p=128),
                       waits=[last_A_pe], sig="dWp")
        m_bias["p"] = sdma(bpj_bc,
                           bproj_d.rearrange("(a b) -> a b", a=1).broadcast_to((128, D)),
                           waits=[m_v_done], sig="dBp")
        wffn_r = wffn_d.rearrange("(r p) n -> p r n", p=128)
        arena_guard = [r_xin[0], r_xin[1], r_xin[2]]
        m_wfc = []
        for c4 in range(4):
            g4 = arena_guard[c4 * 2:c4 * 2 + 2]
            m_wfc.append(gdma(wstage[:, c4 * 2:c4 * 2 + 2, :],
                              wffn_r[:, c4 * 2:c4 * 2 + 2, :],
                              waits=g4, sig=f"dWf{c4}"))
        m_wt_done = None
        for r in range(KC):
            m_tr = None
            for dc in range(KC):
                m_tr = P.op("pe", lambda e, r=r, dc=dc: e.transpose(
                                ps_bf[:, dc * 128:(dc + 1) * 128],
                                wstage[:, r, dc * 128:(dc + 1) * 128], id_b[:]),
                            waits=[m_wfc[r // 2], r_psbf] if dc == 0 else [],
                            sig="pe" if dc == KC - 1 else None)
            r_psbf = P.op("dve", lambda e, r=r: e.tensor_copy(
                              wt_bf[:, :, r * 128:(r + 1) * 128],
                              ps_bf[:, :].rearrange("p (c n) -> p c n", c=KC)),
                          waits=[m_tr, m_wproj if r == 0 else None], sig="dv")
        m_wt_done = m_tr
        m_wt_evac = r_psbf
        m_trc = None
        for c in range(KC):
            m_trc = P.op("pe", lambda e, c=c: e.transpose(
                             ps_bf[0:32, c * 128:(c + 1) * 128], c_st[:, c, :], id_b[:]),
                         waits=[m_cst, r_psbf] if c == 0 else [],
                         sig="pe" if c == KC - 1 else None)
        m_ct = P.op("dve", lambda e: e.tensor_copy(
                        ct_bf[0:32, :], ps_bf[0:32, :].rearrange("p (c n) -> p c n", c=KC)),
                    waits=[m_trc], sig="dv")
        r_psbf = m_ct

        pair_reader = [m_qkv_done, m_qkv_done]
        ch_reader = [last_A_pe] * 5
        gki = 0
        m_pv_read = [m_v_evac[0], m_v_evac[1]]
        w_ao = [None, None]
        r_ao = [None, None]
        r_psw = m_bqkvT
        r_aoT = [last_A_pe, last_A_pe]
        r_hT = [last_A_pe, last_A_pe]
        w_x2 = [None, None]
        r_x2 = [m_wt_done, m_wt_done]
        r_xrr = [m_wt_done, m_wt_done, m_wt_done]
        m_xr_all = [None] * T
        next_pref = 0
        r_pvsb = last_A_pe
        r_recips = None
        w_ao_g0 = [None]

        def emit_B_pair(g, hc, defer=None):
            nonlocal gki, r_pvsb, r_recips
            par = g % 2
            he, ho = 2 * hc, 2 * hc + 1
            K = 4 * g + 4
            pend = []
            m_pv_last = [None, None]
            if defer is not None:
                defer[0]()
            def pop_pv(drain=False):
                nonlocal m_pv_last
                while len(pend) > 6 or (drain and pend):
                    cp2, off2, ncol2, gate2, ki2, hd2 = pend.pop(0)
                    m_pv_last[hd2] = P.op("pe", lambda e, cp=cp2, off=off2, ncol=ncol2,
                                          hd=hd2, hh=2 * hc + hd2, ki2=ki2,
                                          st=(ki2 == 0), sp=(ki2 == K - 1):
                                          e.matmul(ps_pv[hd][0:66, off * 128:512],
                                                   v_on[:, ki2, 66 * hh:66 * hh + 66],
                                                   chunkpair(cp)[:, hd * 512:hd * 512 + ncol],
                                                   start=st, stop=sp),
                                          waits=[gate2, (m_pv_read[hd2] if ki2 == 0 else None),
                                                 (m_v_done if ki2 == 0 else None)],
                                          sig="pe")
                    ch_reader[cp2] = m_pv_last[hd2]

            for ki in range(K):
                off = max(0, ki - 4 * g)
                ncol = 512 - off * 128
                pp = gki % 2
                cp = gki % 5
                m_sc = None
                for hd, hp in ((0, 0), (1, 64)):
                    m_sc = P.op("pe", lambda e, pp=pp, hd=hd, hp=hp, off=off, ki=ki, hc=hc, g=g:
                                e.matmul(ps_sc[:, pp * 1024 + hd * 512 + off * 128:
                                              pp * 1024 + hd * 512 + 512],
                                         k_fm[hp:hp + 64, hc, ki * 128:(ki + 1) * 128],
                                         q_fm[hp:hp + 64, hc, g * 512 + off * 128:(g + 1) * 512],
                                         start=True, stop=True,
                                         tile_position=(hp, 0), skip_group_check=True),
                                waits=[pair_reader[pp], m_qkv_done] if hd == 0 else [],
                                sig="pe" if hd == 1 else None)
                m_ex = P.op("act", lambda e, pp=pp, cp=cp, off=off, ncol=ncol:
                            e.activation(chunkpair(cp).rearrange("p (a n) -> p a n", a=2)
                                         [:, :, 0:ncol],
                                         ps_sc[:, pp * 1024:(pp + 1) * 1024]
                                         .rearrange("p (a n) -> p a n", a=2)
                                         [:, :, off * 128:512], AF.Exp,
                                         bias=0.0, scale=0.125),
                            waits=[m_sc, ch_reader[cp]], sig="ac")
                pair_reader[pp] = m_ex
                for hd in range(2):
                    gate = m_ex
                    if ki >= 4 * g:
                        gate = P.op("pool", lambda e, cp=cp, hd=hd: e.tensor_tensor(
                                        chunkpair(cp)[:, hd * 512:hd * 512 + 128],
                                        chunkpair(cp)[:, hd * 512:hd * 512 + 128],
                                        tri01[:], ALU.mult),
                                    waits=[m_ex, m_cm], sig="pl")
                    pend.append((cp, off, ncol, gate, ki, hd))
                gki += 1
                pop_pv()
                pump()
            pop_pv(drain=True)
            if defer is not None:
                defer[1]()
            cell = []

            def norm_a(m_pv_last=tuple(m_pv_last)):
                nonlocal r_pvsb
                for hd in range(2):
                    m_evac = P.op("dve", lambda e, hd=hd: e.tensor_copy(
                                      pv_sbB[0:66, hd * 512:(hd + 1) * 512],
                                      ps_pv[hd][0:66, 0:512]),
                                  waits=[m_pv_last[hd], r_pvsb], sig="dv")
                    m_pv_read[hd] = m_evac
                    cell.append(m_evac)

            def normalize(par=par, hc=hc, g=g):
                nonlocal r_pvsb, r_recips, r_psbf
                m_pvT = None
                for hd in range(2):
                    for qt in range(4):
                        m_pvT = P.op("pe", lambda e, qt=qt, hd=hd: e.transpose(
                                         ps_bf[:, hd * 288 + qt * 72:hd * 288 + qt * 72 + 66],
                                         pv_sbB[0:66, hd * 512 + qt * 128:hd * 512 + (qt + 1) * 128],
                                         id_b[0:66, 0:66]),
                                     waits=[cell[hd], r_psbf] if qt == 0 else [],
                                     sig="pe" if (hd == 1 and qt == 3) else None)
                r_pvsb = m_pvT
                m_last = None
                for hd in range(2):
                    hh = 2 * hc + hd
                    m_rc = P.op("dve", lambda e, hd=hd: e.reciprocal(
                                    recips.rearrange("p (a b) -> p a b", b=1),
                                    ps_bf[:, hd * 288:hd * 288 + 288]
                                    .rearrange("p (a b) -> p a b", b=72)[:, :, 64:65]),
                                waits=[m_pvT, r_recips], sig="dv")
                    for qt in range(4):
                        dst = (ao(par, qt) if g > 0 else ao_g0(qt))
                        m_last = P.op("dve", lambda e, qt=qt, hh=hh, hd=hd, dst=dst:
                                      e.tensor_scalar(
                                          dst[:, hh * 64:(hh + 1) * 64],
                                          ps_bf[:, hd * 288 + qt * 72:hd * 288 + qt * 72 + 64],
                                          recips[:, qt:qt + 1], None, ALU.mult),
                                      waits=[m_rc, ((r_ao[par] if g > 0 else r_aoT[qt % 2])
                                                    if hc == 0 and hd == 0 and qt == 0 else None),
                                             (last_A_pe if g <= 2 and hc == 0 and hd == 0 and qt == 0 else None)],
                                      sig="dv")
                    r_recips = m_last
                r_psbf = m_last
                if hc == H // 2 - 1:
                    if g > 0:
                        w_ao[par] = m_last
                    else:
                        w_ao_g0[0] = m_last
            return (norm_a, normalize)

        cstate = {}

        def gen_C1a1(cg, qt):
            nonlocal next_pref, r_psbf
            par = cg % 2
            qi = 4 * cg + qt
            i3, i2 = qi % 3, qi % 2
            while next_pref <= min(qi + 2, T - 1):
                m_xr_all[next_pref] = sdma(x_rr(next_pref % 3), x_t[next_pref],
                                           waits=[r_xrr[next_pref % 3]], sig=f"dX{next_pref % 3}")
                next_pref += 1
            m_xpb = P.op("pool", lambda e, i3=i3: e.tensor_tensor(
                             x_rr(i3), x_rr(i3), bpj_bc[:], ALU.add),
                         waits=[m_xr_all[qi], m_bias["p"]], sig="pl")
            yield
            yield
            yield
            src_ao = ao(par, qt) if cg > 0 else ao_g0(qt)
            m_tr = transpose8(src_ao, [w_ao[par] if cg > 0 else w_ao_g0[0], r_psbf])
            yield
            if cg > 0:
                r_ao[par] = m_tr
            m_aoT = P.op("dve", lambda e, src_ao=src_ao: e.tensor_copy(
                             src_ao.rearrange("p (c n) -> p c n", c=KC), psbf3),
                         waits=[m_tr, r_aoT[i2]], sig="dv")
            r_psbf = m_aoT
            cstate[(cg, qt)] = [i2, qi, i3, src_ao, m_xpb, m_aoT, None, None]

        def gen_C1a2(cg, qt, use_sc=False):
            nonlocal next_pref, r_psw
            st = cstate[(cg, qt)]
            i2, qi, i3, src_ao, m_xpb, m_aoT = st[0], st[1], st[2], st[3], st[4], st[5]
            aoTv = src_ao.rearrange("p (c n) -> p c n", c=KC)
            mmp = None
            m_x2 = None
            for half in range(2):
                pdst = ps_sc[:, half * 512:(half + 1) * 512] if use_sc else ps_w[:, 0:512]
                if use_sc:
                    w0 = [m_aoT, m_wproj,
                          r_scp[half] if r_scp[half] is not None else pair_reader[0]]
                else:
                    w0 = [m_aoT, m_wproj, r_psw] if half == 0 else [m_x2]
                for kc in range(KC):
                    mmp = P.op("pe", lambda e, kc=kc, aoTv=aoTv, pdst=pdst, half=half:
                               e.matmul(pdst,
                                        aoTv[:, kc, :],
                                        wproj_bf[:, kc, half * 512:(half + 1) * 512],
                                        start=(kc == 0), stop=(kc == KC - 1)),
                               waits=w0 if kc == 0 else [],
                               sig="pe")
                    if kc == 3:
                        yield
                m_x2 = P.op("dve", lambda e, i2=i2, i3=i3, half=half, pdst=pdst, mmp=mmp:
                            e.scalar_tensor_tensor(
                                x2b(i2)[:, half * 512:(half + 1) * 512], pdst, 1.0,
                                x_rr(i3)[:, half * 512:(half + 1) * 512], ALU.mult, ALU.add),
                            waits=[mmp, m_xpb, r_x2[i2] if (half == 0 or use_sc) else None],
                            sig="dv")
                if use_sc:
                    r_scp[half] = m_x2
                yield
            r_aoT[i2] = mmp
            if not use_sc:
                r_psw = m_x2
            r_xrr[i3] = m_x2
            if next_pref <= min(qi + 3, T - 1):
                m_xr_all[next_pref] = sdma(x_rr(next_pref % 3), x_t[next_pref],
                                           waits=[r_xrr[next_pref % 3]], sig=f"dX{next_pref % 3}")
                next_pref += 1
            w_x2[i2] = m_x2
            yield
            st[6] = ln_stats(x2b(i2), m_x2, qi % 2)

        def gen_C1b(cg, qt):
            st = cstate[(cg, qt)]
            st[7] = ln_fin(x2b(st[0]), st[0], st[6], st[1] % 2)
            yield

        def gen_C2(cg, qt, use_pv=False):
            nonlocal r_psw, r_psbf
            st = cstate.pop((cg, qt))
            i2, qi, m_h = st[0], st[1], st[7]
            m_x2f = P.op("pool", lambda e, i2=i2: e.tensor_tensor(
                             x2b(i2), x2b(i2), bfn_bc[:], ALU.add),
                         waits=[m_h, m_bias["f"], r_x2[i2]], sig="pl")
            yield
            yield
            m_tr = transpose8(xn_b[:, i2, :], [m_h, r_psbf])
            yield
            m_hT = P.op("dve", lambda e, i2=i2: e.tensor_copy(
                            xn_b.rearrange("p a (c n) -> p a c n", c=KC)[:, i2],
                            psbf3),
                        waits=[m_tr, r_hT[i2]], sig="dv")
            r_xn[i2] = m_hT
            r_psbf = m_hT
            yield
            hTv = xn_b.rearrange("p a (c n) -> p a c n", c=KC)
            mmu = None
            for kc in range(KC):
                mmu = P.op("pe", lambda e, kc=kc, i2=i2:
                           e.matmul(ps_w[0:32, 0:128], vr_bf[:, kc, :], hTv[:, i2, kc, :],
                                    start=(kc == 0), stop=(kc == KC - 1)),
                           waits=[m_hT, m_vr, r_psw] if kc == 0 else [],
                           sig="pe" if kc == KC - 1 else None)
            m_u = P.op("dve", lambda e: e.tensor_copy(u_bf, ps_w[0:32, 0:128]),
                       waits=[mmu], sig="dv")
            r_psw = m_u
            yield
            def fdst(half):
                return ps_pv[half][:, 0:512] if use_pv else ps_w[:, 0:512]
            mmf = None
            m_y = None
            for half in range(2):
                w0 = [m_hT, m_wt_evac]
                if not use_pv:
                    w0.append(r_psw if half == 0 else m_y)
                if use_pv:
                    w0.append(r_pvffn[half] if r_pvffn[half] is not None
                              else m_pv_read[half])
                for kc in range(KC):
                    P.op("pe", lambda e, half=half, kc=kc, i2=i2:
                         e.matmul(fdst(half),
                                  hTv[:, i2, kc, :],
                                  wt_bf[:, kc, half * 512:(half + 1) * 512],
                                  start=(kc == 0), stop=False),
                         waits=w0 if kc == 0 else [],
                         sig=None)
                    if kc == 3:
                        yield
                mmf = P.op("pe", lambda e, half=half:
                           e.matmul(fdst(half), u_bf,
                                    ct_bf[0:32, half * 512:(half + 1) * 512],
                                    start=False, stop=True),
                           waits=[m_u, m_ct] if half == 0 else [],
                           sig="pe")
                m_y = P.op("dve", lambda e, i2=i2, half=half, mmf=mmf: e.scalar_tensor_tensor(
                               x2b(i2)[:, half * 512:(half + 1) * 512],
                               fdst(half), 1.0,
                               x2b(i2)[:, half * 512:(half + 1) * 512],
                               ALU.mult, ALU.add),
                           waits=[mmf, m_x2f] if half == 0 else [mmf], sig="dv")
                if use_pv:
                    r_pvffn[half] = m_y
                yield
            r_hT[i2] = mmf
            if not use_pv:
                r_psw = m_y
            m_dy = P.op("sync", lambda e, i2=i2, qi=qi: e.dma_start(
                            out=y_t[qi], in_=x2b(i2)),
                        waits=[m_y], sig=f"dY{i2}", inc=DMA16)
            r_x2[i2] = m_dy

        from collections import deque
        fillers = deque()

        def pump():
            while fillers:
                try:
                    next(fillers[0])
                    return
                except StopIteration:
                    fillers.popleft()

        def drain_fillers():
            while fillers:
                try:
                    next(fillers[0])
                except StopIteration:
                    fillers.popleft()

        slotmap = {
            0: [("c", 2, 2), ("a2", 3, 2)],
            1: [("b", 3, 2), ("a1", 0, 1)],
            2: [("c", 3, 2), ("a2", 0, 1)],
            3: [("b", 0, 1), ("a1", 1, 1)],
            4: [("c", 0, 1), ("a2", 1, 1)],
            5: [("b", 1, 1), ("a1", 2, 1)],
            6: [("c", 1, 1), ("a2", 2, 1)],
            7: [("b", 2, 1), ("a1", 3, 1)],
        }
        gens = {"a1": gen_C1a1, "a2": gen_C1a2, "b": gen_C1b, "c": gen_C2}
        r_pvffn = [None, None]
        r_scp = [None, None]
        norm = None
        for g in range(G):
            for p in range(H // 2):
                norm = emit_B_pair(g, p, defer=norm)
                for kind, qt, goff in slotmap.get(p, []):
                    if g - goff >= 0:
                        fillers.append(gens[kind](g - goff, qt))
        drain_fillers()
        if norm is not None:
            norm[0]()
            norm[1]()
        if G >= 2:
            for job in (gens["c"](G - 2, 2, use_pv=True), gens["a2"](G - 2, 3, use_sc=True),
                        gens["b"](G - 2, 3), gens["c"](G - 2, 3, use_pv=True)):
                fillers.append(job)
                drain_fillers()
        tail = [("a1", 0), ("a2", 0), ("a1", 1), ("a2", 1),
                ("b", 0), ("c", 0), ("b", 1), ("c", 1),
                ("a1", 2), ("a2", 2), ("a1", 3), ("a2", 3),
                ("b", 2), ("c", 2), ("b", 3), ("c", 3)]
        for kind, qt in tail:
            if kind == "c":
                job = gens["c"](G - 1, qt, use_pv=True)
            elif kind == "a2":
                job = gens["a2"](G - 1, qt, use_sc=True)
            else:
                job = gens[kind](G - 1, qt)
            fillers.append(job)
            drain_fillers()
        P.op("sync", lambda e: e.nop(),
             waits=[("dY0", P.cnt.get("dY0", 0)), ("dY1", P.cnt.get("dY1", 0))])

        P.emit(block, sems)
    return nc


_NC_CACHE = {}


def _get_nc(L):
    if L not in _NC_CACHE:
        _NC_CACHE[L] = build_nc(L)
    return _NC_CACHE[L]


def _consts():
    i, j = np.meshgrid(np.arange(128), np.arange(128), indexing="ij")
    tri01 = np.where(i <= j, np.float32(1.0), np.float32(0.0)).astype(bfdt)
    return {
        "identb": np.eye(128, dtype=np.float32).astype(bfdt),
        "identf": np.eye(66, dtype=np.float32),
        "tri01": tri01,
    }


def make_in_maps(x, Wqkv, bqkv, Wproj, bproj, W, b, V_r, C):
    shared = {
        "wqkv": np.ascontiguousarray(np.asarray(Wqkv, dtype=np.float32)),
        "bqkv": np.ascontiguousarray(np.asarray(bqkv, dtype=np.float32)),
        "wproj": np.ascontiguousarray(np.asarray(Wproj, dtype=np.float32)),
        "bproj": np.ascontiguousarray(np.asarray(bproj, dtype=np.float32)),
        "wffn": np.ascontiguousarray(np.asarray(W, dtype=np.float32)),
        "bffn": np.ascontiguousarray(np.asarray(b, dtype=np.float32)),
        "vr": np.ascontiguousarray(np.asarray(V_r, dtype=np.float32)),
        "cmat": np.ascontiguousarray(np.asarray(C, dtype=np.float32)),
        **_consts(),
    }
    return [dict(shared, x=np.ascontiguousarray(x[bi])) for bi in range(x.shape[0])]


def kernel(L0_attn_x, L0_attn_Wqkv, L0_attn_bqkv, L0_attn_Wproj, L0_attn_bproj,
           L0_ffn_W, L0_ffn_b, L0_ffn_V_r, L0_ffn_C):
    from concourse.bass_utils import run_bass_kernel_spmd
    x = np.ascontiguousarray(np.asarray(L0_attn_x, dtype=np.float32))
    B, L, _ = x.shape
    nc = _get_nc(L)
    in_maps = make_in_maps(x, L0_attn_Wqkv, L0_attn_bqkv, L0_attn_Wproj, L0_attn_bproj,
                           L0_ffn_W, L0_ffn_b, L0_ffn_V_r, L0_ffn_C)
    res = run_bass_kernel_spmd(nc, in_maps, core_ids=list(range(B)))
    return np.stack([np.asarray(res.results[bi]["y"]) for bi in range(B)], axis=0)


# version 43
# speedup vs baseline: 1.0138x; 1.0003x over previous
import numpy as np
import ml_dtypes
from contextlib import ExitStack

import concourse.bass as bass
import concourse.mybir as mybir

F32 = mybir.dt.float32
BF16 = mybir.dt.bfloat16
AF = mybir.ActivationFunctionType
ALU = mybir.AluOpType
bfdt = ml_dtypes.bfloat16

D = 1024
KC = 8
H = 16
NEG = -1.0e30
DMA16 = 16


class Plan:
    ENGS = ("sync", "pool", "pe", "dve", "act")

    def __init__(self):
        self.ops = {e: [] for e in self.ENGS}
        self.cnt = {}

    def op(self, eng, fn, waits=(), sig=None, inc=1):
        ws = [w for w in waits if w is not None]
        self.ops[eng].append([fn, ws, sig, inc])
        if sig is not None:
            self.cnt[sig] = self.cnt.get(sig, 0) + inc
            return (sig, self.cnt[sig])
        return None

    def emit(self, block, sems):
        def make(ops):
            def run(e):
                seen = {}
                for fn, waits, sig, inc in ops:
                    for s, v in waits:
                        if seen.get(s, 0) < v:
                            e.wait_ge(sems[s], v)
                            seen[s] = v
                    instr = fn(e)
                    if sig is not None:
                        instr.then_inc(sems[sig], inc)
            return run
        block.sync(make(self.ops["sync"]))
        block.gpsimd(make(self.ops["pool"]))
        block.tensor(make(self.ops["pe"]))
        block.vector(make(self.ops["dve"]))
        block.scalar(make(self.ops["act"]))


def build_nc(L):
    T = L // 128
    G = L // 512
    assert L % 512 == 0

    nc = bass.Bass("TRN2", target_bir_lowering=False)

    x_d = nc.dram_tensor("x", [L, D], F32, kind="ExternalInput")
    wqkv_d = nc.dram_tensor("wqkv", [D, 3 * D], F32, kind="ExternalInput")
    bqkv_d = nc.dram_tensor("bqkv", [3 * D], F32, kind="ExternalInput")
    wproj_d = nc.dram_tensor("wproj", [D, D], F32, kind="ExternalInput")
    bproj_d = nc.dram_tensor("bproj", [D], F32, kind="ExternalInput")
    wffn_d = nc.dram_tensor("wffn", [D, D], F32, kind="ExternalInput")
    bffn_d = nc.dram_tensor("bffn", [D], F32, kind="ExternalInput")
    vr_d = nc.dram_tensor("vr", [D, 32], F32, kind="ExternalInput")
    cmat_d = nc.dram_tensor("cmat", [D, 32], F32, kind="ExternalInput")
    identb_d = nc.dram_tensor("identb", [128, 128], BF16, kind="ExternalInput")
    identf_d = nc.dram_tensor("identf", [66, 66], F32, kind="ExternalInput")
    tri01_d = nc.dram_tensor("tri01", [128, 128], BF16, kind="ExternalInput")
    y_d = nc.dram_tensor("y", [L, D], F32, kind="ExternalOutput")

    x_t = x_d.rearrange("(a p) n -> a p n", p=128)
    y_t = y_d.rearrange("(a p) n -> a p n", p=128)

    with ExitStack() as ctx:
        E = ctx.enter_context
        q_fm = E(nc.sbuf_tensor("q_fm", [128, KC, L], BF16))
        k_fm = E(nc.sbuf_tensor("k_fm", [128, KC, L], BF16))
        v_on = E(nc.sbuf_tensor("v_on", [128, T, 1118], BF16))
        wq_a = E(nc.sbuf_tensor("wq_a", [128, KC, 3 * D], BF16))
        xnt_a = E(nc.sbuf_tensor("xnt_a", [128, 8192], BF16))
        xf_a = E(nc.sbuf_tensor("xf_a", [128, 6656], F32))
        xn_b = E(nc.sbuf_tensor("xn_b", [128, 2, D], BF16))
        bias_a = E(nc.sbuf_tensor("bias_a", [128, 2 * D], F32))
        bv_bc = bias_a[:, 0:D]
        bpj_bc = bias_a[:, 0:D]
        bfn_bc = bias_a[:, D:2 * D]
        tri01 = E(nc.sbuf_tensor("tri01_sb", [128, 128], BF16))
        id_b = E(nc.sbuf_tensor("id_b", [128, 128], BF16))
        id_f = E(nc.sbuf_tensor("id_f", [66, 66], F32))
        bqkvT = E(nc.sbuf_tensor("bqkvT", [128, 24], F32))
        bqkv_sb = E(nc.sbuf_tensor("bqkv_sb", [24, 128], F32))
        vr_bf = E(nc.sbuf_tensor("vr_bf", [128, KC, 32], BF16))
        c_st = E(nc.sbuf_tensor("c_st", [128, KC, 32], BF16))
        ct_bf = E(nc.sbuf_tensor("ct_bf", [32, D], BF16))
        stats = E(nc.sbuf_tensor("stats", [128, 2, 12], F32))
        mv = E(nc.sbuf_tensor("mv", [128, 2, 2], F32))
        ln_t = E(nc.sbuf_tensor("ln_t", [128, 2], F32))
        rstd = E(nc.sbuf_tensor("rstd", [128, 2], F32))
        negmr = E(nc.sbuf_tensor("negmr", [128, 2], F32))
        recips = E(nc.sbuf_tensor("recips", [128, 4], F32))
        eps_t = E(nc.sbuf_tensor("eps_t", [128, 1], F32))
        ps_sc = E(nc.psum_tensor("ps_sc", [128, 2048], F32))
        ps_w = E(nc.psum_tensor("ps_w", [128, 512], F32))
        ps_bf = E(nc.psum_tensor("ps_bf", [128, 1024], BF16))
        ps_pv = [E(nc.psum_tensor(f"ps_pv{i}", [128, 512], F32)) for i in range(2)]
        semnames = ["dI", "dX0", "dX1", "dX2", "dY0", "dY1", "dWq", "dWk", "dWv", "dWs", "dWp", "dWf0", "dWf1", "dWf2", "dWf3", "dBp", "pe", "dv", "ac", "pl"]
        sems = {n: E(nc.semaphore(n)) for n in semnames}
        block = E(nc.Block())

        P = Plan()

        def xnT(buf):
            return xnt_a[:, buf * 4096:(buf + 1) * 4096].rearrange("p (c n) -> p c n", c=KC)
        def aoT(buf):
            return xnt_a[:, buf * 1024:(buf + 1) * 1024].rearrange("p (c n) -> p c n", c=KC)
        def hT(buf):
            return xnt_a[:, 2048 + buf * 1024:2048 + (buf + 1) * 1024].rearrange("p (c n) -> p c n", c=KC)
        def chunkpair(cp):
            if cp >= 2:
                o = 5120 + (cp - 2) * 512
                return xf_a[:, o:o + 512].bitcast(BF16)
            return xnt_a[:, 4096 + cp * 1024:4096 + (cp + 1) * 1024]
        u_bf = xnt_a[0:32, 7168:7296]
        def ao_g0(qt):
            return xnt_a[:, qt * 1024:(qt + 1) * 1024]
        pv_sbB = xnt_a[:, 6144:7168]
        def x_in(i):
            return xf_a[:, i * D:(i + 1) * D]
        x_rr = x_in
        def x2b(i):
            return xf_a[:, 3072 + i * D:3072 + (i + 1) * D]

        wstage = xf_a[:, 0:4096].bitcast(BF16).rearrange("p (c n) -> p c n", c=KC)
        wproj_bf = wq_a[:, :, 0:D]
        wt_bf = wq_a[:, :, D:2 * D]
        def ao(par, qt):
            return wq_a[:, 4 * par + qt, 2 * D:3 * D]
        psbf3 = ps_bf.rearrange("p (c n) -> p c n", c=KC)

        def sdma(dst, src, waits=(), sig="dI"):
            return P.op("sync", lambda e, d=dst, s=src: e.dma_start(out=d, in_=s),
                        waits=waits, sig=sig, inc=DMA16)
        m_idb = sdma(id_b[:], identb_d.ap())
        m_idf = sdma(id_f[:], identf_d.ap())
        m_cm = sdma(tri01[:], tri01_d.ap())
        m_bq = sdma(bqkv_sb[:], bqkv_d.rearrange("(a b) -> a b", b=128))
        m_init = ("dI", P.cnt["dI"])
        m_idb = m_idf = m_cm = m_bq = m_init

        def gdma(dst, src, waits=(), sig="dWs"):
            return P.op("pool", lambda e, d=dst, s=src: e.dma_start(out=d, in_=s),
                        waits=waits, sig=sig, inc=DMA16)
        wqkv_r = wqkv_d.rearrange("(c p) n -> p c n", p=128)
        m_wq = gdma(wq_a[:, :, 0:D], wqkv_r[:, :, 0:D], sig="dWq")
        m_wk = gdma(wq_a[:, :, D:2 * D], wqkv_r[:, :, D:2 * D], sig="dWk")
        m_wv = gdma(wq_a[:, :, 2 * D:3 * D], wqkv_r[:, :, 2 * D:3 * D], sig="dWv")
        gdma(vr_bf[:, :, :], vr_d.rearrange("(c p) n -> p c n", p=128))
        gdma(c_st[:, :, :], cmat_d.rearrange("(c p) n -> p c n", p=128))
        m_vr = m_cst = ("dWs", 64)
        gdma(bv_bc, bqkv_d.rearrange("(a b) -> a b", a=3)[2:3, :].broadcast_to((128, D)),
             sig="dWs")
        gdma(bfn_bc, bffn_d.rearrange("(a b) -> a b", a=1).broadcast_to((128, D)),
             sig="dWs")
        m_bv_bc = m_bfn_bc = ("dWs", 64)

        m_eps = P.op("dve", lambda e: e.memset(eps_t[:], 1e-5), sig="dv")
        m_vones = P.op("dve", lambda e: e.memset(
                           v_on[:, :, 0:1056].rearrange("p t (h n) -> p t h n", h=H)[:, :, :, 64:66],
                           1.0),
                       waits=[m_eps], sig="dv")
        m_vones = P.op("dve", lambda e: e.memset(v_on[:, :, 1056:1118], 0.0),
                       waits=[m_vones], sig="dv")
        m_tbq = P.op("pe", lambda e: e.transpose(ps_w[0:128, 0:24], bqkv_sb[0:24, 0:128],
                                                 id_f[0:24, 0:24]),
                     waits=[m_bq, m_idf], sig="pe")
        m_bqkvT = P.op("dve", lambda e: e.tensor_copy(bqkvT[:], ps_w[0:128, 0:24]),
                       waits=[m_tbq, m_vones], sig="dv")
        m_bias = {"v": m_bv_bc, "f": m_bfn_bc}
        m_psw_free = None

        r_xin = [None, None, None]
        w_xn = [None, None]
        r_xn = [None, None]
        w_xnt = [None, None]
        r_xnt = [None, None]
        r_psbf = None
        m_ln_last = [None, None]
        r_rstd = [None, None]
        bank_reader = [None, None, None, None]
        qkv_bank = 0
        m_v_evac = [None, None]
        last_A_pe = None

        def ln_stats(src, extra_wait, lp):
            m1 = P.op("dve", lambda e, lp=lp: e.bn_stats(stats[:, lp, 0:6], src[:, 0:512]),
                      waits=[extra_wait, m_ln_last[lp]], sig="dv")
            m2 = P.op("dve", lambda e, lp=lp: e.bn_stats(stats[:, lp, 6:12], src[:, 512:1024]),
                      waits=[m1], sig="dv")
            m3 = P.op("dve", lambda e, lp=lp: e.bn_aggr(mv[:, lp, :], stats[:, lp, :]),
                      waits=[m2], sig="dv")
            return m3

        def ln_fin(src, i2, m3, lp):
            m4 = P.op("act", lambda e, lp=lp: e.activation(ln_t[:, lp:lp + 1], mv[:, lp, 1:2],
                                                           AF.Ln, bias=eps_t[:], scale=1.0),
                      waits=[m3, r_rstd[lp], m_eps], sig="ac")
            m5 = P.op("act", lambda e, lp=lp: e.activation(rstd[:, lp:lp + 1], ln_t[:, lp:lp + 1],
                                                           AF.Exp, bias=0.0, scale=-0.5),
                      waits=[m4], sig="ac")
            m6 = P.op("dve", lambda e, lp=lp: e.tensor_scalar(negmr[:, lp:lp + 1], mv[:, lp, 0:1],
                                                              rstd[:, lp:lp + 1], -1.0,
                                                              ALU.mult, ALU.mult),
                      waits=[m5, m3], sig="dv")
            m7 = P.op("dve", lambda e, i2=i2, lp=lp: e.tensor_scalar(
                          xn_b[:, i2, :], src, rstd[:, lp:lp + 1], negmr[:, lp:lp + 1],
                          ALU.mult, ALU.add),
                      waits=[m6, r_xn[i2]], sig="dv")
            w_xn[i2] = m7
            m_ln_last[lp] = m7
            r_rstd[lp] = m7
            return m7

        def ln_chain(src, i2, extra_wait, lp):
            return ln_fin(src, i2, ln_stats(src, extra_wait, lp), lp)

        def transpose8(src1024_bf, waits_first):
            m_tr = None
            for c in range(KC):
                m_tr = P.op("pe", lambda e, c=c: e.transpose(
                                ps_bf[:, c * 128:(c + 1) * 128],
                                src1024_bf[:, c * 128:(c + 1) * 128], id_b[:]),
                            waits=(waits_first + [m_idb]) if c == 0 else [],
                            sig="pe" if c == KC - 1 else None)
            return m_tr

        qkv_banks = [ps_sc[:, i * 512:(i + 1) * 512] for i in range(4)]

        def mats_gen(tb):
            nonlocal qkv_bank, m_v_evac, last_A_pe
            tbuf = tb % 2
            for oc in range(16):
                mm = None
                for kc in range(KC):
                    mm = P.op("pe", lambda e, oc=oc, kc=kc, b=qkv_bank, tbuf=tbuf:
                              e.matmul(qkv_banks[b][:, :],
                                       wq_a[:, kc, oc * 128:(oc + 1) * 128],
                                       xnT(tbuf)[:, kc, :],
                                       start=(kc == 0), stop=(kc == KC - 1)),
                              waits=[(m_wq if oc < 8 else m_wk), w_xnt[tbuf],
                                     bank_reader[qkv_bank]] if kc == 0 else [],
                              sig="pe" if kc == KC - 1 else None)
                dst = q_fm if oc < 8 else k_fm
                cc = oc % 8
                m_qk = P.op("act", lambda e, dst=dst, cc=cc, oc=oc, b=qkv_bank, tb=tb:
                            e.activation(dst[:, cc, tb * 512:(tb + 1) * 512],
                                         qkv_banks[b][:, :], AF.Identity,
                                         bias=bqkvT[:, oc:oc + 1], scale=1.0),
                            waits=[mm, m_bqkvT], sig="ac")
                bank_reader[qkv_bank] = m_qk
                r_xnt[tbuf] = mm
                qkv_bank = (qkv_bank + 1) % 4
                if oc % 2 == 1:
                    yield
            for tt in range(4):
                kt = tb * 4 + tt
                mmv = [None, None]
                for half in range(2):
                    for kc in range(KC):
                        mmv[half] = P.op("pe", lambda e, half=half, kc=kc, tbuf=tbuf, tt=tt:
                                   e.matmul(ps_pv[half][:, 0:512],
                                            xnT(tbuf)[:, kc, tt * 128:(tt + 1) * 128],
                                            wq_a[:, kc, 2 * D + half * 512:2 * D + (half + 1) * 512],
                                            start=(kc == 0), stop=(kc == KC - 1)),
                                   waits=[m_v_evac[half], w_xnt[tbuf], m_wv] if kc == 0 else [],
                                   sig="pe" if kc == KC - 1 else None)
                r_xnt[tbuf] = mmv[1]
                for half in range(2):
                    m_v_evac[half] = P.op("dve", lambda e, kt=kt, half=half: e.tensor_tensor(
                                        v_on[:, kt, half * 528:(half + 1) * 528]
                                        .rearrange("p (h n) -> p h n", h=H // 2)[:, :, 0:64],
                                        ps_pv[half].rearrange("p (h n) -> p h n", h=H // 2),
                                        bv_bc[:, half * 512:(half + 1) * 512]
                                        .rearrange("p (h n) -> p h n", h=H // 2), ALU.add),
                                    waits=[mmv[half], m_bias["v"], m_vones], sig="dv")
                last_A_pe = mmv[1]
                yield

        a_gen = [None]

        def pump_a():
            if a_gen[0] is not None:
                try:
                    next(a_gen[0])
                except StopIteration:
                    a_gen[0] = None

        def drain_a():
            while a_gen[0] is not None:
                pump_a()

        for t in range(T):
            tb, ti = t // 4, t % 4
            i3, i2 = t % 3, t % 2
            tbuf = tb % 2
            m_x = sdma(x_in(i3), x_t[t], waits=[r_xin[i3]], sig=f"dX{i3}")
            m_xn = ln_chain(x_in(i3), i2, m_x, t % 2)
            r_xin[i3] = m_xn
            pump_a()
            m_tr = transpose8(xn_b[:, i2, :], [m_xn, r_psbf])
            r_xn[i2] = m_tr
            m_ev = P.op("dve", lambda e, tbuf=tbuf, ti=ti: e.tensor_copy(
                            xnT(tbuf)[:, :, ti * 128:(ti + 1) * 128], psbf3),
                        waits=[m_tr, r_xnt[tbuf]], sig="dv")
            r_psbf = m_ev
            pump_a()
            if ti == 3:
                w_xnt[tbuf] = m_ev
                drain_a()
                a_gen[0] = mats_gen(tb)
        drain_a()
        m_qkv_done = bank_reader[(qkv_bank - 1) % 4]
        m_v_done = m_v_evac[1]

        m_wproj = gdma(wproj_bf, wproj_d.rearrange("(c p) n -> p c n", p=128),
                       waits=[last_A_pe], sig="dWp")
        m_bias["p"] = sdma(bpj_bc,
                           bproj_d.rearrange("(a b) -> a b", a=1).broadcast_to((128, D)),
                           waits=[m_v_done], sig="dBp")
        wffn_r = wffn_d.rearrange("(r p) n -> p r n", p=128)
        arena_guard = [r_xin[0], r_xin[1], r_xin[2]]
        m_wfc = []
        for c4 in range(4):
            g4 = arena_guard[c4 * 2:c4 * 2 + 2]
            m_wfc.append(gdma(wstage[:, c4 * 2:c4 * 2 + 2, :],
                              wffn_r[:, c4 * 2:c4 * 2 + 2, :],
                              waits=g4, sig=f"dWf{c4}"))
        m_wt_done = None
        for r in range(KC):
            m_tr = None
            for dc in range(KC):
                m_tr = P.op("pe", lambda e, r=r, dc=dc: e.transpose(
                                ps_bf[:, dc * 128:(dc + 1) * 128],
                                wstage[:, r, dc * 128:(dc + 1) * 128], id_b[:]),
                            waits=[m_wfc[r // 2], r_psbf] if dc == 0 else [],
                            sig="pe" if dc == KC - 1 else None)
            r_psbf = P.op("dve", lambda e, r=r: e.tensor_copy(
                              wt_bf[:, :, r * 128:(r + 1) * 128],
                              ps_bf[:, :].rearrange("p (c n) -> p c n", c=KC)),
                          waits=[m_tr, m_wproj if r == 0 else None], sig="dv")
        m_wt_done = m_tr
        m_wt_evac = r_psbf
        m_trc = None
        for c in range(KC):
            m_trc = P.op("pe", lambda e, c=c: e.transpose(
                             ps_bf[0:32, c * 128:(c + 1) * 128], c_st[:, c, :], id_b[:]),
                         waits=[m_cst, r_psbf] if c == 0 else [],
                         sig="pe" if c == KC - 1 else None)
        m_ct = P.op("dve", lambda e: e.tensor_copy(
                        ct_bf[0:32, :], ps_bf[0:32, :].rearrange("p (c n) -> p c n", c=KC)),
                    waits=[m_trc], sig="dv")
        r_psbf = m_ct

        pair_reader = [m_qkv_done, m_qkv_done]
        ch_reader = [last_A_pe] * 5
        gki = 0
        m_pv_read = [m_v_evac[0], m_v_evac[1]]
        w_ao = [None, None]
        r_ao = [None, None]
        r_psw = m_bqkvT
        r_aoT = [last_A_pe, last_A_pe]
        r_hT = [last_A_pe, last_A_pe]
        w_x2 = [None, None]
        r_x2 = [m_wt_done, m_wt_done]
        r_xrr = [m_wt_done, m_wt_done, m_wt_done]
        m_xr_all = [None] * T
        next_pref = 0
        r_pvsb = last_A_pe
        r_recips = None
        w_ao_g0 = [None]

        def emit_B_pair(g, hc, defer=None):
            nonlocal gki, r_pvsb, r_recips
            par = g % 2
            he, ho = 2 * hc, 2 * hc + 1
            K = 4 * g + 4
            pend = []
            m_pv_last = [None, None]
            if defer is not None:
                defer[0]()
            def pop_pv(drain=False):
                nonlocal m_pv_last
                while len(pend) > 6 or (drain and pend):
                    cp2, off2, ncol2, gate2, ki2, hd2 = pend.pop(0)
                    m_pv_last[hd2] = P.op("pe", lambda e, cp=cp2, off=off2, ncol=ncol2,
                                          hd=hd2, hh=2 * hc + hd2, ki2=ki2,
                                          st=(ki2 == 0), sp=(ki2 == K - 1):
                                          e.matmul(ps_pv[hd][0:66, off * 128:512],
                                                   v_on[:, ki2, 66 * hh:66 * hh + 66],
                                                   chunkpair(cp)[:, hd * 512:hd * 512 + ncol],
                                                   start=st, stop=sp),
                                          waits=[gate2, (m_pv_read[hd2] if ki2 == 0 else None),
                                                 (m_v_done if ki2 == 0 else None)],
                                          sig="pe")
                    ch_reader[cp2] = m_pv_last[hd2]

            for ki in range(K):
                off = max(0, ki - 4 * g)
                ncol = 512 - off * 128
                if ki % 2 == 1:
                    pop_pv()
                pp = gki % 2
                cp = gki % 5
                m_sc = None
                for hd, hp in ((0, 0), (1, 64)):
                    m_sc = P.op("pe", lambda e, pp=pp, hd=hd, hp=hp, off=off, ki=ki, hc=hc, g=g:
                                e.matmul(ps_sc[:, pp * 1024 + hd * 512 + off * 128:
                                              pp * 1024 + hd * 512 + 512],
                                         k_fm[hp:hp + 64, hc, ki * 128:(ki + 1) * 128],
                                         q_fm[hp:hp + 64, hc, g * 512 + off * 128:(g + 1) * 512],
                                         start=True, stop=True,
                                         tile_position=(hp, 0), skip_group_check=True),
                                waits=[pair_reader[pp], m_qkv_done] if hd == 0 else [],
                                sig="pe" if hd == 1 else None)
                m_ex = P.op("act", lambda e, pp=pp, cp=cp, off=off, ncol=ncol:
                            e.activation(chunkpair(cp).rearrange("p (a n) -> p a n", a=2)
                                         [:, :, 0:ncol],
                                         ps_sc[:, pp * 1024:(pp + 1) * 1024]
                                         .rearrange("p (a n) -> p a n", a=2)
                                         [:, :, off * 128:512], AF.Exp,
                                         bias=0.0, scale=0.125),
                            waits=[m_sc, ch_reader[cp]], sig="ac")
                pair_reader[pp] = m_ex
                for hd in range(2):
                    gate = m_ex
                    if ki >= 4 * g:
                        gate = P.op("pool", lambda e, cp=cp, hd=hd: e.tensor_tensor(
                                        chunkpair(cp)[:, hd * 512:hd * 512 + 128],
                                        chunkpair(cp)[:, hd * 512:hd * 512 + 128],
                                        tri01[:], ALU.mult),
                                    waits=[m_ex, m_cm], sig="pl")
                    pend.append((cp, off, ncol, gate, ki, hd))
                gki += 1
                pump()
            pop_pv(drain=True)
            if defer is not None:
                defer[1]()
            cell = []

            def norm_a(m_pv_last=tuple(m_pv_last)):
                nonlocal r_pvsb
                for hd in range(2):
                    m_evac = P.op("dve", lambda e, hd=hd: e.tensor_copy(
                                      pv_sbB[0:66, hd * 512:(hd + 1) * 512],
                                      ps_pv[hd][0:66, 0:512]),
                                  waits=[m_pv_last[hd], r_pvsb], sig="dv")
                    m_pv_read[hd] = m_evac
                    cell.append(m_evac)

            def normalize(par=par, hc=hc, g=g):
                nonlocal r_pvsb, r_recips, r_psbf
                m_pvT = None
                for hd in range(2):
                    for qt in range(4):
                        m_pvT = P.op("pe", lambda e, qt=qt, hd=hd: e.transpose(
                                         ps_bf[:, hd * 288 + qt * 72:hd * 288 + qt * 72 + 66],
                                         pv_sbB[0:66, hd * 512 + qt * 128:hd * 512 + (qt + 1) * 128],
                                         id_b[0:66, 0:66]),
                                     waits=[cell[hd], r_psbf] if qt == 0 else [],
                                     sig="pe" if (hd == 1 and qt == 3) else None)
                r_pvsb = m_pvT
                m_last = None
                for hd in range(2):
                    hh = 2 * hc + hd
                    m_rc = P.op("dve", lambda e, hd=hd: e.reciprocal(
                                    recips.rearrange("p (a b) -> p a b", b=1),
                                    ps_bf[:, hd * 288:hd * 288 + 288]
                                    .rearrange("p (a b) -> p a b", b=72)[:, :, 64:65]),
                                waits=[m_pvT, r_recips], sig="dv")
                    for qt in range(4):
                        dst = (ao(par, qt) if g > 0 else ao_g0(qt))
                        m_last = P.op("dve", lambda e, qt=qt, hh=hh, hd=hd, dst=dst:
                                      e.tensor_scalar(
                                          dst[:, hh * 64:(hh + 1) * 64],
                                          ps_bf[:, hd * 288 + qt * 72:hd * 288 + qt * 72 + 64],
                                          recips[:, qt:qt + 1], None, ALU.mult),
                                      waits=[m_rc, ((r_ao[par] if g > 0 else r_aoT[qt % 2])
                                                    if hc == 0 and hd == 0 and qt == 0 else None),
                                             (last_A_pe if g <= 2 and hc == 0 and hd == 0 and qt == 0 else None)],
                                      sig="dv")
                    r_recips = m_last
                r_psbf = m_last
                if hc == H // 2 - 1:
                    if g > 0:
                        w_ao[par] = m_last
                    else:
                        w_ao_g0[0] = m_last
            return (norm_a, normalize)

        cstate = {}

        def gen_C1a1(cg, qt):
            nonlocal next_pref, r_psbf
            par = cg % 2
            qi = 4 * cg + qt
            i3, i2 = qi % 3, qi % 2
            while next_pref <= min(qi + 2, T - 1):
                m_xr_all[next_pref] = sdma(x_rr(next_pref % 3), x_t[next_pref],
                                           waits=[r_xrr[next_pref % 3]], sig=f"dX{next_pref % 3}")
                next_pref += 1
            m_xpb = P.op("pool", lambda e, i3=i3: e.tensor_tensor(
                             x_rr(i3), x_rr(i3), bpj_bc[:], ALU.add),
                         waits=[m_xr_all[qi], m_bias["p"]], sig="pl")
            yield
            yield
            yield
            src_ao = ao(par, qt) if cg > 0 else ao_g0(qt)
            m_tr = transpose8(src_ao, [w_ao[par] if cg > 0 else w_ao_g0[0], r_psbf])
            yield
            if cg > 0:
                r_ao[par] = m_tr
            m_aoT = P.op("dve", lambda e, src_ao=src_ao: e.tensor_copy(
                             src_ao.rearrange("p (c n) -> p c n", c=KC), psbf3),
                         waits=[m_tr, r_aoT[i2]], sig="dv")
            r_psbf = m_aoT
            cstate[(cg, qt)] = [i2, qi, i3, src_ao, m_xpb, m_aoT, None, None]

        def gen_C1a2(cg, qt, use_sc=False):
            nonlocal next_pref, r_psw
            st = cstate[(cg, qt)]
            i2, qi, i3, src_ao, m_xpb, m_aoT = st[0], st[1], st[2], st[3], st[4], st[5]
            aoTv = src_ao.rearrange("p (c n) -> p c n", c=KC)
            mmp = None
            m_x2 = None
            for half in range(2):
                pdst = ps_sc[:, half * 512:(half + 1) * 512] if use_sc else ps_w[:, 0:512]
                if use_sc:
                    w0 = [m_aoT, m_wproj,
                          r_scp[half] if r_scp[half] is not None else pair_reader[0]]
                else:
                    w0 = [m_aoT, m_wproj, r_psw] if half == 0 else [m_x2]
                for kc in range(KC):
                    mmp = P.op("pe", lambda e, kc=kc, aoTv=aoTv, pdst=pdst, half=half:
                               e.matmul(pdst,
                                        aoTv[:, kc, :],
                                        wproj_bf[:, kc, half * 512:(half + 1) * 512],
                                        start=(kc == 0), stop=(kc == KC - 1)),
                               waits=w0 if kc == 0 else [],
                               sig="pe")
                    if kc == 3:
                        yield
                m_x2 = P.op("dve", lambda e, i2=i2, i3=i3, half=half, pdst=pdst, mmp=mmp:
                            e.scalar_tensor_tensor(
                                x2b(i2)[:, half * 512:(half + 1) * 512], pdst, 1.0,
                                x_rr(i3)[:, half * 512:(half + 1) * 512], ALU.mult, ALU.add),
                            waits=[mmp, m_xpb, r_x2[i2] if (half == 0 or use_sc) else None],
                            sig="dv")
                if use_sc:
                    r_scp[half] = m_x2
                yield
            r_aoT[i2] = mmp
            if not use_sc:
                r_psw = m_x2
            r_xrr[i3] = m_x2
            if next_pref <= min(qi + 3, T - 1):
                m_xr_all[next_pref] = sdma(x_rr(next_pref % 3), x_t[next_pref],
                                           waits=[r_xrr[next_pref % 3]], sig=f"dX{next_pref % 3}")
                next_pref += 1
            w_x2[i2] = m_x2
            yield
            st[6] = ln_stats(x2b(i2), m_x2, qi % 2)

        def gen_C1b(cg, qt):
            st = cstate[(cg, qt)]
            st[7] = ln_fin(x2b(st[0]), st[0], st[6], st[1] % 2)
            yield

        def gen_C2(cg, qt, use_pv=False):
            nonlocal r_psw, r_psbf
            st = cstate.pop((cg, qt))
            i2, qi, m_h = st[0], st[1], st[7]
            m_x2f = P.op("pool", lambda e, i2=i2: e.tensor_tensor(
                             x2b(i2), x2b(i2), bfn_bc[:], ALU.add),
                         waits=[m_h, m_bias["f"], r_x2[i2]], sig="pl")
            yield
            yield
            m_tr = transpose8(xn_b[:, i2, :], [m_h, r_psbf])
            yield
            m_hT = P.op("dve", lambda e, i2=i2: e.tensor_copy(
                            xn_b.rearrange("p a (c n) -> p a c n", c=KC)[:, i2],
                            psbf3),
                        waits=[m_tr, r_hT[i2]], sig="dv")
            r_xn[i2] = m_hT
            r_psbf = m_hT
            yield
            hTv = xn_b.rearrange("p a (c n) -> p a c n", c=KC)
            mmu = None
            for kc in range(KC):
                mmu = P.op("pe", lambda e, kc=kc, i2=i2:
                           e.matmul(ps_w[0:32, 0:128], vr_bf[:, kc, :], hTv[:, i2, kc, :],
                                    start=(kc == 0), stop=(kc == KC - 1)),
                           waits=[m_hT, m_vr, r_psw] if kc == 0 else [],
                           sig="pe" if kc == KC - 1 else None)
            m_u = P.op("dve", lambda e: e.tensor_copy(u_bf, ps_w[0:32, 0:128]),
                       waits=[mmu], sig="dv")
            r_psw = m_u
            yield
            def fdst(half):
                return ps_pv[half][:, 0:512] if use_pv else ps_w[:, 0:512]
            mmf = None
            m_y = None
            for half in range(2):
                w0 = [m_hT, m_wt_evac]
                if not use_pv:
                    w0.append(r_psw if half == 0 else m_y)
                if use_pv:
                    w0.append(r_pvffn[half] if r_pvffn[half] is not None
                              else m_pv_read[half])
                for kc in range(KC):
                    P.op("pe", lambda e, half=half, kc=kc, i2=i2:
                         e.matmul(fdst(half),
                                  hTv[:, i2, kc, :],
                                  wt_bf[:, kc, half * 512:(half + 1) * 512],
                                  start=(kc == 0), stop=False),
                         waits=w0 if kc == 0 else [],
                         sig=None)
                    if kc == 3:
                        yield
                mmf = P.op("pe", lambda e, half=half:
                           e.matmul(fdst(half), u_bf,
                                    ct_bf[0:32, half * 512:(half + 1) * 512],
                                    start=False, stop=True),
                           waits=[m_u, m_ct] if half == 0 else [],
                           sig="pe")
                m_y = P.op("dve", lambda e, i2=i2, half=half, mmf=mmf: e.scalar_tensor_tensor(
                               x2b(i2)[:, half * 512:(half + 1) * 512],
                               fdst(half), 1.0,
                               x2b(i2)[:, half * 512:(half + 1) * 512],
                               ALU.mult, ALU.add),
                           waits=[mmf, m_x2f] if half == 0 else [mmf], sig="dv")
                if use_pv:
                    r_pvffn[half] = m_y
                yield
            r_hT[i2] = mmf
            if not use_pv:
                r_psw = m_y
            m_dy = P.op("sync", lambda e, i2=i2, qi=qi: e.dma_start(
                            out=y_t[qi], in_=x2b(i2)),
                        waits=[m_y], sig=f"dY{i2}", inc=DMA16)
            r_x2[i2] = m_dy

        from collections import deque
        fillers = deque()

        def pump():
            while fillers:
                try:
                    next(fillers[0])
                    return
                except StopIteration:
                    fillers.popleft()

        def drain_fillers():
            while fillers:
                try:
                    next(fillers[0])
                except StopIteration:
                    fillers.popleft()

        slotmap = {
            0: [("c", 2, 2), ("a2", 3, 2)],
            1: [("b", 3, 2), ("a1", 0, 1)],
            2: [("c", 3, 2), ("a2", 0, 1)],
            3: [("b", 0, 1), ("a1", 1, 1)],
            4: [("c", 0, 1), ("a2", 1, 1)],
            5: [("b", 1, 1), ("a1", 2, 1)],
            6: [("c", 1, 1), ("a2", 2, 1)],
            7: [("b", 2, 1), ("a1", 3, 1)],
        }
        gens = {"a1": gen_C1a1, "a2": gen_C1a2, "b": gen_C1b, "c": gen_C2}
        r_pvffn = [None, None]
        r_scp = [None, None]
        norm = None
        for g in range(G):
            for p in range(H // 2):
                norm = emit_B_pair(g, p, defer=norm)
                for kind, qt, goff in slotmap.get(p, []):
                    if g - goff >= 0:
                        fillers.append(gens[kind](g - goff, qt))
        drain_fillers()
        if norm is not None:
            norm[0]()
            norm[1]()
        if G >= 2:
            for job in (gens["c"](G - 2, 2, use_pv=True), gens["a2"](G - 2, 3, use_sc=True),
                        gens["b"](G - 2, 3), gens["c"](G - 2, 3, use_pv=True)):
                fillers.append(job)
                drain_fillers()
        tail = [("a1", 0), ("a2", 0), ("a1", 1), ("a2", 1),
                ("b", 0), ("c", 0), ("b", 1), ("c", 1),
                ("a1", 2), ("a2", 2), ("a1", 3), ("a2", 3),
                ("b", 2), ("c", 2), ("b", 3), ("c", 3)]
        for kind, qt in tail:
            if kind == "c":
                job = gens["c"](G - 1, qt, use_pv=True)
            elif kind == "a2":
                job = gens["a2"](G - 1, qt, use_sc=True)
            else:
                job = gens[kind](G - 1, qt)
            fillers.append(job)
            drain_fillers()
        P.op("sync", lambda e: e.nop(),
             waits=[("dY0", P.cnt.get("dY0", 0)), ("dY1", P.cnt.get("dY1", 0))])

        P.emit(block, sems)
    return nc


_NC_CACHE = {}


def _get_nc(L):
    if L not in _NC_CACHE:
        _NC_CACHE[L] = build_nc(L)
    return _NC_CACHE[L]


def _consts():
    i, j = np.meshgrid(np.arange(128), np.arange(128), indexing="ij")
    tri01 = np.where(i <= j, np.float32(1.0), np.float32(0.0)).astype(bfdt)
    return {
        "identb": np.eye(128, dtype=np.float32).astype(bfdt),
        "identf": np.eye(66, dtype=np.float32),
        "tri01": tri01,
    }


def make_in_maps(x, Wqkv, bqkv, Wproj, bproj, W, b, V_r, C):
    shared = {
        "wqkv": np.ascontiguousarray(np.asarray(Wqkv, dtype=np.float32)),
        "bqkv": np.ascontiguousarray(np.asarray(bqkv, dtype=np.float32)),
        "wproj": np.ascontiguousarray(np.asarray(Wproj, dtype=np.float32)),
        "bproj": np.ascontiguousarray(np.asarray(bproj, dtype=np.float32)),
        "wffn": np.ascontiguousarray(np.asarray(W, dtype=np.float32)),
        "bffn": np.ascontiguousarray(np.asarray(b, dtype=np.float32)),
        "vr": np.ascontiguousarray(np.asarray(V_r, dtype=np.float32)),
        "cmat": np.ascontiguousarray(np.asarray(C, dtype=np.float32)),
        **_consts(),
    }
    return [dict(shared, x=np.ascontiguousarray(x[bi])) for bi in range(x.shape[0])]


def kernel(L0_attn_x, L0_attn_Wqkv, L0_attn_bqkv, L0_attn_Wproj, L0_attn_bproj,
           L0_ffn_W, L0_ffn_b, L0_ffn_V_r, L0_ffn_C):
    from concourse.bass_utils import run_bass_kernel_spmd
    x = np.ascontiguousarray(np.asarray(L0_attn_x, dtype=np.float32))
    B, L, _ = x.shape
    nc = _get_nc(L)
    in_maps = make_in_maps(x, L0_attn_Wqkv, L0_attn_bqkv, L0_attn_Wproj, L0_attn_bproj,
                           L0_ffn_W, L0_ffn_b, L0_ffn_V_r, L0_ffn_C)
    res = run_bass_kernel_spmd(nc, in_maps, core_ids=list(range(B)))
    return np.stack([np.asarray(res.results[bi]["y"]) for bi in range(B)], axis=0)


# version 44
# speedup vs baseline: 1.0213x; 1.0074x over previous
import numpy as np
import ml_dtypes
from contextlib import ExitStack

import concourse.bass as bass
import concourse.mybir as mybir

F32 = mybir.dt.float32
BF16 = mybir.dt.bfloat16
AF = mybir.ActivationFunctionType
ALU = mybir.AluOpType
bfdt = ml_dtypes.bfloat16

D = 1024
KC = 8
H = 16
NEG = -1.0e30
DMA16 = 16


class Plan:
    ENGS = ("sync", "pool", "pe", "dve", "act")

    def __init__(self):
        self.ops = {e: [] for e in self.ENGS}
        self.cnt = {}

    def op(self, eng, fn, waits=(), sig=None, inc=1):
        ws = [w for w in waits if w is not None]
        self.ops[eng].append([fn, ws, sig, inc])
        if sig is not None:
            self.cnt[sig] = self.cnt.get(sig, 0) + inc
            return (sig, self.cnt[sig])
        return None

    def emit(self, block, sems):
        def make(ops):
            def run(e):
                seen = {}
                for fn, waits, sig, inc in ops:
                    for s, v in waits:
                        if seen.get(s, 0) < v:
                            e.wait_ge(sems[s], v)
                            seen[s] = v
                    instr = fn(e)
                    if sig is not None:
                        instr.then_inc(sems[sig], inc)
            return run
        block.sync(make(self.ops["sync"]))
        block.gpsimd(make(self.ops["pool"]))
        block.tensor(make(self.ops["pe"]))
        block.vector(make(self.ops["dve"]))
        block.scalar(make(self.ops["act"]))


def build_nc(L):
    T = L // 128
    G = L // 512
    assert L % 512 == 0

    nc = bass.Bass("TRN2", target_bir_lowering=False)

    x_d = nc.dram_tensor("x", [L, D], F32, kind="ExternalInput")
    wqkv_d = nc.dram_tensor("wqkv", [D, 3 * D], F32, kind="ExternalInput")
    bqkv_d = nc.dram_tensor("bqkv", [3 * D], F32, kind="ExternalInput")
    wproj_d = nc.dram_tensor("wproj", [D, D], F32, kind="ExternalInput")
    bproj_d = nc.dram_tensor("bproj", [D], F32, kind="ExternalInput")
    wffn_d = nc.dram_tensor("wffn", [D, D], F32, kind="ExternalInput")
    bffn_d = nc.dram_tensor("bffn", [D], F32, kind="ExternalInput")
    vr_d = nc.dram_tensor("vr", [D, 32], F32, kind="ExternalInput")
    cmat_d = nc.dram_tensor("cmat", [D, 32], F32, kind="ExternalInput")
    identb_d = nc.dram_tensor("identb", [128, 128], BF16, kind="ExternalInput")
    identf_d = nc.dram_tensor("identf", [66, 66], F32, kind="ExternalInput")
    tri01_d = nc.dram_tensor("tri01", [128, 128], BF16, kind="ExternalInput")
    y_d = nc.dram_tensor("y", [L, D], F32, kind="ExternalOutput")

    x_t = x_d.rearrange("(a p) n -> a p n", p=128)
    y_t = y_d.rearrange("(a p) n -> a p n", p=128)

    with ExitStack() as ctx:
        E = ctx.enter_context
        q_fm = E(nc.sbuf_tensor("q_fm", [128, KC, L], BF16))
        k_fm = E(nc.sbuf_tensor("k_fm", [128, KC, L], BF16))
        v_on = E(nc.sbuf_tensor("v_on", [128, T, 1118], BF16))
        wq_a = E(nc.sbuf_tensor("wq_a", [128, KC, 3 * D], BF16))
        xnt_a = E(nc.sbuf_tensor("xnt_a", [128, 8192], BF16))
        xf_a = E(nc.sbuf_tensor("xf_a", [128, 6656], F32))
        xn_b = E(nc.sbuf_tensor("xn_b", [128, 2, D], BF16))
        bias_a = E(nc.sbuf_tensor("bias_a", [128, 2 * D], F32))
        bv_bc = bias_a[:, 0:D]
        bpj_bc = bias_a[:, 0:D]
        bfn_bc = bias_a[:, D:2 * D]
        tri01 = E(nc.sbuf_tensor("tri01_sb", [128, 128], BF16))
        id_b = E(nc.sbuf_tensor("id_b", [128, 128], BF16))
        id_f = E(nc.sbuf_tensor("id_f", [66, 66], F32))
        bqkvT = E(nc.sbuf_tensor("bqkvT", [128, 24], F32))
        bqkv_sb = E(nc.sbuf_tensor("bqkv_sb", [24, 128], F32))
        vr_bf = E(nc.sbuf_tensor("vr_bf", [128, KC, 32], BF16))
        c_st = E(nc.sbuf_tensor("c_st", [128, KC, 32], BF16))
        ct_bf = E(nc.sbuf_tensor("ct_bf", [32, D], BF16))
        stats = E(nc.sbuf_tensor("stats", [128, 2, 12], F32))
        mv = E(nc.sbuf_tensor("mv", [128, 2, 2], F32))
        ln_t = E(nc.sbuf_tensor("ln_t", [128, 2], F32))
        rstd = E(nc.sbuf_tensor("rstd", [128, 2], F32))
        negmr = E(nc.sbuf_tensor("negmr", [128, 2], F32))
        recips = E(nc.sbuf_tensor("recips", [128, 4], F32))
        eps_t = E(nc.sbuf_tensor("eps_t", [128, 1], F32))
        ps_sc = E(nc.psum_tensor("ps_sc", [128, 2048], F32))
        ps_w = E(nc.psum_tensor("ps_w", [128, 512], F32))
        ps_bf = E(nc.psum_tensor("ps_bf", [128, 1024], BF16))
        ps_pv = [E(nc.psum_tensor(f"ps_pv{i}", [128, 512], F32)) for i in range(2)]
        semnames = ["dI", "dX0", "dX1", "dX2", "dY0", "dY1", "dWq", "dWk", "dWv", "dWs", "dWp", "dWf0", "dWf1", "dWf2", "dWf3", "dBp", "pe", "dv", "ac", "pl"]
        sems = {n: E(nc.semaphore(n)) for n in semnames}
        block = E(nc.Block())

        P = Plan()

        def xnT(buf):
            return xnt_a[:, buf * 4096:(buf + 1) * 4096].rearrange("p (c n) -> p c n", c=KC)
        def aoT(buf):
            return xnt_a[:, buf * 1024:(buf + 1) * 1024].rearrange("p (c n) -> p c n", c=KC)
        def hT(buf):
            return xnt_a[:, 2048 + buf * 1024:2048 + (buf + 1) * 1024].rearrange("p (c n) -> p c n", c=KC)
        def chunkpair(cp):
            if cp >= 2:
                o = 5120 + (cp - 2) * 512
                return xf_a[:, o:o + 512].bitcast(BF16)
            return xnt_a[:, 4096 + cp * 1024:4096 + (cp + 1) * 1024]
        u_bf = xnt_a[0:32, 7168:7296]
        def ao_g0(qt):
            return xnt_a[:, qt * 1024:(qt + 1) * 1024]
        pv_sbB = xnt_a[:, 6144:7168]
        def x_in(i):
            return xf_a[:, i * D:(i + 1) * D]
        x_rr = x_in
        def x2b(i):
            return xf_a[:, 3072 + i * D:3072 + (i + 1) * D]

        wstage = xf_a[:, 0:4096].bitcast(BF16).rearrange("p (c n) -> p c n", c=KC)
        wproj_bf = wq_a[:, :, 0:D]
        wt_bf = wq_a[:, :, D:2 * D]
        def ao(par, qt):
            return wq_a[:, 4 * par + qt, 2 * D:3 * D]
        psbf3 = ps_bf.rearrange("p (c n) -> p c n", c=KC)

        def sdma(dst, src, waits=(), sig="dI"):
            return P.op("sync", lambda e, d=dst, s=src: e.dma_start(out=d, in_=s),
                        waits=waits, sig=sig, inc=DMA16)
        m_idb = sdma(id_b[:], identb_d.ap())
        m_idf = sdma(id_f[:], identf_d.ap())
        m_cm = sdma(tri01[:], tri01_d.ap())
        m_bq = sdma(bqkv_sb[:], bqkv_d.rearrange("(a b) -> a b", b=128))
        m_init = ("dI", P.cnt["dI"])
        m_idb = m_idf = m_cm = m_bq = m_init

        def gdma(dst, src, waits=(), sig="dWs"):
            return P.op("pool", lambda e, d=dst, s=src: e.dma_start(out=d, in_=s),
                        waits=waits, sig=sig, inc=DMA16)
        wqkv_r = wqkv_d.rearrange("(c p) n -> p c n", p=128)
        m_wq = gdma(wq_a[:, :, 0:D], wqkv_r[:, :, 0:D], sig="dWq")
        m_wk = gdma(wq_a[:, :, D:2 * D], wqkv_r[:, :, D:2 * D], sig="dWk")
        m_wv = gdma(wq_a[:, :, 2 * D:3 * D], wqkv_r[:, :, 2 * D:3 * D], sig="dWv")
        gdma(vr_bf[:, :, :], vr_d.rearrange("(c p) n -> p c n", p=128))
        gdma(c_st[:, :, :], cmat_d.rearrange("(c p) n -> p c n", p=128))
        m_vr = m_cst = ("dWs", 64)
        gdma(bv_bc, bqkv_d.rearrange("(a b) -> a b", a=3)[2:3, :].broadcast_to((128, D)),
             sig="dWs")
        gdma(bfn_bc, bffn_d.rearrange("(a b) -> a b", a=1).broadcast_to((128, D)),
             sig="dWs")
        m_bv_bc = m_bfn_bc = ("dWs", 64)

        m_eps = P.op("dve", lambda e: e.memset(eps_t[:], 1e-5), sig="dv")
        m_vones = P.op("dve", lambda e: e.memset(
                           v_on[:, :, 0:1056].rearrange("p t (h n) -> p t h n", h=H)[:, :, :, 64:66],
                           1.0),
                       waits=[m_eps], sig="dv")
        m_vones = P.op("dve", lambda e: e.memset(v_on[:, :, 1056:1118], 0.0),
                       waits=[m_vones], sig="dv")
        m_tbq = P.op("pe", lambda e: e.transpose(ps_w[0:128, 0:24], bqkv_sb[0:24, 0:128],
                                                 id_f[0:24, 0:24]),
                     waits=[m_bq, m_idf], sig="pe")
        m_bqkvT = P.op("dve", lambda e: e.tensor_copy(bqkvT[:], ps_w[0:128, 0:24]),
                       waits=[m_tbq, m_vones], sig="dv")
        m_bias = {"v": m_bv_bc, "f": m_bfn_bc}
        m_psw_free = None

        r_xin = [None, None, None]
        w_xn = [None, None]
        r_xn = [None, None]
        w_xnt = [None, None]
        r_xnt = [None, None]
        r_psbf = None
        m_ln_last = [None, None]
        r_rstd = [None, None]
        bank_reader = [None, None, None, None]
        qkv_bank = 0
        m_v_evac = [None, None]
        last_A_pe = None

        def ln_stats(src, extra_wait, lp):
            m1 = P.op("dve", lambda e, lp=lp: e.bn_stats(stats[:, lp, 0:6], src[:, 0:512]),
                      waits=[extra_wait, m_ln_last[lp]], sig="dv")
            m2 = P.op("dve", lambda e, lp=lp: e.bn_stats(stats[:, lp, 6:12], src[:, 512:1024]),
                      waits=[m1], sig="dv")
            m3 = P.op("dve", lambda e, lp=lp: e.bn_aggr(mv[:, lp, :], stats[:, lp, :]),
                      waits=[m2], sig="dv")
            return m3

        def ln_fin(src, i2, m3, lp):
            m4 = P.op("act", lambda e, lp=lp: e.activation(ln_t[:, lp:lp + 1], mv[:, lp, 1:2],
                                                           AF.Ln, bias=eps_t[:], scale=1.0),
                      waits=[m3, r_rstd[lp], m_eps], sig="ac")
            m5 = P.op("act", lambda e, lp=lp: e.activation(rstd[:, lp:lp + 1], ln_t[:, lp:lp + 1],
                                                           AF.Exp, bias=0.0, scale=-0.5),
                      waits=[m4], sig="ac")
            m6 = P.op("dve", lambda e, lp=lp: e.tensor_scalar(negmr[:, lp:lp + 1], mv[:, lp, 0:1],
                                                              rstd[:, lp:lp + 1], -1.0,
                                                              ALU.mult, ALU.mult),
                      waits=[m5, m3], sig="dv")
            m7 = P.op("dve", lambda e, i2=i2, lp=lp: e.tensor_scalar(
                          xn_b[:, i2, :], src, rstd[:, lp:lp + 1], negmr[:, lp:lp + 1],
                          ALU.mult, ALU.add),
                      waits=[m6, r_xn[i2]], sig="dv")
            w_xn[i2] = m7
            m_ln_last[lp] = m7
            r_rstd[lp] = m7
            return m7

        def ln_chain(src, i2, extra_wait, lp):
            return ln_fin(src, i2, ln_stats(src, extra_wait, lp), lp)

        def transpose8(src1024_bf, waits_first):
            m_tr = None
            for c in range(KC):
                m_tr = P.op("pe", lambda e, c=c: e.transpose(
                                ps_bf[:, c * 128:(c + 1) * 128],
                                src1024_bf[:, c * 128:(c + 1) * 128], id_b[:]),
                            waits=(waits_first + [m_idb]) if c == 0 else [],
                            sig="pe" if c == KC - 1 else None)
            return m_tr

        qkv_banks = [ps_sc[:, i * 512:(i + 1) * 512] for i in range(4)]

        def mats_gen(tb):
            nonlocal qkv_bank, m_v_evac, last_A_pe
            tbuf = tb % 2
            for oc in range(16):
                mm = None
                for kc in range(KC):
                    mm = P.op("pe", lambda e, oc=oc, kc=kc, b=qkv_bank, tbuf=tbuf:
                              e.matmul(qkv_banks[b][:, :],
                                       wq_a[:, kc, oc * 128:(oc + 1) * 128],
                                       xnT(tbuf)[:, kc, :],
                                       start=(kc == 0), stop=(kc == KC - 1)),
                              waits=[(m_wq if oc < 8 else m_wk), w_xnt[tbuf],
                                     bank_reader[qkv_bank]] if kc == 0 else [],
                              sig="pe" if kc == KC - 1 else None)
                dst = q_fm if oc < 8 else k_fm
                cc = oc % 8
                m_qk = P.op("act", lambda e, dst=dst, cc=cc, oc=oc, b=qkv_bank, tb=tb:
                            e.activation(dst[:, cc, tb * 512:(tb + 1) * 512],
                                         qkv_banks[b][:, :], AF.Identity,
                                         bias=bqkvT[:, oc:oc + 1], scale=1.0),
                            waits=[mm, m_bqkvT], sig="ac")
                bank_reader[qkv_bank] = m_qk
                r_xnt[tbuf] = mm
                qkv_bank = (qkv_bank + 1) % 4
                if oc % 2 == 1:
                    yield
            for tt in range(4):
                kt = tb * 4 + tt
                mmv = [None, None]
                for half in range(2):
                    for kc in range(KC):
                        mmv[half] = P.op("pe", lambda e, half=half, kc=kc, tbuf=tbuf, tt=tt:
                                   e.matmul(ps_pv[half][:, 0:512],
                                            xnT(tbuf)[:, kc, tt * 128:(tt + 1) * 128],
                                            wq_a[:, kc, 2 * D + half * 512:2 * D + (half + 1) * 512],
                                            start=(kc == 0), stop=(kc == KC - 1)),
                                   waits=[m_v_evac[half], w_xnt[tbuf], m_wv] if kc == 0 else [],
                                   sig="pe" if kc == KC - 1 else None)
                r_xnt[tbuf] = mmv[1]
                for half in range(2):
                    m_v_evac[half] = P.op("dve", lambda e, kt=kt, half=half: e.tensor_tensor(
                                        v_on[:, kt, half * 528:(half + 1) * 528]
                                        .rearrange("p (h n) -> p h n", h=H // 2)[:, :, 0:64],
                                        ps_pv[half].rearrange("p (h n) -> p h n", h=H // 2),
                                        bv_bc[:, half * 512:(half + 1) * 512]
                                        .rearrange("p (h n) -> p h n", h=H // 2), ALU.add),
                                    waits=[mmv[half], m_bias["v"], m_vones], sig="dv")
                last_A_pe = mmv[1]
                yield

        a_gen = [None]

        def pump_a():
            if a_gen[0] is not None:
                try:
                    next(a_gen[0])
                except StopIteration:
                    a_gen[0] = None

        def drain_a():
            while a_gen[0] is not None:
                pump_a()

        for t in range(T):
            tb, ti = t // 4, t % 4
            i3, i2 = t % 3, t % 2
            tbuf = tb % 2
            m_x = sdma(x_in(i3), x_t[t], waits=[r_xin[i3]], sig=f"dX{i3}")
            m_xn = ln_chain(x_in(i3), i2, m_x, t % 2)
            r_xin[i3] = m_xn
            pump_a()
            m_tr = transpose8(xn_b[:, i2, :], [m_xn, r_psbf])
            r_xn[i2] = m_tr
            m_ev = P.op("dve", lambda e, tbuf=tbuf, ti=ti: e.tensor_copy(
                            xnT(tbuf)[:, :, ti * 128:(ti + 1) * 128], psbf3),
                        waits=[m_tr, r_xnt[tbuf]], sig="dv")
            r_psbf = m_ev
            pump_a()
            if ti == 3:
                w_xnt[tbuf] = m_ev
                drain_a()
                a_gen[0] = mats_gen(tb)
        drain_a()
        m_qkv_done = bank_reader[(qkv_bank - 1) % 4]
        m_v_done = m_v_evac[1]

        m_wproj = gdma(wproj_bf, wproj_d.rearrange("(c p) n -> p c n", p=128),
                       waits=[last_A_pe], sig="dWp")
        m_bias["p"] = sdma(bpj_bc,
                           bproj_d.rearrange("(a b) -> a b", a=1).broadcast_to((128, D)),
                           waits=[m_v_done], sig="dBp")
        wffn_r = wffn_d.rearrange("(r p) n -> p r n", p=128)
        arena_guard = [r_xin[0], r_xin[1], r_xin[2]]
        m_wfc = []
        for c4 in range(4):
            g4 = arena_guard[c4 * 2:c4 * 2 + 2]
            m_wfc.append(gdma(wstage[:, c4 * 2:c4 * 2 + 2, :],
                              wffn_r[:, c4 * 2:c4 * 2 + 2, :],
                              waits=g4, sig=f"dWf{c4}"))
        m_wt_done = None
        for r in range(KC):
            m_tr = None
            for dc in range(KC):
                m_tr = P.op("pe", lambda e, r=r, dc=dc: e.transpose(
                                ps_bf[:, dc * 128:(dc + 1) * 128],
                                wstage[:, r, dc * 128:(dc + 1) * 128], id_b[:]),
                            waits=[m_wfc[r // 2], r_psbf] if dc == 0 else [],
                            sig="pe" if dc == KC - 1 else None)
            r_psbf = P.op("dve", lambda e, r=r: e.tensor_copy(
                              wt_bf[:, :, r * 128:(r + 1) * 128],
                              ps_bf[:, :].rearrange("p (c n) -> p c n", c=KC)),
                          waits=[m_tr, m_wproj if r == 0 else None], sig="dv")
        m_wt_done = m_tr
        m_wt_evac = r_psbf
        m_trc = None
        for c in range(KC):
            m_trc = P.op("pe", lambda e, c=c: e.transpose(
                             ps_bf[0:32, c * 128:(c + 1) * 128], c_st[:, c, :], id_b[:]),
                         waits=[m_cst, r_psbf] if c == 0 else [],
                         sig="pe" if c == KC - 1 else None)
        m_ct = P.op("dve", lambda e: e.tensor_copy(
                        ct_bf[0:32, :], ps_bf[0:32, :].rearrange("p (c n) -> p c n", c=KC)),
                    waits=[m_trc], sig="dv")
        r_psbf = m_ct
        vrT = xnt_a[0:32, 7168:8192]
        m_trv = None
        for c in range(KC):
            m_trv = P.op("pe", lambda e, c=c: e.transpose(
                             ps_bf[0:32, c * 128:(c + 1) * 128], vr_bf[:, c, :], id_b[:]),
                         waits=[m_vr, r_psbf] if c == 0 else [],
                         sig="pe" if c == KC - 1 else None)
        m_vrT = P.op("dve", lambda e: e.tensor_copy(
                         vrT[:, :], ps_bf[0:32, :].rearrange("p (c n) -> p c n", c=KC)),
                     waits=[m_trv], sig="dv")
        r_psbf = m_vrT
        fold_banks = [ps_w[:, 0:512], ps_pv[0][:, 0:512]]
        fold_r = [m_bqkvT, m_v_evac[0]]
        m_fold = None
        for kc in range(KC):
            for half in range(2):
                pb = (kc * 2 + half) % 2
                mmc = P.op("pe", lambda e, kc=kc, half=half, pb=pb:
                           e.matmul(fold_banks[pb],
                                    vrT[:, kc * 128:(kc + 1) * 128],
                                    ct_bf[0:32, half * 512:(half + 1) * 512],
                                    start=True, stop=True),
                           waits=[m_vrT, m_ct, fold_r[pb]], sig="pe")
                m_fold = P.op("dve", lambda e, kc=kc, half=half, pb=pb, mmc=mmc:
                              e.tensor_tensor(
                                  wt_bf[:, kc, half * 512:(half + 1) * 512],
                                  wt_bf[:, kc, half * 512:(half + 1) * 512],
                                  fold_banks[pb], ALU.add),
                              waits=[mmc, m_wt_evac if kc == 0 and half == 0 else None],
                              sig="dv")
                fold_r[pb] = m_fold
        m_wt_final = m_fold
        r_psw = fold_r[0]
        m_pv_read0_fold = fold_r[1]

        pair_reader = [m_qkv_done, m_qkv_done]
        ch_reader = [last_A_pe] * 5
        gki = 0
        m_pv_read = [m_pv_read0_fold, m_v_evac[1]]
        w_ao = [None, None]
        r_ao = [None, None]
        r_aoT = [last_A_pe, last_A_pe]
        r_hT = [last_A_pe, last_A_pe]
        w_x2 = [None, None]
        r_x2 = [m_wt_done, m_wt_done]
        r_xrr = [m_wt_done, m_wt_done, m_wt_done]
        m_xr_all = [None] * T
        next_pref = 0
        r_pvsb = last_A_pe
        r_recips = None
        w_ao_g0 = [None]

        def emit_B_pair(g, hc, defer=None):
            nonlocal gki, r_pvsb, r_recips
            par = g % 2
            he, ho = 2 * hc, 2 * hc + 1
            K = 4 * g + 4
            pend = []
            m_pv_last = [None, None]
            if defer is not None:
                defer[0]()
            def pop_pv(drain=False):
                nonlocal m_pv_last
                while len(pend) > 6 or (drain and pend):
                    cp2, off2, ncol2, gate2, ki2, hd2 = pend.pop(0)
                    m_pv_last[hd2] = P.op("pe", lambda e, cp=cp2, off=off2, ncol=ncol2,
                                          hd=hd2, hh=2 * hc + hd2, ki2=ki2,
                                          st=(ki2 == 0), sp=(ki2 == K - 1):
                                          e.matmul(ps_pv[hd][0:66, off * 128:512],
                                                   v_on[:, ki2, 66 * hh:66 * hh + 66],
                                                   chunkpair(cp)[:, hd * 512:hd * 512 + ncol],
                                                   start=st, stop=sp),
                                          waits=[gate2, (m_pv_read[hd2] if ki2 == 0 else None),
                                                 (m_v_done if ki2 == 0 else None)],
                                          sig="pe")
                    ch_reader[cp2] = m_pv_last[hd2]

            for ki in range(K):
                off = max(0, ki - 4 * g)
                ncol = 512 - off * 128
                pp = gki % 2
                cp = gki % 5
                m_sc = None
                for hd, hp in ((0, 0), (1, 64)):
                    m_sc = P.op("pe", lambda e, pp=pp, hd=hd, hp=hp, off=off, ki=ki, hc=hc, g=g:
                                e.matmul(ps_sc[:, pp * 1024 + hd * 512 + off * 128:
                                              pp * 1024 + hd * 512 + 512],
                                         k_fm[hp:hp + 64, hc, ki * 128:(ki + 1) * 128],
                                         q_fm[hp:hp + 64, hc, g * 512 + off * 128:(g + 1) * 512],
                                         start=True, stop=True,
                                         tile_position=(hp, 0), skip_group_check=True),
                                waits=[pair_reader[pp], m_qkv_done] if hd == 0 else [],
                                sig="pe" if hd == 1 else None)
                m_ex = P.op("act", lambda e, pp=pp, cp=cp, off=off, ncol=ncol:
                            e.activation(chunkpair(cp).rearrange("p (a n) -> p a n", a=2)
                                         [:, :, 0:ncol],
                                         ps_sc[:, pp * 1024:(pp + 1) * 1024]
                                         .rearrange("p (a n) -> p a n", a=2)
                                         [:, :, off * 128:512], AF.Exp,
                                         bias=0.0, scale=0.125),
                            waits=[m_sc, ch_reader[cp]], sig="ac")
                pair_reader[pp] = m_ex
                for hd in range(2):
                    gate = m_ex
                    if ki >= 4 * g:
                        gate = P.op("pool", lambda e, cp=cp, hd=hd: e.tensor_tensor(
                                        chunkpair(cp)[:, hd * 512:hd * 512 + 128],
                                        chunkpair(cp)[:, hd * 512:hd * 512 + 128],
                                        tri01[:], ALU.mult),
                                    waits=[m_ex, m_cm], sig="pl")
                    pend.append((cp, off, ncol, gate, ki, hd))
                gki += 1
                if ki % 2 == 1:
                    pop_pv()
                pump()
            pop_pv(drain=True)
            if defer is not None:
                defer[1]()
            cell = []

            def norm_a(m_pv_last=tuple(m_pv_last)):
                nonlocal r_pvsb
                for hd in range(2):
                    m_evac = P.op("dve", lambda e, hd=hd: e.tensor_copy(
                                      pv_sbB[0:66, hd * 512:(hd + 1) * 512],
                                      ps_pv[hd][0:66, 0:512]),
                                  waits=[m_pv_last[hd], r_pvsb], sig="dv")
                    m_pv_read[hd] = m_evac
                    cell.append(m_evac)

            def normalize(par=par, hc=hc, g=g):
                nonlocal r_pvsb, r_recips, r_psbf
                m_pvT = None
                for hd in range(2):
                    for qt in range(4):
                        m_pvT = P.op("pe", lambda e, qt=qt, hd=hd: e.transpose(
                                         ps_bf[:, hd * 288 + qt * 72:hd * 288 + qt * 72 + 66],
                                         pv_sbB[0:66, hd * 512 + qt * 128:hd * 512 + (qt + 1) * 128],
                                         id_b[0:66, 0:66]),
                                     waits=[cell[hd], r_psbf] if qt == 0 else [],
                                     sig="pe" if (hd == 1 and qt == 3) else None)
                r_pvsb = m_pvT
                m_last = None
                for hd in range(2):
                    hh = 2 * hc + hd
                    m_rc = P.op("dve", lambda e, hd=hd: e.reciprocal(
                                    recips.rearrange("p (a b) -> p a b", b=1),
                                    ps_bf[:, hd * 288:hd * 288 + 288]
                                    .rearrange("p (a b) -> p a b", b=72)[:, :, 64:65]),
                                waits=[m_pvT, r_recips], sig="dv")
                    for qt in range(4):
                        dst = (ao(par, qt) if g > 0 else ao_g0(qt))
                        m_last = P.op("dve", lambda e, qt=qt, hh=hh, hd=hd, dst=dst:
                                      e.tensor_scalar(
                                          dst[:, hh * 64:(hh + 1) * 64],
                                          ps_bf[:, hd * 288 + qt * 72:hd * 288 + qt * 72 + 64],
                                          recips[:, qt:qt + 1], None, ALU.mult),
                                      waits=[m_rc, ((r_ao[par] if g > 0 else r_aoT[qt % 2])
                                                    if hc == 0 and hd == 0 and qt == 0 else None),
                                             (last_A_pe if g <= 2 and hc == 0 and hd == 0 and qt == 0 else None)],
                                      sig="dv")
                    r_recips = m_last
                r_psbf = m_last
                if hc == H // 2 - 1:
                    if g > 0:
                        w_ao[par] = m_last
                    else:
                        w_ao_g0[0] = m_last
            return (norm_a, normalize)

        cstate = {}

        def gen_C1a1(cg, qt):
            nonlocal next_pref, r_psbf
            par = cg % 2
            qi = 4 * cg + qt
            i3, i2 = qi % 3, qi % 2
            while next_pref <= min(qi + 2, T - 1):
                m_xr_all[next_pref] = sdma(x_rr(next_pref % 3), x_t[next_pref],
                                           waits=[r_xrr[next_pref % 3]], sig=f"dX{next_pref % 3}")
                next_pref += 1
            m_xpb = P.op("pool", lambda e, i3=i3: e.tensor_tensor(
                             x_rr(i3), x_rr(i3), bpj_bc[:], ALU.add),
                         waits=[m_xr_all[qi], m_bias["p"]], sig="pl")
            yield
            yield
            yield
            src_ao = ao(par, qt) if cg > 0 else ao_g0(qt)
            m_tr = transpose8(src_ao, [w_ao[par] if cg > 0 else w_ao_g0[0], r_psbf])
            yield
            if cg > 0:
                r_ao[par] = m_tr
            m_aoT = P.op("dve", lambda e, src_ao=src_ao: e.tensor_copy(
                             src_ao.rearrange("p (c n) -> p c n", c=KC), psbf3),
                         waits=[m_tr, r_aoT[i2]], sig="dv")
            r_psbf = m_aoT
            cstate[(cg, qt)] = [i2, qi, i3, src_ao, m_xpb, m_aoT, None, None]

        def gen_C1a2(cg, qt, use_sc=False):
            nonlocal next_pref, r_psw
            st = cstate[(cg, qt)]
            i2, qi, i3, src_ao, m_xpb, m_aoT = st[0], st[1], st[2], st[3], st[4], st[5]
            aoTv = src_ao.rearrange("p (c n) -> p c n", c=KC)
            mmp = None
            m_x2 = None
            for half in range(2):
                pdst = ps_sc[:, half * 512:(half + 1) * 512] if use_sc else ps_w[:, 0:512]
                if use_sc:
                    w0 = [m_aoT, m_wproj,
                          r_scp[half] if r_scp[half] is not None else pair_reader[0]]
                else:
                    w0 = [m_aoT, m_wproj, r_psw] if half == 0 else [m_x2]
                for kc in range(KC):
                    mmp = P.op("pe", lambda e, kc=kc, aoTv=aoTv, pdst=pdst, half=half:
                               e.matmul(pdst,
                                        aoTv[:, kc, :],
                                        wproj_bf[:, kc, half * 512:(half + 1) * 512],
                                        start=(kc == 0), stop=(kc == KC - 1)),
                               waits=w0 if kc == 0 else [],
                               sig="pe")
                    if kc == 3:
                        yield
                m_x2 = P.op("dve", lambda e, i2=i2, i3=i3, half=half, pdst=pdst, mmp=mmp:
                            e.scalar_tensor_tensor(
                                x2b(i2)[:, half * 512:(half + 1) * 512], pdst, 1.0,
                                x_rr(i3)[:, half * 512:(half + 1) * 512], ALU.mult, ALU.add),
                            waits=[mmp, m_xpb, r_x2[i2] if (half == 0 or use_sc) else None],
                            sig="dv")
                if use_sc:
                    r_scp[half] = m_x2
                yield
            r_aoT[i2] = mmp
            if not use_sc:
                r_psw = m_x2
            r_xrr[i3] = m_x2
            if next_pref <= min(qi + 3, T - 1):
                m_xr_all[next_pref] = sdma(x_rr(next_pref % 3), x_t[next_pref],
                                           waits=[r_xrr[next_pref % 3]], sig=f"dX{next_pref % 3}")
                next_pref += 1
            w_x2[i2] = m_x2
            yield
            st[6] = ln_stats(x2b(i2), m_x2, qi % 2)

        def gen_C1b(cg, qt):
            st = cstate[(cg, qt)]
            st[7] = ln_fin(x2b(st[0]), st[0], st[6], st[1] % 2)
            yield

        def gen_C2(cg, qt, use_pv=False):
            nonlocal r_psw, r_psbf
            st = cstate.pop((cg, qt))
            i2, qi, m_h = st[0], st[1], st[7]
            m_x2f = P.op("pool", lambda e, i2=i2: e.tensor_tensor(
                             x2b(i2), x2b(i2), bfn_bc[:], ALU.add),
                         waits=[m_h, m_bias["f"], r_x2[i2]], sig="pl")
            yield
            yield
            m_tr = transpose8(xn_b[:, i2, :], [m_h, r_psbf])
            yield
            m_hT = P.op("dve", lambda e, i2=i2: e.tensor_copy(
                            xn_b.rearrange("p a (c n) -> p a c n", c=KC)[:, i2],
                            psbf3),
                        waits=[m_tr, r_hT[i2]], sig="dv")
            r_xn[i2] = m_hT
            r_psbf = m_hT
            yield
            hTv = xn_b.rearrange("p a (c n) -> p a c n", c=KC)
            yield
            def fdst(half):
                return ps_pv[half][:, 0:512] if use_pv else ps_w[:, 0:512]
            mmf = None
            m_y = None
            for half in range(2):
                w0 = [m_hT, m_wt_final]
                if not use_pv:
                    w0.append(r_psw if half == 0 else m_y)
                if use_pv:
                    w0.append(r_pvffn[half] if r_pvffn[half] is not None
                              else m_pv_read[half])
                for kc in range(KC):
                    mmf = P.op("pe", lambda e, half=half, kc=kc, i2=i2:
                         e.matmul(fdst(half),
                                  hTv[:, i2, kc, :],
                                  wt_bf[:, kc, half * 512:(half + 1) * 512],
                                  start=(kc == 0), stop=(kc == KC - 1)),
                         waits=w0 if kc == 0 else [],
                         sig="pe" if kc == KC - 1 else None)
                    if kc == 3:
                        yield
                m_y = P.op("dve", lambda e, i2=i2, half=half, mmf=mmf: e.scalar_tensor_tensor(
                               x2b(i2)[:, half * 512:(half + 1) * 512],
                               fdst(half), 1.0,
                               x2b(i2)[:, half * 512:(half + 1) * 512],
                               ALU.mult, ALU.add),
                           waits=[mmf, m_x2f] if half == 0 else [mmf], sig="dv")
                if use_pv:
                    r_pvffn[half] = m_y
                yield
            r_hT[i2] = mmf
            if not use_pv:
                r_psw = m_y
            m_dy = P.op("sync", lambda e, i2=i2, qi=qi: e.dma_start(
                            out=y_t[qi], in_=x2b(i2)),
                        waits=[m_y], sig=f"dY{i2}", inc=DMA16)
            r_x2[i2] = m_dy

        from collections import deque
        fillers = deque()

        def pump():
            while fillers:
                try:
                    next(fillers[0])
                    return
                except StopIteration:
                    fillers.popleft()

        def drain_fillers():
            while fillers:
                try:
                    next(fillers[0])
                except StopIteration:
                    fillers.popleft()

        slotmap = {
            0: [("c", 2, 2), ("a2", 3, 2)],
            1: [("b", 3, 2), ("a1", 0, 1)],
            2: [("c", 3, 2), ("a2", 0, 1)],
            3: [("b", 0, 1), ("a1", 1, 1)],
            4: [("c", 0, 1), ("a2", 1, 1)],
            5: [("b", 1, 1), ("a1", 2, 1)],
            6: [("c", 1, 1), ("a2", 2, 1)],
            7: [("b", 2, 1), ("a1", 3, 1)],
        }
        gens = {"a1": gen_C1a1, "a2": gen_C1a2, "b": gen_C1b, "c": gen_C2}
        r_pvffn = [None, None]
        r_scp = [None, None]
        norm = None
        for g in range(G):
            for p in range(H // 2):
                norm = emit_B_pair(g, p, defer=norm)
                for kind, qt, goff in slotmap.get(p, []):
                    if g - goff >= 0:
                        fillers.append(gens[kind](g - goff, qt))
        drain_fillers()
        if norm is not None:
            norm[0]()
            norm[1]()
        if G >= 2:
            for job in (gens["c"](G - 2, 2, use_pv=True), gens["a2"](G - 2, 3, use_sc=True),
                        gens["b"](G - 2, 3), gens["c"](G - 2, 3, use_pv=True)):
                fillers.append(job)
                drain_fillers()
        tail = [("a1", 0), ("a2", 0), ("a1", 1), ("a2", 1),
                ("b", 0), ("c", 0), ("b", 1), ("c", 1),
                ("a1", 2), ("a2", 2), ("a1", 3), ("a2", 3),
                ("b", 2), ("c", 2), ("b", 3), ("c", 3)]
        for kind, qt in tail:
            if kind == "c":
                job = gens["c"](G - 1, qt, use_pv=True)
            elif kind == "a2":
                job = gens["a2"](G - 1, qt, use_sc=True)
            else:
                job = gens[kind](G - 1, qt)
            fillers.append(job)
            drain_fillers()
        P.op("sync", lambda e: e.nop(),
             waits=[("dY0", P.cnt.get("dY0", 0)), ("dY1", P.cnt.get("dY1", 0))])

        P.emit(block, sems)
    return nc


_NC_CACHE = {}


def _get_nc(L):
    if L not in _NC_CACHE:
        _NC_CACHE[L] = build_nc(L)
    return _NC_CACHE[L]


def _consts():
    i, j = np.meshgrid(np.arange(128), np.arange(128), indexing="ij")
    tri01 = np.where(i <= j, np.float32(1.0), np.float32(0.0)).astype(bfdt)
    return {
        "identb": np.eye(128, dtype=np.float32).astype(bfdt),
        "identf": np.eye(66, dtype=np.float32),
        "tri01": tri01,
    }


def make_in_maps(x, Wqkv, bqkv, Wproj, bproj, W, b, V_r, C):
    shared = {
        "wqkv": np.ascontiguousarray(np.asarray(Wqkv, dtype=np.float32)),
        "bqkv": np.ascontiguousarray(np.asarray(bqkv, dtype=np.float32)),
        "wproj": np.ascontiguousarray(np.asarray(Wproj, dtype=np.float32)),
        "bproj": np.ascontiguousarray(np.asarray(bproj, dtype=np.float32)),
        "wffn": np.ascontiguousarray(np.asarray(W, dtype=np.float32)),
        "bffn": np.ascontiguousarray(np.asarray(b, dtype=np.float32)),
        "vr": np.ascontiguousarray(np.asarray(V_r, dtype=np.float32)),
        "cmat": np.ascontiguousarray(np.asarray(C, dtype=np.float32)),
        **_consts(),
    }
    return [dict(shared, x=np.ascontiguousarray(x[bi])) for bi in range(x.shape[0])]


def kernel(L0_attn_x, L0_attn_Wqkv, L0_attn_bqkv, L0_attn_Wproj, L0_attn_bproj,
           L0_ffn_W, L0_ffn_b, L0_ffn_V_r, L0_ffn_C):
    from concourse.bass_utils import run_bass_kernel_spmd
    x = np.ascontiguousarray(np.asarray(L0_attn_x, dtype=np.float32))
    B, L, _ = x.shape
    nc = _get_nc(L)
    in_maps = make_in_maps(x, L0_attn_Wqkv, L0_attn_bqkv, L0_attn_Wproj, L0_attn_bproj,
                           L0_ffn_W, L0_ffn_b, L0_ffn_V_r, L0_ffn_C)
    res = run_bass_kernel_spmd(nc, in_maps, core_ids=list(range(B)))
    return np.stack([np.asarray(res.results[bi]["y"]) for bi in range(B)], axis=0)


# version 45 (final)
# speedup vs baseline: 1.0274x; 1.0059x over previous
import numpy as np
import ml_dtypes
from contextlib import ExitStack

import concourse.bass as bass
import concourse.mybir as mybir

F32 = mybir.dt.float32
BF16 = mybir.dt.bfloat16
AF = mybir.ActivationFunctionType
ALU = mybir.AluOpType
bfdt = ml_dtypes.bfloat16

D = 1024
KC = 8
H = 16
NEG = -1.0e30
DMA16 = 16


class Plan:
    ENGS = ("sync", "pool", "pe", "dve", "act")

    def __init__(self):
        self.ops = {e: [] for e in self.ENGS}
        self.cnt = {}

    def op(self, eng, fn, waits=(), sig=None, inc=1):
        ws = [w for w in waits if w is not None]
        self.ops[eng].append([fn, ws, sig, inc])
        if sig is not None:
            self.cnt[sig] = self.cnt.get(sig, 0) + inc
            return (sig, self.cnt[sig])
        return None

    def emit(self, block, sems):
        def make(ops):
            def run(e):
                seen = {}
                for fn, waits, sig, inc in ops:
                    for s, v in waits:
                        if seen.get(s, 0) < v:
                            e.wait_ge(sems[s], v)
                            seen[s] = v
                    instr = fn(e)
                    if sig is not None:
                        instr.then_inc(sems[sig], inc)
            return run
        block.sync(make(self.ops["sync"]))
        block.gpsimd(make(self.ops["pool"]))
        block.tensor(make(self.ops["pe"]))
        block.vector(make(self.ops["dve"]))
        block.scalar(make(self.ops["act"]))


def build_nc(L):
    T = L // 128
    G = L // 512
    assert L % 512 == 0

    nc = bass.Bass("TRN2", target_bir_lowering=False)

    x_d = nc.dram_tensor("x", [L, D], F32, kind="ExternalInput")
    wqkv_d = nc.dram_tensor("wqkv", [D, 3 * D], F32, kind="ExternalInput")
    bqkv_d = nc.dram_tensor("bqkv", [3 * D], F32, kind="ExternalInput")
    wproj_d = nc.dram_tensor("wproj", [D, D], F32, kind="ExternalInput")
    bproj_d = nc.dram_tensor("bproj", [D], F32, kind="ExternalInput")
    wffn_d = nc.dram_tensor("wffn", [D, D], F32, kind="ExternalInput")
    bffn_d = nc.dram_tensor("bffn", [D], F32, kind="ExternalInput")
    vr_d = nc.dram_tensor("vr", [D, 32], F32, kind="ExternalInput")
    cmat_d = nc.dram_tensor("cmat", [D, 32], F32, kind="ExternalInput")
    identb_d = nc.dram_tensor("identb", [128, 128], BF16, kind="ExternalInput")
    identf_d = nc.dram_tensor("identf", [66, 66], F32, kind="ExternalInput")
    tri01_d = nc.dram_tensor("tri01", [128, 128], BF16, kind="ExternalInput")
    y_d = nc.dram_tensor("y", [L, D], F32, kind="ExternalOutput")

    x_t = x_d.rearrange("(a p) n -> a p n", p=128)
    y_t = y_d.rearrange("(a p) n -> a p n", p=128)

    with ExitStack() as ctx:
        E = ctx.enter_context
        q_fm = E(nc.sbuf_tensor("q_fm", [128, KC, L], BF16))
        k_fm = E(nc.sbuf_tensor("k_fm", [128, KC, L], BF16))
        v_on = E(nc.sbuf_tensor("v_on", [128, T, 1118], BF16))
        wq_a = E(nc.sbuf_tensor("wq_a", [128, KC, 3 * D], BF16))
        xnt_a = E(nc.sbuf_tensor("xnt_a", [128, 8192], BF16))
        xf_a = E(nc.sbuf_tensor("xf_a", [128, 6656], F32))
        xn_b = E(nc.sbuf_tensor("xn_b", [128, 2, D], BF16))
        bias_a = E(nc.sbuf_tensor("bias_a", [128, 2 * D], F32))
        bv_bc = bias_a[:, 0:D]
        bpj_bc = bias_a[:, 0:D]
        bfn_bc = bias_a[:, D:2 * D]
        tri01 = E(nc.sbuf_tensor("tri01_sb", [128, 128], BF16))
        id_b = E(nc.sbuf_tensor("id_b", [128, 128], BF16))
        id_f = E(nc.sbuf_tensor("id_f", [66, 66], F32))
        bqkvT = E(nc.sbuf_tensor("bqkvT", [128, 24], F32))
        bqkv_sb = E(nc.sbuf_tensor("bqkv_sb", [24, 128], F32))
        vr_bf = E(nc.sbuf_tensor("vr_bf", [128, KC, 32], BF16))
        c_st = E(nc.sbuf_tensor("c_st", [128, KC, 32], BF16))
        ct_bf = E(nc.sbuf_tensor("ct_bf", [32, D], BF16))
        stats = E(nc.sbuf_tensor("stats", [128, 2, 12], F32))
        mv = E(nc.sbuf_tensor("mv", [128, 2, 2], F32))
        ln_t = E(nc.sbuf_tensor("ln_t", [128, 2], F32))
        rstd = E(nc.sbuf_tensor("rstd", [128, 2], F32))
        negmr = E(nc.sbuf_tensor("negmr", [128, 2], F32))
        recips = E(nc.sbuf_tensor("recips", [128, 4], F32))
        eps_t = E(nc.sbuf_tensor("eps_t", [128, 1], F32))
        ps_sc = E(nc.psum_tensor("ps_sc", [128, 2048], F32))
        ps_w = E(nc.psum_tensor("ps_w", [128, 512], F32))
        ps_bf = E(nc.psum_tensor("ps_bf", [128, 1024], BF16))
        ps_pv = [E(nc.psum_tensor(f"ps_pv{i}", [128, 512], F32)) for i in range(2)]
        semnames = ["dI", "dX0", "dX1", "dX2", "dY0", "dY1", "dWq", "dWk", "dWv", "dWs", "dWp", "dWf0", "dWf1", "dWf2", "dWf3", "dBp", "pe", "dv", "ac", "pl"]
        sems = {n: E(nc.semaphore(n)) for n in semnames}
        block = E(nc.Block())

        P = Plan()

        def xnT(buf):
            return xnt_a[:, buf * 4096:(buf + 1) * 4096].rearrange("p (c n) -> p c n", c=KC)
        def aoT(buf):
            return xnt_a[:, buf * 1024:(buf + 1) * 1024].rearrange("p (c n) -> p c n", c=KC)
        def hT(buf):
            return xnt_a[:, 2048 + buf * 1024:2048 + (buf + 1) * 1024].rearrange("p (c n) -> p c n", c=KC)
        def chunkpair(cp):
            if cp >= 2:
                o = 5120 + (cp - 2) * 512
                return xf_a[:, o:o + 512].bitcast(BF16)
            return xnt_a[:, 4096 + cp * 1024:4096 + (cp + 1) * 1024]
        u_bf = xnt_a[0:32, 7168:7296]
        def ao_g0(qt):
            return xnt_a[:, qt * 1024:(qt + 1) * 1024]
        pv_sbB = xnt_a[:, 6144:7168]
        def x_in(i):
            return xf_a[:, i * D:(i + 1) * D]
        x_rr = x_in
        def x2b(i):
            return xf_a[:, 3072 + i * D:3072 + (i + 1) * D]

        wstage = xf_a[:, 0:4096].bitcast(BF16).rearrange("p (c n) -> p c n", c=KC)
        wproj_bf = wq_a[:, :, 0:D]
        wt_bf = wq_a[:, :, D:2 * D]
        def ao(par, qt):
            return wq_a[:, 4 * par + qt, 2 * D:3 * D]
        psbf3 = ps_bf.rearrange("p (c n) -> p c n", c=KC)

        def sdma(dst, src, waits=(), sig="dI"):
            return P.op("sync", lambda e, d=dst, s=src: e.dma_start(out=d, in_=s),
                        waits=waits, sig=sig, inc=DMA16)
        m_idb = sdma(id_b[:], identb_d.ap())
        m_idf = sdma(id_f[:], identf_d.ap())
        m_cm = sdma(tri01[:], tri01_d.ap())
        m_bq = sdma(bqkv_sb[:], bqkv_d.rearrange("(a b) -> a b", b=128))
        m_init = ("dI", P.cnt["dI"])
        m_idb = m_idf = m_cm = m_bq = m_init

        def gdma(dst, src, waits=(), sig="dWs"):
            return P.op("pool", lambda e, d=dst, s=src: e.dma_start(out=d, in_=s),
                        waits=waits, sig=sig, inc=DMA16)
        wqkv_r = wqkv_d.rearrange("(c p) n -> p c n", p=128)
        m_wq = gdma(wq_a[:, :, 0:D], wqkv_r[:, :, 0:D], sig="dWq")
        m_wk = gdma(wq_a[:, :, D:2 * D], wqkv_r[:, :, D:2 * D], sig="dWk")
        m_wv = gdma(wq_a[:, :, 2 * D:3 * D], wqkv_r[:, :, 2 * D:3 * D], sig="dWv")
        gdma(vr_bf[:, :, :], vr_d.rearrange("(c p) n -> p c n", p=128))
        gdma(c_st[:, :, :], cmat_d.rearrange("(c p) n -> p c n", p=128))
        m_vr = m_cst = ("dWs", 64)
        gdma(bv_bc, bqkv_d.rearrange("(a b) -> a b", a=3)[2:3, :].broadcast_to((128, D)),
             sig="dWs")
        gdma(bfn_bc, bffn_d.rearrange("(a b) -> a b", a=1).broadcast_to((128, D)),
             sig="dWs")
        m_bv_bc = m_bfn_bc = ("dWs", 64)

        m_eps = P.op("dve", lambda e: e.memset(eps_t[:], 1e-5), sig="dv")
        m_vones = P.op("dve", lambda e: e.memset(
                           v_on[:, :, 0:1056].rearrange("p t (h n) -> p t h n", h=H)[:, :, :, 64:66],
                           1.0),
                       waits=[m_eps], sig="dv")
        m_vones = P.op("dve", lambda e: e.memset(v_on[:, :, 1056:1118], 0.0),
                       waits=[m_vones], sig="dv")
        m_tbq = P.op("pe", lambda e: e.transpose(ps_w[0:128, 0:24], bqkv_sb[0:24, 0:128],
                                                 id_f[0:24, 0:24]),
                     waits=[m_bq, m_idf], sig="pe")
        m_bqkvT = P.op("dve", lambda e: e.tensor_copy(bqkvT[:], ps_w[0:128, 0:24]),
                       waits=[m_tbq, m_vones], sig="dv")
        m_bias = {"v": m_bv_bc, "f": m_bfn_bc}
        m_psw_free = None

        r_xin = [None, None, None]
        w_xn = [None, None]
        r_xn = [None, None]
        w_xnt = [None, None]
        r_xnt = [None, None]
        r_psbf = None
        m_ln_last = [None, None]
        r_rstd = [None, None]
        bank_reader = [None, None, None, None]
        qkv_bank = 0
        m_v_evac = [None, None]
        last_A_pe = None

        def ln_stats(src, extra_wait, lp):
            m1 = P.op("dve", lambda e, lp=lp: e.bn_stats(stats[:, lp, 0:6], src[:, 0:512]),
                      waits=[extra_wait, m_ln_last[lp]], sig="dv")
            m2 = P.op("dve", lambda e, lp=lp: e.bn_stats(stats[:, lp, 6:12], src[:, 512:1024]),
                      waits=[m1], sig="dv")
            m3 = P.op("dve", lambda e, lp=lp: e.bn_aggr(mv[:, lp, :], stats[:, lp, :]),
                      waits=[m2], sig="dv")
            return m3

        def ln_fin(src, i2, m3, lp):
            m4 = P.op("act", lambda e, lp=lp: e.activation(ln_t[:, lp:lp + 1], mv[:, lp, 1:2],
                                                           AF.Ln, bias=eps_t[:], scale=1.0),
                      waits=[m3, r_rstd[lp], m_eps], sig="ac")
            m5 = P.op("act", lambda e, lp=lp: e.activation(rstd[:, lp:lp + 1], ln_t[:, lp:lp + 1],
                                                           AF.Exp, bias=0.0, scale=-0.5),
                      waits=[m4], sig="ac")
            m6 = P.op("dve", lambda e, lp=lp: e.tensor_scalar(negmr[:, lp:lp + 1], mv[:, lp, 0:1],
                                                              rstd[:, lp:lp + 1], -1.0,
                                                              ALU.mult, ALU.mult),
                      waits=[m5, m3], sig="dv")
            m7 = P.op("dve", lambda e, i2=i2, lp=lp: e.tensor_scalar(
                          xn_b[:, i2, :], src, rstd[:, lp:lp + 1], negmr[:, lp:lp + 1],
                          ALU.mult, ALU.add),
                      waits=[m6, r_xn[i2]], sig="dv")
            w_xn[i2] = m7
            m_ln_last[lp] = m7
            r_rstd[lp] = m7
            return m7

        def ln_chain(src, i2, extra_wait, lp):
            return ln_fin(src, i2, ln_stats(src, extra_wait, lp), lp)

        def transpose8(src1024_bf, waits_first):
            m_tr = None
            for c in range(KC):
                m_tr = P.op("pe", lambda e, c=c: e.transpose(
                                ps_bf[:, c * 128:(c + 1) * 128],
                                src1024_bf[:, c * 128:(c + 1) * 128], id_b[:]),
                            waits=(waits_first + [m_idb]) if c == 0 else [],
                            sig="pe" if c == KC - 1 else None)
            return m_tr

        qkv_banks = [ps_sc[:, i * 512:(i + 1) * 512] for i in range(4)]

        def mats_gen(tb):
            nonlocal qkv_bank, m_v_evac, last_A_pe
            tbuf = tb % 2
            for oc in range(16):
                mm = None
                for kc in range(KC):
                    mm = P.op("pe", lambda e, oc=oc, kc=kc, b=qkv_bank, tbuf=tbuf:
                              e.matmul(qkv_banks[b][:, :],
                                       wq_a[:, kc, oc * 128:(oc + 1) * 128],
                                       xnT(tbuf)[:, kc, :],
                                       start=(kc == 0), stop=(kc == KC - 1)),
                              waits=[(m_wq if oc < 8 else m_wk), w_xnt[tbuf],
                                     bank_reader[qkv_bank]] if kc == 0 else [],
                              sig="pe" if kc == KC - 1 else None)
                dst = q_fm if oc < 8 else k_fm
                cc = oc % 8
                m_qk = P.op("act", lambda e, dst=dst, cc=cc, oc=oc, b=qkv_bank, tb=tb:
                            e.activation(dst[:, cc, tb * 512:(tb + 1) * 512],
                                         qkv_banks[b][:, :], AF.Identity,
                                         bias=bqkvT[:, oc:oc + 1], scale=1.0),
                            waits=[mm, m_bqkvT], sig="ac")
                bank_reader[qkv_bank] = m_qk
                r_xnt[tbuf] = mm
                qkv_bank = (qkv_bank + 1) % 4
                if oc % 2 == 1:
                    yield
            for tt in range(4):
                kt = tb * 4 + tt
                mmv = [None, None]
                for half in range(2):
                    for kc in range(KC):
                        mmv[half] = P.op("pe", lambda e, half=half, kc=kc, tbuf=tbuf, tt=tt:
                                   e.matmul(ps_pv[half][:, 0:512],
                                            xnT(tbuf)[:, kc, tt * 128:(tt + 1) * 128],
                                            wq_a[:, kc, 2 * D + half * 512:2 * D + (half + 1) * 512],
                                            start=(kc == 0), stop=(kc == KC - 1)),
                                   waits=[m_v_evac[half], w_xnt[tbuf], m_wv] if kc == 0 else [],
                                   sig="pe" if kc == KC - 1 else None)
                r_xnt[tbuf] = mmv[1]
                for half in range(2):
                    m_v_evac[half] = P.op("dve", lambda e, kt=kt, half=half: e.tensor_tensor(
                                        v_on[:, kt, half * 528:(half + 1) * 528]
                                        .rearrange("p (h n) -> p h n", h=H // 2)[:, :, 0:64],
                                        ps_pv[half].rearrange("p (h n) -> p h n", h=H // 2),
                                        bv_bc[:, half * 512:(half + 1) * 512]
                                        .rearrange("p (h n) -> p h n", h=H // 2), ALU.add),
                                    waits=[mmv[half], m_bias["v"], m_vones], sig="dv")
                last_A_pe = mmv[1]
                yield

        a_gen = [None]

        def pump_a():
            if a_gen[0] is not None:
                try:
                    next(a_gen[0])
                except StopIteration:
                    a_gen[0] = None

        def drain_a():
            while a_gen[0] is not None:
                pump_a()

        for t in range(T):
            tb, ti = t // 4, t % 4
            i3, i2 = t % 3, t % 2
            tbuf = tb % 2
            m_x = sdma(x_in(i3), x_t[t], waits=[r_xin[i3]], sig=f"dX{i3}")
            m_xn = ln_chain(x_in(i3), i2, m_x, t % 2)
            r_xin[i3] = m_xn
            pump_a()
            m_tr = transpose8(xn_b[:, i2, :], [m_xn, r_psbf])
            r_xn[i2] = m_tr
            m_ev = P.op("dve", lambda e, tbuf=tbuf, ti=ti: e.tensor_copy(
                            xnT(tbuf)[:, :, ti * 128:(ti + 1) * 128], psbf3),
                        waits=[m_tr, r_xnt[tbuf]], sig="dv")
            r_psbf = m_ev
            pump_a()
            if ti == 3:
                w_xnt[tbuf] = m_ev
                drain_a()
                a_gen[0] = mats_gen(tb)
        drain_a()
        m_qkv_done = bank_reader[(qkv_bank - 1) % 4]
        m_v_done = m_v_evac[1]

        m_wproj = gdma(wproj_bf, wproj_d.rearrange("(c p) n -> p c n", p=128),
                       waits=[last_A_pe], sig="dWp")
        m_bias["p"] = sdma(bpj_bc,
                           bproj_d.rearrange("(a b) -> a b", a=1).broadcast_to((128, D)),
                           waits=[m_v_done], sig="dBp")
        wffn_r = wffn_d.rearrange("(r p) n -> p r n", p=128)
        arena_guard = [r_xin[0], r_xin[1], r_xin[2]]
        m_wfc = []
        for c4 in range(4):
            g4 = arena_guard[c4 * 2:c4 * 2 + 2]
            m_wfc.append(gdma(wstage[:, c4 * 2:c4 * 2 + 2, :],
                              wffn_r[:, c4 * 2:c4 * 2 + 2, :],
                              waits=g4, sig=f"dWf{c4}"))
        m_wt_done = None
        for r in range(KC):
            m_tr = None
            for dc in range(KC):
                m_tr = P.op("pe", lambda e, r=r, dc=dc: e.transpose(
                                ps_bf[:, dc * 128:(dc + 1) * 128],
                                wstage[:, r, dc * 128:(dc + 1) * 128], id_b[:]),
                            waits=[m_wfc[r // 2], r_psbf] if dc == 0 else [],
                            sig="pe" if dc == KC - 1 else None)
            r_psbf = P.op("dve", lambda e, r=r: e.tensor_copy(
                              wt_bf[:, :, r * 128:(r + 1) * 128],
                              ps_bf[:, :].rearrange("p (c n) -> p c n", c=KC)),
                          waits=[m_tr, m_wproj if r == 0 else None], sig="dv")
        m_wt_done = m_tr
        m_wt_evac = r_psbf
        m_trc = None
        for c in range(KC):
            m_trc = P.op("pe", lambda e, c=c: e.transpose(
                             ps_bf[0:32, c * 128:(c + 1) * 128], c_st[:, c, :], id_b[:]),
                         waits=[m_cst, r_psbf] if c == 0 else [],
                         sig="pe" if c == KC - 1 else None)
        m_ct = P.op("dve", lambda e: e.tensor_copy(
                        ct_bf[0:32, :], ps_bf[0:32, :].rearrange("p (c n) -> p c n", c=KC)),
                    waits=[m_trc], sig="dv")
        r_psbf = m_ct
        vrT = xnt_a[0:32, 7168:8192]
        m_wt_final = None
        r_psw = m_bqkvT

        def fold_gen():
            nonlocal r_psbf, r_psw, m_wt_final
            m_trv = None
            for c in range(KC):
                m_trv = P.op("pe", lambda e, c=c: e.transpose(
                                 ps_bf[0:32, c * 128:(c + 1) * 128], vr_bf[:, c, :], id_b[:]),
                             waits=[m_vr, r_psbf] if c == 0 else [],
                             sig="pe" if c == KC - 1 else None)
            m_vrT = P.op("dve", lambda e: e.tensor_copy(
                             vrT[:, :], ps_bf[0:32, :].rearrange("p (c n) -> p c n", c=KC)),
                         waits=[m_trv], sig="dv")
            r_psbf = m_vrT
            fold_banks = [ps_w[:, 0:512], ps_pv[0][:, 0:512]]
            fold_r = [m_bqkvT, m_v_evac[0]]
            m_fold = None
            for kc in range(KC):
                for half in range(2):
                    pb = (kc * 2 + half) % 2
                    mmc = P.op("pe", lambda e, kc=kc, half=half, pb=pb:
                               e.matmul(fold_banks[pb],
                                        vrT[:, kc * 128:(kc + 1) * 128],
                                        ct_bf[0:32, half * 512:(half + 1) * 512],
                                        start=True, stop=True),
                               waits=[m_vrT, m_ct, fold_r[pb]], sig="pe")
                    m_fold = P.op("dve", lambda e, kc=kc, half=half, pb=pb, mmc=mmc:
                                  e.tensor_tensor(
                                      wt_bf[:, kc, half * 512:(half + 1) * 512],
                                      wt_bf[:, kc, half * 512:(half + 1) * 512],
                                      fold_banks[pb], ALU.add),
                                  waits=[mmc, m_wt_evac if kc == 0 and half == 0 else None],
                                  sig="dv")
                    fold_r[pb] = m_fold
                if kc == 3:
                    yield
            m_wt_final = m_fold
            r_psw = fold_r[0]
            m_pv_read[0] = fold_r[1]

        pair_reader = [m_qkv_done, m_qkv_done]
        ch_reader = [last_A_pe] * 5
        gki = 0
        m_pv_read = [m_v_evac[0], m_v_evac[1]]
        w_ao = [None, None]
        r_ao = [None, None]
        r_aoT = [last_A_pe, last_A_pe]
        r_hT = [last_A_pe, last_A_pe]
        w_x2 = [None, None]
        r_x2 = [m_wt_done, m_wt_done]
        r_xrr = [m_wt_done, m_wt_done, m_wt_done]
        m_xr_all = [None] * T
        next_pref = 0
        r_pvsb = last_A_pe
        r_recips = None
        w_ao_g0 = [None]

        def emit_B_pair(g, hc, defer=None):
            nonlocal gki, r_pvsb, r_recips
            par = g % 2
            he, ho = 2 * hc, 2 * hc + 1
            K = 4 * g + 4
            pend = []
            m_pv_last = [None, None]
            if defer is not None:
                defer[0]()
            def pop_pv(drain=False):
                nonlocal m_pv_last
                while len(pend) > 6 or (drain and pend):
                    cp2, off2, ncol2, gate2, ki2, hd2 = pend.pop(0)
                    m_pv_last[hd2] = P.op("pe", lambda e, cp=cp2, off=off2, ncol=ncol2,
                                          hd=hd2, hh=2 * hc + hd2, ki2=ki2,
                                          st=(ki2 == 0), sp=(ki2 == K - 1):
                                          e.matmul(ps_pv[hd][0:66, off * 128:512],
                                                   v_on[:, ki2, 66 * hh:66 * hh + 66],
                                                   chunkpair(cp)[:, hd * 512:hd * 512 + ncol],
                                                   start=st, stop=sp),
                                          waits=[gate2, (m_pv_read[hd2] if ki2 == 0 else None),
                                                 (m_v_done if ki2 == 0 else None)],
                                          sig="pe")
                    ch_reader[cp2] = m_pv_last[hd2]

            for ki in range(K):
                off = max(0, ki - 4 * g)
                ncol = 512 - off * 128
                pp = gki % 2
                cp = gki % 5
                m_sc = None
                for hd, hp in ((0, 0), (1, 64)):
                    m_sc = P.op("pe", lambda e, pp=pp, hd=hd, hp=hp, off=off, ki=ki, hc=hc, g=g:
                                e.matmul(ps_sc[:, pp * 1024 + hd * 512 + off * 128:
                                              pp * 1024 + hd * 512 + 512],
                                         k_fm[hp:hp + 64, hc, ki * 128:(ki + 1) * 128],
                                         q_fm[hp:hp + 64, hc, g * 512 + off * 128:(g + 1) * 512],
                                         start=True, stop=True,
                                         tile_position=(hp, 0), skip_group_check=True),
                                waits=[pair_reader[pp], m_qkv_done] if hd == 0 else [],
                                sig="pe" if hd == 1 else None)
                m_ex = P.op("act", lambda e, pp=pp, cp=cp, off=off, ncol=ncol:
                            e.activation(chunkpair(cp).rearrange("p (a n) -> p a n", a=2)
                                         [:, :, 0:ncol],
                                         ps_sc[:, pp * 1024:(pp + 1) * 1024]
                                         .rearrange("p (a n) -> p a n", a=2)
                                         [:, :, off * 128:512], AF.Exp,
                                         bias=0.0, scale=0.125),
                            waits=[m_sc, ch_reader[cp]], sig="ac")
                pair_reader[pp] = m_ex
                for hd in range(2):
                    gate = m_ex
                    if ki >= 4 * g:
                        gate = P.op("pool", lambda e, cp=cp, hd=hd: e.tensor_tensor(
                                        chunkpair(cp)[:, hd * 512:hd * 512 + 128],
                                        chunkpair(cp)[:, hd * 512:hd * 512 + 128],
                                        tri01[:], ALU.mult),
                                    waits=[m_ex, m_cm], sig="pl")
                    pend.append((cp, off, ncol, gate, ki, hd))
                gki += 1
                if ki % 2 == 1:
                    pop_pv()
                pump()
            pop_pv(drain=True)
            if defer is not None:
                defer[1]()
            cell = []

            def norm_a(m_pv_last=tuple(m_pv_last)):
                nonlocal r_pvsb
                for hd in range(2):
                    m_evac = P.op("dve", lambda e, hd=hd: e.tensor_copy(
                                      pv_sbB[0:66, hd * 512:(hd + 1) * 512],
                                      ps_pv[hd][0:66, 0:512]),
                                  waits=[m_pv_last[hd], r_pvsb], sig="dv")
                    m_pv_read[hd] = m_evac
                    cell.append(m_evac)

            def normalize(par=par, hc=hc, g=g):
                nonlocal r_pvsb, r_recips, r_psbf
                m_pvT = None
                for hd in range(2):
                    for qt in range(4):
                        m_pvT = P.op("pe", lambda e, qt=qt, hd=hd: e.transpose(
                                         ps_bf[:, hd * 288 + qt * 72:hd * 288 + qt * 72 + 66],
                                         pv_sbB[0:66, hd * 512 + qt * 128:hd * 512 + (qt + 1) * 128],
                                         id_b[0:66, 0:66]),
                                     waits=[cell[hd], r_psbf] if qt == 0 else [],
                                     sig="pe" if (hd == 1 and qt == 3) else None)
                r_pvsb = m_pvT
                m_last = None
                for hd in range(2):
                    hh = 2 * hc + hd
                    m_rc = P.op("dve", lambda e, hd=hd: e.reciprocal(
                                    recips.rearrange("p (a b) -> p a b", b=1),
                                    ps_bf[:, hd * 288:hd * 288 + 288]
                                    .rearrange("p (a b) -> p a b", b=72)[:, :, 64:65]),
                                waits=[m_pvT, r_recips], sig="dv")
                    for qt in range(4):
                        dst = (ao(par, qt) if g > 0 else ao_g0(qt))
                        m_last = P.op("dve", lambda e, qt=qt, hh=hh, hd=hd, dst=dst:
                                      e.tensor_scalar(
                                          dst[:, hh * 64:(hh + 1) * 64],
                                          ps_bf[:, hd * 288 + qt * 72:hd * 288 + qt * 72 + 64],
                                          recips[:, qt:qt + 1], None, ALU.mult),
                                      waits=[m_rc, ((r_ao[par] if g > 0 else r_aoT[qt % 2])
                                                    if hc == 0 and hd == 0 and qt == 0 else None),
                                             (last_A_pe if g <= 2 and hc == 0 and hd == 0 and qt == 0 else None)],
                                      sig="dv")
                    r_recips = m_last
                r_psbf = m_last
                if hc == H // 2 - 1:
                    if g > 0:
                        w_ao[par] = m_last
                    else:
                        w_ao_g0[0] = m_last
            return (norm_a, normalize)

        cstate = {}

        def gen_C1a1(cg, qt):
            nonlocal next_pref, r_psbf
            par = cg % 2
            qi = 4 * cg + qt
            i3, i2 = qi % 3, qi % 2
            while next_pref <= min(qi + 2, T - 1):
                m_xr_all[next_pref] = sdma(x_rr(next_pref % 3), x_t[next_pref],
                                           waits=[r_xrr[next_pref % 3]], sig=f"dX{next_pref % 3}")
                next_pref += 1
            m_xpb = P.op("pool", lambda e, i3=i3: e.tensor_tensor(
                             x_rr(i3), x_rr(i3), bpj_bc[:], ALU.add),
                         waits=[m_xr_all[qi], m_bias["p"]], sig="pl")
            yield
            yield
            yield
            src_ao = ao(par, qt) if cg > 0 else ao_g0(qt)
            m_tr = transpose8(src_ao, [w_ao[par] if cg > 0 else w_ao_g0[0], r_psbf])
            yield
            if cg > 0:
                r_ao[par] = m_tr
            m_aoT = P.op("dve", lambda e, src_ao=src_ao: e.tensor_copy(
                             src_ao.rearrange("p (c n) -> p c n", c=KC), psbf3),
                         waits=[m_tr, r_aoT[i2]], sig="dv")
            r_psbf = m_aoT
            cstate[(cg, qt)] = [i2, qi, i3, src_ao, m_xpb, m_aoT, None, None]

        def gen_C1a2(cg, qt, use_sc=False):
            nonlocal next_pref, r_psw
            st = cstate[(cg, qt)]
            i2, qi, i3, src_ao, m_xpb, m_aoT = st[0], st[1], st[2], st[3], st[4], st[5]
            aoTv = src_ao.rearrange("p (c n) -> p c n", c=KC)
            mmp = None
            m_x2 = None
            for half in range(2):
                pdst = ps_sc[:, half * 512:(half + 1) * 512] if use_sc else ps_w[:, 0:512]
                if use_sc:
                    w0 = [m_aoT, m_wproj,
                          r_scp[half] if r_scp[half] is not None else pair_reader[0]]
                else:
                    w0 = [m_aoT, m_wproj, r_psw] if half == 0 else [m_x2]
                for kc in range(KC):
                    mmp = P.op("pe", lambda e, kc=kc, aoTv=aoTv, pdst=pdst, half=half:
                               e.matmul(pdst,
                                        aoTv[:, kc, :],
                                        wproj_bf[:, kc, half * 512:(half + 1) * 512],
                                        start=(kc == 0), stop=(kc == KC - 1)),
                               waits=w0 if kc == 0 else [],
                               sig="pe")
                    if kc == 3:
                        yield
                m_x2 = P.op("dve", lambda e, i2=i2, i3=i3, half=half, pdst=pdst, mmp=mmp:
                            e.scalar_tensor_tensor(
                                x2b(i2)[:, half * 512:(half + 1) * 512], pdst, 1.0,
                                x_rr(i3)[:, half * 512:(half + 1) * 512], ALU.mult, ALU.add),
                            waits=[mmp, m_xpb, r_x2[i2] if (half == 0 or use_sc) else None],
                            sig="dv")
                if use_sc:
                    r_scp[half] = m_x2
                yield
            r_aoT[i2] = mmp
            if not use_sc:
                r_psw = m_x2
            r_xrr[i3] = m_x2
            if next_pref <= min(qi + 3, T - 1):
                m_xr_all[next_pref] = sdma(x_rr(next_pref % 3), x_t[next_pref],
                                           waits=[r_xrr[next_pref % 3]], sig=f"dX{next_pref % 3}")
                next_pref += 1
            w_x2[i2] = m_x2
            yield
            st[6] = ln_stats(x2b(i2), m_x2, qi % 2)

        def gen_C1b(cg, qt):
            st = cstate[(cg, qt)]
            st[7] = ln_fin(x2b(st[0]), st[0], st[6], st[1] % 2)
            yield

        def gen_C2(cg, qt, use_pv=False):
            nonlocal r_psw, r_psbf
            st = cstate.pop((cg, qt))
            i2, qi, m_h = st[0], st[1], st[7]
            m_x2f = P.op("pool", lambda e, i2=i2: e.tensor_tensor(
                             x2b(i2), x2b(i2), bfn_bc[:], ALU.add),
                         waits=[m_h, m_bias["f"], r_x2[i2]], sig="pl")
            yield
            yield
            m_tr = transpose8(xn_b[:, i2, :], [m_h, r_psbf])
            yield
            m_hT = P.op("dve", lambda e, i2=i2: e.tensor_copy(
                            xn_b.rearrange("p a (c n) -> p a c n", c=KC)[:, i2],
                            psbf3),
                        waits=[m_tr, r_hT[i2]], sig="dv")
            r_xn[i2] = m_hT
            r_psbf = m_hT
            yield
            hTv = xn_b.rearrange("p a (c n) -> p a c n", c=KC)
            yield
            def fdst(half):
                return ps_pv[half][:, 0:512] if use_pv else ps_w[:, 0:512]
            mmf = None
            m_y = None
            for half in range(2):
                w0 = [m_hT, m_wt_final]
                if not use_pv:
                    w0.append(r_psw if half == 0 else m_y)
                if use_pv:
                    w0.append(r_pvffn[half] if r_pvffn[half] is not None
                              else m_pv_read[half])
                for kc in range(KC):
                    mmf = P.op("pe", lambda e, half=half, kc=kc, i2=i2:
                         e.matmul(fdst(half),
                                  hTv[:, i2, kc, :],
                                  wt_bf[:, kc, half * 512:(half + 1) * 512],
                                  start=(kc == 0), stop=(kc == KC - 1)),
                         waits=w0 if kc == 0 else [],
                         sig="pe" if kc == KC - 1 else None)
                    if kc == 3:
                        yield
                m_y = P.op("dve", lambda e, i2=i2, half=half, mmf=mmf: e.scalar_tensor_tensor(
                               x2b(i2)[:, half * 512:(half + 1) * 512],
                               fdst(half), 1.0,
                               x2b(i2)[:, half * 512:(half + 1) * 512],
                               ALU.mult, ALU.add),
                           waits=[mmf, m_x2f] if half == 0 else [mmf], sig="dv")
                if use_pv:
                    r_pvffn[half] = m_y
                yield
            r_hT[i2] = mmf
            if not use_pv:
                r_psw = m_y
            m_dy = P.op("sync", lambda e, i2=i2, qi=qi: e.dma_start(
                            out=y_t[qi], in_=x2b(i2)),
                        waits=[m_y], sig=f"dY{i2}", inc=DMA16)
            r_x2[i2] = m_dy

        from collections import deque
        fillers = deque()

        def pump():
            while fillers:
                try:
                    next(fillers[0])
                    return
                except StopIteration:
                    fillers.popleft()

        def drain_fillers():
            while fillers:
                try:
                    next(fillers[0])
                except StopIteration:
                    fillers.popleft()

        slotmap = {
            0: [("c", 2, 2), ("a2", 3, 2)],
            1: [("b", 3, 2), ("a1", 0, 1)],
            2: [("c", 3, 2), ("a2", 0, 1)],
            3: [("b", 0, 1), ("a1", 1, 1)],
            4: [("c", 0, 1), ("a2", 1, 1)],
            5: [("b", 1, 1), ("a1", 2, 1)],
            6: [("c", 1, 1), ("a2", 2, 1)],
            7: [("b", 2, 1), ("a1", 3, 1)],
        }
        gens = {"a1": gen_C1a1, "a2": gen_C1a2, "b": gen_C1b, "c": gen_C2}
        r_pvffn = [None, None]
        r_scp = [None, None]
        fillers.append(fold_gen())
        norm = None
        for g in range(G):
            for p in range(H // 2):
                norm = emit_B_pair(g, p, defer=norm)
                for kind, qt, goff in slotmap.get(p, []):
                    if g - goff >= 0:
                        fillers.append(gens[kind](g - goff, qt))
        drain_fillers()
        if norm is not None:
            norm[0]()
            norm[1]()
        if G >= 2:
            for job in (gens["c"](G - 2, 2, use_pv=True), gens["a2"](G - 2, 3, use_sc=True),
                        gens["b"](G - 2, 3), gens["c"](G - 2, 3, use_pv=True)):
                fillers.append(job)
                drain_fillers()
        tail = [("a1", 0), ("a2", 0), ("a1", 1), ("a2", 1),
                ("b", 0), ("c", 0), ("b", 1), ("c", 1),
                ("a1", 2), ("a2", 2), ("a1", 3), ("a2", 3),
                ("b", 2), ("c", 2), ("b", 3), ("c", 3)]
        for kind, qt in tail:
            if kind == "c":
                job = gens["c"](G - 1, qt, use_pv=True)
            elif kind == "a2":
                job = gens["a2"](G - 1, qt, use_sc=True)
            else:
                job = gens[kind](G - 1, qt)
            fillers.append(job)
            drain_fillers()
        P.op("sync", lambda e: e.nop(),
             waits=[("dY0", P.cnt.get("dY0", 0)), ("dY1", P.cnt.get("dY1", 0))])

        P.emit(block, sems)
    return nc


_NC_CACHE = {}


def _get_nc(L):
    if L not in _NC_CACHE:
        _NC_CACHE[L] = build_nc(L)
    return _NC_CACHE[L]


def _consts():
    i, j = np.meshgrid(np.arange(128), np.arange(128), indexing="ij")
    tri01 = np.where(i <= j, np.float32(1.0), np.float32(0.0)).astype(bfdt)
    return {
        "identb": np.eye(128, dtype=np.float32).astype(bfdt),
        "identf": np.eye(66, dtype=np.float32),
        "tri01": tri01,
    }


def make_in_maps(x, Wqkv, bqkv, Wproj, bproj, W, b, V_r, C):
    shared = {
        "wqkv": np.ascontiguousarray(np.asarray(Wqkv, dtype=np.float32)),
        "bqkv": np.ascontiguousarray(np.asarray(bqkv, dtype=np.float32)),
        "wproj": np.ascontiguousarray(np.asarray(Wproj, dtype=np.float32)),
        "bproj": np.ascontiguousarray(np.asarray(bproj, dtype=np.float32)),
        "wffn": np.ascontiguousarray(np.asarray(W, dtype=np.float32)),
        "bffn": np.ascontiguousarray(np.asarray(b, dtype=np.float32)),
        "vr": np.ascontiguousarray(np.asarray(V_r, dtype=np.float32)),
        "cmat": np.ascontiguousarray(np.asarray(C, dtype=np.float32)),
        **_consts(),
    }
    return [dict(shared, x=np.ascontiguousarray(x[bi])) for bi in range(x.shape[0])]


def kernel(L0_attn_x, L0_attn_Wqkv, L0_attn_bqkv, L0_attn_Wproj, L0_attn_bproj,
           L0_ffn_W, L0_ffn_b, L0_ffn_V_r, L0_ffn_C):
    from concourse.bass_utils import run_bass_kernel_spmd
    x = np.ascontiguousarray(np.asarray(L0_attn_x, dtype=np.float32))
    B, L, _ = x.shape
    nc = _get_nc(L)
    in_maps = make_in_maps(x, L0_attn_Wqkv, L0_attn_bqkv, L0_attn_Wproj, L0_attn_bproj,
                           L0_ffn_W, L0_ffn_b, L0_ffn_V_r, L0_ffn_C)
    res = run_bass_kernel_spmd(nc, in_maps, core_ids=list(range(B)))
    return np.stack([np.asarray(res.results[bi]["y"]) for bi in range(B)], axis=0)
